# version 23
# baseline (speedup 1.0000x reference)
import numpy as np
from contextlib import ExitStack

import concourse.bass as bass
import concourse.mybir as mybir
from concourse.bass import IndirectOffsetOnAxis
from concourse.bass_utils import run_bass_kernel_spmd
import concourse.bass_utils as _bass_utils

if not getattr(_bass_utils, "_ldw_patched", False):
    _orig_run_command = _bass_utils.run_command

    def _run_command_ldw(argv, **kwargs):
        argv = ["--enable-ldw-opt=true" if a == "--enable-ldw-opt=false" else a
                for a in argv]
        return _orig_run_command(argv, **kwargs)

    _bass_utils.run_command = _run_command_ldw
    _bass_utils._ldw_patched = True

F32 = mybir.dt.float32
BF16 = mybir.dt.bfloat16
I32 = mybir.dt.int32
AF = mybir.ActivationFunctionType
ALU = mybir.AluOpType
AX = mybir.AxisListType

B, T, D = 4, 4096, 1024
HT = T // 2
EXT = HT + 128
E, FF = 8, 4096
CAP = 320
JSEG = 12
EPS = 1e-5
NBIS = 36
DUMP = T


CFG = {"p7_gather": True, "p7_transpose": True, "p7_scatter": True,
       "p7_w2": True, "p7_E": 8, "p7_scblock": True}


def build_nc(upto=99, dbg=False):
    nc = bass.Bass("TRN2", target_bir_lowering=False,
                   dynamic_dma_scratch_size=16384)

    x_row_d = nc.dram_tensor("x_row", [T, D], F32, kind="ExternalInput")
    x_ext_d = nc.dram_tensor("x_ext", [EXT, D], F32, kind="ExternalInput")
    mask_halo_d = nc.dram_tensor("mask_halo", [128, 1], F32, kind="ExternalInput")
    half_sel_d = nc.dram_tensor("half_sel", [128, 32], F32, kind="ExternalInput")
    own_ids_d = nc.dram_tensor("own_ids", [128, 16], I32, kind="ExternalInput")
    ext_ids_d = nc.dram_tensor("ext_ids", [128, 32], I32, kind="ExternalInput")
    w1_d = nc.dram_tensor("w1", [D, D], F32, kind="ExternalInput")
    b1_d = nc.dram_tensor("b1", [1, D], F32, kind="ExternalInput")
    w2_d = nc.dram_tensor("w2", [D, D], F32, kind="ExternalInput")
    b2_d = nc.dram_tensor("b2", [1, D], F32, kind="ExternalInput")
    mw_d = nc.dram_tensor("mw", [41, D], F32, kind="ExternalInput")
    wg_d = nc.dram_tensor("wg", [E, D], F32, kind="ExternalInput")
    we1_d = nc.dram_tensor("we1", [E, D, FF], F32, kind="ExternalInput")
    wb1_d = nc.dram_tensor("wb1", [E, FF], F32, kind="ExternalInput")
    we2_d = nc.dram_tensor("we2", [E, FF, D], F32, kind="ExternalInput")
    wb2_d = nc.dram_tensor("wb2", [E, D], F32, kind="ExternalInput")

    out_d = nc.dram_tensor("out", [HT, D], F32, kind="ExternalOutput")

    xn_tok_d = nc.dram_tensor("xn_tok", [T, D], BF16, kind="Internal")
    xn_ext_d = nc.dram_tensor("xn_ext", [EXT + 1, D], BF16, kind="Internal")
    acc_d = nc.dram_tensor("acc", [T + 128, D], F32, kind="Internal")

    A = nc.alloc_sbuf_tensor
    xnE = A("xnE", [128, 8, EXT], BF16)
    lgch = A("lgch", [128, 8, 256], BF16)
    p_sb = A("p_sb", [128, 8, EXT], BF16)
    chain = [A(f"chain{i}", [128, 512], BF16) for i in range(5)]
    tc0 = A("tc0", [128, 512], F32)
    tc1 = A("tc1", [128, 512], F32)
    w1sb = A("w1sb", [128, 8, D], BF16)
    xch = [A(f"xch{i}", [128, D], F32) for i in range(2)]
    xnst = [A(f"xnst{i}", [128, D], BF16) for i in range(2)]
    stat = A("stat", [128, 16], F32)
    sqscr = A("sqscr", [128, D], F32)
    lgc = A("lgc", [8, 256], F32)
    lg_t = A("lg_t", [128, 256], F32)
    smx = A("smx", [128, 32], F32)
    probs = A("probs", [128, 256], F32)
    pse = A("pse", [128, 256], F32)
    csum = A("csum", [128, 8], F32)
    cscr = A("cscr", [128, 8], F32)
    totb = A("totb", [128, 8], BF16)
    totp = A("totp", [128, 8], F32)
    mwT = A("mwT", [128, 8, 41], F32)
    taps = A("taps", [128, 8, 40], F32)
    mw0t = A("mw0t", [128, 8], F32)
    b1T = A("b1T", [128, 8], F32)
    b2T = A("b2T", [128, 8], F32)
    wgT = A("wgT", [128, 8, 8], BF16)
    mwsb = A("mwsb", [41, 128], F32)
    wgsb = A("wgsb", [8, D], F32)
    bsb = A("bsb", [1, D], F32)
    b1rb = A("b1rb", [1, D], BF16)
    b2rb = A("b2rb", [1, D], BF16)
    identf = A("identf", [128, 128], F32)
    tri = A("tri", [128, 128], BF16)
    onesc = A("onesc", [128, 512], BF16)
    onesf = A("onesf", [128, 1], F32)
    onesr = A("onesr", [1, 128], F32)
    zeros = A("zeros", [128, D], F32)
    c4096 = A("c4096", [1, 1], BF16)
    mask_halo = A("mask_halo_sb", [128, 1], F32)
    half_sel = A("half_sel_sb", [128, 32], F32)
    own_ids = A("own_ids_sb", [128, 16], I32)
    ext_ids = A("ext_ids_sb", [128, 32], I32)
    lo2 = A("lo2", [128, 8], F32)
    hi2 = A("hi2", [128, 8], F32)
    mid2 = A("mid2", [128, 8], F32)
    cnt8 = A("cnt8", [1, 8], F32)
    cmp_sc = A("cmp_sc", [128, 256], F32)
    part8 = A("part8", [128, 8], F32)
    cge = A("cge", [128, 8], F32)
    cgei_f = A("cgei_f", [128, 8], F32)
    bscr = A("bscr", [128, 8], F32)
    maskf = A("maskf", [128, 256], F32)
    maske = A("maske", [128, 8, 32], F32)
    incl = A("incl", [128, 8, 32], F32)
    lrank = A("lrank", [128, 8, 32], F32)
    cnts = A("cnts", [128, 8], F32)
    cnts_b = A("cnts_b", [128, 8], BF16)
    starts = A("starts", [128, 8], F32)
    tokid1 = A("tokid1", [128, 32], F32)
    iotaJ = A("iotaJ", [128, JSEG], F32)
    iotaS = A("iotaS", [128, 128], F32)
    indsc = A("indsc", [128, JSEG * 32], F32)
    ind2 = A("ind2", [128, JSEG * 32], F32)
    ej = A("ej", [128, 128], F32)
    sdiff = A("sdiff", [128, 128], F32)
    ej2 = A("ej2", [128, 128], F32)
    scr2x = A("scr2x", [2, 128], F32)
    lst2 = A("lst2", [128, 8, 2 * JSEG], F32)
    idsf = A("idsf", [128, 8, 3], F32)
    gval = A("gval", [128, 8, 3], F32)
    iscr = A("iscr", [128, 8, 3], F32)
    idx_g = A("idx_g", [128, 8, 3], I32)
    idx_s = A("idx_s", [128, 8, 3], I32)
    xe = A("xe", [128, 3, D], BF16)
    xeT = [A(f"xeT{i}", [128, 8, CAP], BF16) for i in range(2)]
    h1 = [A(f"h1{i}", [128, CAP], BF16) for i in range(2)]
    w1e = [A(f"w1e{i}", [128, 8, 256], BF16) for i in range(2)]
    w2e = [A(f"w2e{i}", [128, 2, D], BF16) for i in range(2)]
    b1all = A("b1all", [128, E * FF // 128], F32)
    b2er = [A(f"b2er{i}", [1, D], BF16) for i in range(2)]
    def overlay(name, shape, dtype, host, byte_off=0):
        return nc.alloc_sbuf_tensor_at(
            name, shape, dtype, offset=nc.lookup_mloc(host).addr + byte_off)

    oute = overlay("oute", [128, 3, D], F32, xnE)
    w1e.append(overlay("w1e2", [128, 8, 256], BF16, wgsb))
    w2e.append(overlay("w2e2", [128, 2, D], BF16, bsb))
    w1e.append(overlay("w1e3", [128, 8, 256], BF16, zeros))
    w2e.append(overlay("w2e3", [128, 2, D], BF16, maskf))
    accsb = [overlay("accsb0", [128, D], F32, xnE, 12288),
             overlay("accsb1", [128, D], F32, xnE, 16384)]
    gsb = overlay("gsb", [128, D], F32, xnE, 20480)
    outch = [overlay("outch0", [128, D], F32, xnE, 24576),
             overlay("outch1", [128, D], F32, xnE, 28672)]
    xe2 = overlay("xe2", [128, 3, D], BF16, sqscr)
    assert nc.lookup_mloc(smx).addr >= nc.lookup_mloc(sqscr).addr + 6144
    xeb = [xe, xe2]
    lgchB = [lgch, overlay("lgch2", [128, 8, 256], BF16, xeT[0])]

    pb = [nc.alloc_psum_tensor(f"pb{i}", [128, 512], F32) for i in range(8)]

    ctx = ExitStack()

    def sem(name):
        return ctx.enter_context(nc.semaphore(name))

    s0 = sem("s0")
    s0g = sem("s0g")
    s0f = sem("s0f")
    sZ = sem("sZ")
    s1 = sem("s1")
    s1t = sem("s1t")
    sL = [sem("sL0"), sem("sL1")]
    sLa = sem("sLa")
    sLv = sem("sLv")
    sLs = [sem("sLs0"), sem("sLs1")]
    sES = sem("sES")
    sT2b = sem("sT2b")
    sTE = sem("sTE")
    sT2 = sem("sT2")
    sP2 = sem("sP2")
    sV2 = sem("sV2")
    sE2 = sem("sE2")
    sP2b = sem("sP2b")
    sBv = sem("sBv")
    sBp = sem("sBp")
    sRv = sem("sRv")
    sRp = sem("sRp")
    sP5 = sem("sP5")
    sA5 = sem("sA5")
    sWe = [sem("sWe0"), sem("sWe1"), sem("sWe2"), sem("sWe3")]
    sWb = [sem("sWb0"), sem("sWb1")]
    sX = sem("sX")
    sXT = sem("sXT")
    sH = sem("sH")
    sPh = sem("sPh")
    sPw = sem("sPw")
    sPo = sem("sPo")
    sO = sem("sO")
    sSC = sem("sSC")
    s8w = sem("s8w")
    s8p = sem("s8p")
    s8g = [sem("s8g0"), sem("s8g1")]
    s8x = [sem("s8x0"), sem("s8x1")]
    s8a = sem("s8a")
    s8v = sem("s8v")
    s8o = sem("s8o")

    sVsn = sem("sVsn")
    _VS = {"n": 0}

    def vs(e, inst):
        _VS["n"] += 1
        inst.then_inc(sVsn, 1)
        e.wait_ge(sVsn, _VS["n"])

    band = p_sb

    _phases = []

    def _ph0():
        with nc.Block() as blk:
            @blk.sync
            def _(e):
                for (dst, src) in [
                    (mask_halo[:, :], mask_halo_d[:, :]),
                    (half_sel[:, :], half_sel_d[:, :]),
                    (own_ids[:, :], own_ids_d[:, :]),
                    (ext_ids[:, :], ext_ids_d[:, :]),
                    (wgsb[:, :], wg_d[:, :]),
                    (bsb[:, :], b1_d[:, :]),
                ]:
                    e.dma_start(out=dst, in_=src).then_inc(s0, 16)

            @blk.gpsimd
            def _(e):
                e.dma_start(
                    out=w1sb[:, :, :],
                    in_=w1_d[:, :].rearrange("(dc p) dp -> p dc dp", p=128),
                ).then_inc(s0g, 16)
                e.dma_start(out=b1rb[:, :], in_=b1_d[:, :]).then_inc(s0g, 16)
                e.dma_start(out=b2rb[:, :], in_=b2_d[:, :]).then_inc(s0g, 16)
                e.memset(zeros[:, :], 0.0).then_inc(s0f, 1)
                e.memset(onesc[:, :], 1.0).then_inc(s0f, 1)
                e.memset(onesf[:, :], 1.0).then_inc(s0f, 1)
                e.memset(onesr[:, :], 1.0).then_inc(s0f, 1)
                e.memset(c4096[:, :], 4096.0).then_inc(s0f, 1)
                e.iota(tokid1[:, :], [[128, 32]], base=1, channel_multiplier=1,
                       allow_small_or_imprecise_dtypes=True).then_inc(s0f, 1)
                e.iota(iotaJ[:, :], [[1, JSEG]], base=1, channel_multiplier=0,
                       allow_small_or_imprecise_dtypes=True).then_inc(s0f, 1)
                e.iota(iotaS[:, :], [[1, 128]], base=0, channel_multiplier=0,
                       allow_small_or_imprecise_dtypes=True).then_inc(s0f, 1)
                e.memset(identf[:, :], 1.0).then_inc(s0f, 1)
                e.memset(tri[:, :], 1.0).then_inc(s0f, 1)
                e.wait_ge(s0f, 10)
                e.affine_select(identf[:, :], identf[:, :], [[1, 128]],
                                ALU.is_equal, 0.0, base=0,
                                channel_multiplier=-1).then_inc(s0f, 1)
                e.affine_select(tri[:, :], tri[:, :], [[1, 128]],
                                ALU.is_ge, 0.0, base=-1,
                                channel_multiplier=-1).then_inc(s0f, 1)
                e.wait_ge(s0f, 12)
                for i in range(33):
                    e.dma_start(out=acc_d[128 * i:128 * (i + 1), :],
                                in_=zeros[:, :]).then_inc(sZ, 16)

    _phases.append(_ph0)

    def _ph1():
        with nc.Block() as blk:
            @blk.sync
            def _(e):
                e.wait_ge(s0, 96)
                for dc in range(8):
                    if dc > 0:
                        e.wait_ge(s1t, dc)
                    e.dma_start(out=mwsb[:, :],
                                in_=mw_d[:, 128 * dc:128 * (dc + 1)]).then_inc(s1, 16)
                e.wait_ge(s1t, 9)
                e.dma_start(out=bsb[:, :], in_=b2_d[:, :]).then_inc(s1, 16)
                for half in range(2):
                    e.dma_start(
                        out=sqscr[:, 128 * half:128 * (half + 1)],
                        in_=wb1_d.rearrange("e (ft p) -> (e ft) p", p=128)
                        [128 * half:128 * (half + 1), :]).then_inc(s1, 16)

            @blk.tensor
            def _(e):
                for dc in range(8):
                    e.wait_ge(s1, 16 * (dc + 1))
                    e.transpose(pb[0][:, 41 * dc:41 * (dc + 1)], mwsb[:, :],
                                identf[0:41, 0:41]).then_inc(s1t, 1)
                for dc in range(8):
                    e.transpose(pb[1][:, 8 * dc:8 * (dc + 1)],
                                wgsb[:, 128 * dc:128 * (dc + 1)], identf[0:8, 0:8])
                for dp in range(8):
                    inst = e.transpose(pb[2][:, dp:dp + 1],
                                       bsb[0:1, 128 * dp:128 * (dp + 1)],
                                       identf[0:1, 0:1])
                inst.then_inc(s1t, 1)
                e.wait_ge(s1, 16 * 9)
                for dp in range(8):
                    inst = e.transpose(pb[2][:, 8 + dp:9 + dp],
                                       bsb[0:1, 128 * dp:128 * (dp + 1)],
                                       identf[0:1, 0:1])
                inst.then_inc(s1t, 1)
                e.wait_ge(s1, 16 * 11)
                for half in range(2):
                    e.transpose(pb[3][:, 128 * half:128 * (half + 1)],
                                sqscr[:, 128 * half:128 * (half + 1)],
                                identf[:, :]).then_inc(s1t, 1)

            @blk.vector
            def _(e):
                e.wait_ge(s1t, 12)
                e.tensor_copy(b1all[:, :], pb[3][:, 0:256])
                e.tensor_copy(mwT[:, :, :],
                              pb[0][:, 0:328].rearrange("p (dc k) -> p dc k", dc=8))
                e.tensor_copy(wgT[:, :, :],
                              pb[1][:, 0:64].rearrange("p (dc g) -> p dc g", dc=8))
                e.tensor_copy(b1T[:, :], pb[2][:, 0:8])
                e.tensor_copy(b2T[:, :], pb[2][:, 8:16])
                e.tensor_tensor(taps[:, :, :], mwT[:, :, 1:41],
                                mwT[:, :, 0:1].broadcast_to([128, 8, 40]),
                                ALU.subtract)

    _phases.append(_ph1)

    def _ph2b(blk):
        chunks = [(x_row_d, i, xn_tok_d) for i in range(32)] + \
                 [(x_ext_d, 0, xn_ext_d)]
        NCH = len(chunks)

        @blk.gpsimd
        def _(e):
            e.wait_ge(s0, 96)
            for j in range(NCH):
                e.wait_ge(sLv, 2 * j + 2)
                src2, cj, dj = chunks[j]
                e.dma_start(out=dj[128 * cj:128 * (cj + 1), :],
                            in_=xnst[j % 2][:, :]).then_inc(sLs[j % 2], 16)
                if j < 32:
                    e.indirect_dma_start(
                        xn_ext_d[0:128, :],
                        IndirectOffsetOnAxis(ext_ids[:, j:j + 1], 0),
                        xnst[j % 2][0:128, :], None,
                    ).then_inc(sES, 16)

        @blk.sync
        def _(e):
            for tc in range(16):
                e.wait_ge(sLs[0], 16 * (tc + 1))
                e.wait_ge(sLs[1], 16 * (tc + 1))
                if tc >= 2:
                    e.wait_ge(sV2, 2 * tc - 2)
                for dc in range(8):
                    e.dma_start(
                        out=lgchB[tc % 2][:, dc, :],
                        in_=xn_tok_d[256 * tc:256 * (tc + 1),
                                     128 * dc:128 * (dc + 1)],
                        transpose=True).then_inc(sT2, 16)

        @blk.scalar
        def _(e):
            for i in range(min(2, NCH)):
                e.dma_start(out=xch[i % 2][:, :],
                            in_=chunks[i][0][128 * chunks[i][1]:
                                             128 * chunks[i][1] + 128, :]
                            ).then_inc(sL[i % 2], 16)
            for i in range(NCH):
                e.wait_ge(sL[i % 2], 16 * (i // 2 + 1))
                if i >= 1:
                    e.wait_ge(sLv, 2 * i - 1)
                e.activation(sqscr[:, :], xch[i % 2][:, :],
                             AF.Square).then_inc(sLa, 1)
                e.wait_ge(sLv, 2 * i + 1)
                e.activation(stat[:, 5:6], stat[:, 4:5],
                             AF.Sqrt).then_inc(sLa, 1)
                if i + 2 < NCH:
                    e.wait_ge(sLv, 2 * i + 2)
                    src, ci, dst = chunks[i + 2]
                    e.dma_start(out=xch[i % 2][:, :],
                                in_=src[128 * ci:128 * (ci + 1), :]
                                ).then_inc(sL[i % 2], 16)

        def _evac_tc(e, tc):
            e.wait_ge(sP2, 2 * tc + 1)
            e.tensor_copy(lgc[:, :], pb[4][0:8, 0:256]).then_inc(sV2, 1)
            vs(e, e.tensor_reduce(cscr[:, :], lgchB[tc % 2][:, :, :], AX.X,
                                  ALU.add))
            e.tensor_tensor(csum[:, :],
                            csum[:, :] if tc > 0 else zeros[:, 0:8],
                            cscr[:, :], ALU.add).then_inc(sV2, 1)

        @blk.vector
        def _(e):
            for i in range(NCH):
                e.wait_ge(sL[i % 2], 16 * (i // 2 + 1))
                e.tensor_reduce(stat[:, 0:1], xch[i % 2][:, :], AX.X, ALU.add)
                e.wait_ge(sLa, 2 * i + 1)
                vs(e, e.tensor_reduce(stat[:, 1:2], sqscr[:, :], AX.X, ALU.add))
                vs(e, e.tensor_scalar_mul(stat[:, 2:3], stat[:, 0:1], 1.0 / D))
                vs(e, e.tensor_tensor(stat[:, 3:4], stat[:, 2:3], stat[:, 2:3],
                                      ALU.mult))
                vs(e, e.scalar_tensor_tensor(stat[:, 4:5], stat[:, 1:2], 1.0 / D,
                                             stat[:, 3:4], ALU.mult,
                                             ALU.subtract))
                vs(e, e.tensor_scalar_add(stat[:, 4:5], stat[:, 4:5], EPS))
                e.reciprocal(stat[:, 4:5], stat[:, 4:5]).then_inc(sLv, 1)
                e.wait_ge(sLa, 2 * i + 2)
                if i >= 2:
                    e.wait_ge(sLs[i % 2], 16 * ((i - 2) // 2 + 1))
                    e.wait_ge(sES, 16 * (i - 1))
                e.tensor_scalar(xnst[i % 2][:, :], xch[i % 2][:, :],
                                stat[:, 2:3], stat[:, 5:6],
                                ALU.subtract, ALU.mult).then_inc(sLv, 1)
                if i >= 3 and i % 2 == 1:
                    _evac_tc(e, (i - 3) // 2)
            _evac_tc(e, 15)
            e.wait_ge(sP2, 32)
            e.tensor_copy(lg_t[:, 0:128], pb[5][:, 0:128])
            vs(e, e.tensor_copy(lg_t[:, 128:256], pb[6][:, 0:128]))
            lg3 = lg_t[:, :].rearrange("p (k e) -> p k e", e=8)
            vs(e, e.reduce_max(smx[:, :], lg3, AX.X))
            e.tensor_tensor(probs[:, :].rearrange("p (k e) -> p k e", e=8), lg3,
                            smx[:, :].unsqueeze(-1).broadcast_to([128, 32, 8]),
                            ALU.subtract).then_inc(sV2, 2)
            e.wait_ge(sE2, 1)
            pr3 = probs[:, :].rearrange("p (k e) -> p k e", e=8)
            vs(e, e.tensor_reduce(smx[:, :], pr3, AX.X, ALU.add))
            vs(e, e.reciprocal(smx[:, :], smx[:, :]))
            e.tensor_tensor(pr3, pr3,
                            smx[:, :].unsqueeze(-1).broadcast_to([128, 32, 8]),
                            ALU.mult)
            e.tensor_scalar_mul(pse[:, :], probs[:, :], 2.0)
            e.tensor_copy(totb[:, :], csum[:, :]).then_inc(sV2, 1)

    _phases.append(lambda: None)

    def _ph3(blk):
        if True:
            @blk.scalar
            def _(e):
                e.wait_ge(sV2, 34)
                e.activation(probs[:, :], probs[:, :], AF.Exp).then_inc(sE2, 1)
                e.wait_ge(sLs[0], 16 * 17)
                e.wait_ge(sES, 16 * 32)
                for dc in range(8):
                    e.dma_start(out=xnE[:, dc, :],
                                in_=xn_ext_d[0:EXT, 128 * dc:128 * (dc + 1)],
                                transpose=True).then_inc(sTE, 16)

            @blk.tensor
            def _(e):
                for tc in range(16):
                    e.wait_ge(sT2, 128 * (tc + 1))
                    if tc > 0:
                        e.wait_ge(sV2, 2 * tc)
                    for dc in range(8):
                        inst = e.matmul(pb[4][0:8, 0:256], wgT[:, dc, :],
                                        lgchB[tc % 2][:, dc, :],
                                        start=(dc == 0), stop=(dc == 7))
                    inst.then_inc(sP2, 1)
                    e.wait_ge(sV2, 2 * tc + 1)
                    for half in range(2):
                        k = 2 * tc + half
                        bank = 5 if k < 16 else 6
                        inst = e.transpose(
                            pb[bank][:, 8 * (k % 16):8 * (k % 16) + 8],
                            lgc[0:8, 128 * half:128 * (half + 1)],
                            identf[0:8, 0:8])
                    inst.then_inc(sP2, 1)

    def _phM3():
        with nc.Block() as blk:
            _ph2b(blk)
            _ph3(blk)
            _ph4(blk)
            _ph5(blk)
            _ph6(blk)

    _phases.append(_phM3)

    def _ph4(blk):
        if True:
            @blk.tensor
            def _(e):
                e.wait_ge(sV2, 35)
                e.wait_ge(s0g, 48)
                for dp in range(8):
                    for dc in range(8):
                        e.matmul(pb[7][:, 2 * dp:2 * dp + 1],
                                 w1sb[:, dc, 128 * dp:128 * (dp + 1)],
                                 totb[:, dc:dc + 1], start=(dc == 0), stop=False)
                    inst = e.matmul(pb[7][:, 2 * dp:2 * dp + 1],
                                    b1rb[0:1, 128 * dp:128 * (dp + 1)],
                                    c4096[:, :], start=False, stop=True)
                inst.then_inc(sP2b, 1)

            @blk.vector
            def _(e):
                e.wait_ge(sP2b, 1)
                vs(e, e.tensor_copy(totp[:, :],
                                    pb[7][:, 0:16].rearrange("p (a b) -> p a b", b=2)[:, :, 0]))
                e.tensor_tensor(mw0t[:, :], mwT[:, :, 0], totp[:, :], ALU.mult)

    _phases.append(lambda: None)

    def _ph5(blk):
        if True:
            @blk.vector
            def _(e):
                vs(e, e.tensor_scalar(lo2[:, :], pse[:, 0:8], 0.0, None, ALU.mult))
                vs(e, e.tensor_scalar(hi2[:, :], pse[:, 0:8], 0.0, None, ALU.mult))
                vs(e, e.tensor_scalar_add(hi2[:, :], hi2[:, :], 2.0))
                for it in range(NBIS):
                    vs(e, e.tensor_tensor(mid2[:, :], lo2[:, :], hi2[:, :], ALU.add))
                    vs(e, e.tensor_scalar_mul(mid2[:, :], mid2[:, :], 0.5))
                    e.tensor_tensor(cmp_sc[:, :].rearrange("p (k e) -> p k e", e=8),
                                    pse[:, :].rearrange("p (k e) -> p k e", e=8),
                                    mid2[:, :].unsqueeze(1).broadcast_to([128, 32, 8]),
                                    ALU.is_ge)
                    e.tensor_reduce(part8[:, :],
                                    cmp_sc[:, :].rearrange("p (k e) -> p e k", e=8),
                                    AX.X, ALU.add).then_inc(sBv, 1)
                    e.wait_ge(sBp, 2 * it + 1)
                    e.tensor_copy(cnt8[:, :], pb[4][0:1, 0:8]).then_inc(sBv, 1)
                    e.wait_ge(sBp, 2 * it + 2)
                    vs(e, e.tensor_copy(cge[:, :], pb[4][:, 8:16]))
                    vs(e, e.tensor_scalar(cgei_f[:, :], cge[:, :], 512.0, None,
                                          ALU.is_ge))
                    vs(e, e.tensor_tensor(bscr[:, :], mid2[:, :], lo2[:, :],
                                          ALU.subtract))
                    vs(e, e.tensor_tensor(bscr[:, :], bscr[:, :], cgei_f[:, :],
                                          ALU.mult))
                    vs(e, e.tensor_tensor(lo2[:, :], lo2[:, :], bscr[:, :], ALU.add))
                    vs(e, e.tensor_tensor(bscr[:, :], hi2[:, :], mid2[:, :],
                                          ALU.subtract))
                    vs(e, e.tensor_tensor(bscr[:, :], bscr[:, :], cgei_f[:, :],
                                          ALU.mult))
                    e.tensor_tensor(hi2[:, :], mid2[:, :], bscr[:, :], ALU.add
                                    ).then_inc(sBv, 1)

            @blk.tensor
            def _(e):
                for it in range(NBIS):
                    e.wait_ge(sBv, 3 * it + 1)
                    e.matmul(pb[4][0:1, 0:8], onesf[:, :], part8[:, :],
                             start=True, stop=True).then_inc(sBp, 1)
                    e.wait_ge(sBv, 3 * it + 2)
                    e.matmul(pb[4][:, 8:16], onesr[:, :], cnt8[:, :],
                             start=True, stop=True).then_inc(sBp, 1)

    _phases.append(lambda: None)

    def _ph6(blk):
        if True:
            @blk.vector
            def _(e):
                pr3 = pse[:, :].rearrange("p (k e) -> p k e", e=8)
                mf3 = maskf[:, :].rearrange("p (k e) -> p k e", e=8)
                e.tensor_tensor(mf3, pr3,
                                lo2[:, :].unsqueeze(1).broadcast_to([128, 32, 8]),
                                ALU.is_ge)
                e.tensor_tensor(mf3, mf3,
                                half_sel[:, :].unsqueeze(-1).broadcast_to([128, 32, 8]),
                                ALU.mult)
                il = None
                for ee in range(E):
                    il = e.tensor_copy(maske[:, ee, :],
                                       maskf[:, :].rearrange("p (k e) -> p e k", e=8)[:, ee, :])
                vs(e, il)
                for ee in range(E):
                    il = e.tensor_tensor_scan(incl[:, ee, :], maske[:, ee, :],
                                              zeros[:, 0:32], 0.0, ALU.add, ALU.add)
                vs(e, il)
                vs(e, e.tensor_copy(cnts[:, :], incl[:, :, 31]))
                vs(e, e.tensor_copy(cnts_b[:, :], cnts[:, :]))
                e.tensor_tensor(lrank[:, :, :], incl[:, :, :], maske[:, :, :],
                                ALU.subtract)
                e.scalar_tensor_tensor(lrank[:, :, :], lrank[:, :, :], 1.0,
                                       maske[:, :, :], ALU.add, ALU.mult
                                       ).then_inc(sRv, 1)
                e.wait_ge(sRp, 1)
                vs(e, e.tensor_copy(starts[:, :], pb[4][:, 16:24]))
                e.tensor_copy(cscr[:, 0:8], starts[:, :]).then_inc(sRv, 1)
                for ee in range(E):
                    i3 = indsc[:, :].rearrange("p (j k) -> p j k", j=JSEG)
                    o3 = ind2[:, :].rearrange("p (j k) -> p j k", j=JSEG)
                    e.tensor_tensor(
                        i3,
                        lrank[:, ee, :].unsqueeze(1).broadcast_to([128, JSEG, 32]),
                        iotaJ[:, :].unsqueeze(-1).broadcast_to([128, JSEG, 32]),
                        ALU.is_equal)
                    e.tensor_tensor(
                        o3, i3,
                        tokid1[:, :].unsqueeze(1).broadcast_to([128, JSEG, 32]),
                        ALU.mult)
                    e.tensor_reduce(lst2[:, ee, 0:JSEG], o3, AX.X, ALU.add)
                    e.tensor_tensor(
                        o3, i3,
                        probs[:, :].rearrange("p (k e) -> p e k", e=8)[:, ee, :]
                            .unsqueeze(1).broadcast_to([128, JSEG, 32]),
                        ALU.mult)
                    e.tensor_reduce(lst2[:, ee, JSEG:2 * JSEG], o3, AX.X,
                                    ALU.add).then_inc(sRv, 1)
                ejb = [ej, ej2]
                rv, rp = 2 + E, 1
                for ee in range(E):
                    vs(e, e.tensor_single_scalar(sdiff[:, :], iotaS[:, :],
                                                 starts[:, ee:ee + 1],
                                                 ALU.subtract))
                    for cchunk in range(3):
                        for j in range(JSEG):
                            if j >= 2:
                                e.wait_ge(sRp, rp - 1)
                            elif 12 * cchunk + j >= 2 or ee > 0:
                                e.wait_ge(sRp, rp - 2)
                            e.tensor_scalar(ejb[j % 2][:, :], sdiff[:, :],
                                            float(j - 128 * cchunk), None,
                                            ALU.is_equal).then_inc(sRv, 1)
                            rv += 1
                            rp += 1
                        e.wait_ge(sRp, rp)
                        rp += 1
                        e.tensor_copy(scr2x[:, :], pb[5][0:2, 0:128]
                                      ).then_inc(sRv, 1)
                        rv += 1
                        e.wait_ge(sRp, rp)
                        vs(e, e.tensor_scalar(idsf[:, ee, cchunk:cchunk + 1],
                                              pb[5][:, 128:129],
                                              1.0, None, ALU.subtract))
                        inst = e.tensor_copy(gval[:, ee, cchunk:cchunk + 1],
                                             pb[5][:, 129:130])
                        inst.then_inc(sRv, 1)
                        rv += 1
                vs(e, e.tensor_copy(cscr[:, 0:8], starts[:, :]))
                vs(e, e.tensor_scalar_max(iscr[:, :, :], idsf[:, :, :], 0.0))
                vs(e, e.tensor_copy(idx_g[:, :, :], iscr[:, :, :]))
                vs(e, e.tensor_scalar(iscr[:, :, :], idsf[:, :, :], 0.0, None,
                                      ALU.is_lt))
                vs(e, e.scalar_tensor_tensor(iscr[:, :, :], iscr[:, :, :],
                                             float(DUMP + 1), idsf[:, :, :],
                                             ALU.mult, ALU.add))
                e.tensor_copy(idx_s[:, :, :], iscr[:, :, :])

            @blk.tensor
            def _(e):
                e.wait_ge(sRv, 1)
                e.matmul(pb[4][:, 16:24], tri[:, :], cnts_b[:, :],
                         start=True, stop=True).then_inc(sRp, 1)
                e.wait_ge(sRv, 2 + E)
                ejb = [ej, ej2]
                rv, rp = 2 + E, 1
                for ee in range(E):
                    for cchunk in range(3):
                        for j in range(JSEG):
                            rv += 1
                            e.wait_ge(sRv, rv)
                            e.matmul(pb[5][0:2, 0:128],
                                     lst2[:, ee, :].rearrange("p (a j) -> p j a", a=2)[:, j, :],
                                     ejb[j % 2][:, :],
                                     start=(j == 0), stop=(j == JSEG - 1)
                                     ).then_inc(sRp, 1)
                            rp += 1
                        rv += 1
                        e.wait_ge(sRv, rv)
                        e.transpose(pb[5][:, 128:130], scr2x[:, :],
                                    identf[0:2, 0:2]).then_inc(sRp, 1)
                        rp += 1
                        rv += 1

    _phases.append(lambda: None)

    def _ph7(blk):
        cols5 = [(0, 512), (512, 1024), (1024, 1536), (1536, 2048), (2048, EXT)]
        if True:
            @blk.tensor
            def _(e):
                e.wait_ge(sTE, 128)
                e.wait_ge(sV2, 34)
                idx = 0
                for dp in range(8):
                    for (c0, c1) in cols5:
                        if idx >= 2:
                            e.wait_ge(sA5, idx - 1)
                        bank = pb[4 + (idx % 2)]
                        for dc in range(8):
                            inst = e.matmul(bank[:, 0:c1 - c0],
                                            w1sb[:, dc, 128 * dp:128 * (dp + 1)],
                                            xnE[:, dc, c0:c1], start=(dc == 0),
                                            stop=(dc == 7))
                        inst.then_inc(sP5, 1)
                        idx += 1

            @blk.scalar
            def _(e):
                idx = 0
                for dp in range(8):
                    for (c0, c1) in cols5:
                        e.wait_ge(sP5, idx + 1)
                        e.activation(p_sb[:, dp, c0:c1],
                                     pb[4 + (idx % 2)][:, 0:c1 - c0], AF.Identity,
                                     bias=b1T[:, dp:dp + 1]).then_inc(sA5, 1)
                        idx += 1

    _phases.append(lambda: None)

    def _ph8():
        pass

    _phases.append(_ph8)

    def _ph9():
        TT3 = [(0, 128), (128, 256), (256, 320)]
        with nc.Block() as blk:
            _ph7(blk)
            def _gather(e, ee, ti):
                if CFG["p7_gather"]:
                    e.indirect_dma_start(
                        xeb[ee % 2][:, ti, :], None,
                        xn_tok_d[0:128, :],
                        IndirectOffsetOnAxis(idx_g[:, ee, ti:ti + 1], 0),
                    ).then_inc(sX, 16)
                else:
                    e.dma_start(out=xeb[ee % 2][:, ti, :],
                                in_=xn_tok_d[0:128, :]).then_inc(sX, 16)

            def _scatter(e, ee, ti, r0, r1):
                if CFG["p7_scatter"]:
                    e.indirect_dma_start(
                        acc_d[0:r1 - r0, :],
                        IndirectOffsetOnAxis(idx_s[0:r1 - r0, ee, ti:ti + 1], 0),
                        oute[0:r1 - r0, ti, :], None,
                        compute_op=ALU.add,
                    ).then_inc(sSC, 16)
                else:
                    e.dma_start(out=acc_d[T:T + r1 - r0, :],
                                in_=oute[0:r1 - r0, ti, :]).then_inc(sSC, 16)

            @blk.gpsimd
            def _(e):
                for ti in range(3):
                    _gather(e, 0, ti)
                for ee in range(CFG['p7_E']):
                    if ee + 1 < CFG['p7_E']:
                        if ee >= 1:
                            e.wait_ge(sXT, 16 * 24 * ee)
                        if CFG["p7_scblock"] and ee >= 2:
                            e.wait_ge(sSC, 48 * (ee - 1))
                        for ti in range(3):
                            _gather(e, ee + 1, ti)
                    if ee >= 2:
                        e.wait_ge(sPo, 6 * (ee - 1))
                    e.dma_start(out=b2er[ee % 2][:, :], in_=wb2_d[ee:ee + 1, :]
                                ).then_inc(sWb[ee % 2], 16)
                    for g in range(16):
                        G = 16 * ee + g
                        if G >= 4:
                            e.wait_ge(sPw, 2 * (G - 3))
                        e.dma_start(
                            out=w1e[G % 4][:, :, :],
                            in_=we1_d[ee, :, 256 * g:256 * (g + 1)]
                                .rearrange("(dc p) f -> p dc f", p=128),
                        ).then_inc(sWe[G % 4], 16)
                        e.dma_start(
                            out=w2e[G % 4][:, :, :],
                            in_=we2_d[ee, 256 * g:256 * (g + 1), :]
                                .rearrange("(ft p) dd -> p ft dd", p=128),
                        ).then_inc(sWe[G % 4], 16)
                    if ee > 0 and CFG["p7_scblock"]:
                        e.wait_ge(sO, 6 * ee)
                        e.wait_ge(sSC, 48 * (ee - 1))
                        e.wait_ge(sXT, 16 * 24 * min(ee + 2, CFG['p7_E']))
                        if ee == 1:
                            e.wait_ge(sZ, 528)
                        for ti, (r0, r1) in enumerate(TT3):
                            _scatter(e, ee - 1, ti, r0, r1)
                e.wait_ge(sO, 6 * CFG['p7_E'])
                if not CFG["p7_scblock"]:
                    return
                e.wait_ge(sSC, 48 * (CFG['p7_E'] - 1))
                for ti, (r0, r1) in enumerate(TT3):
                    if CFG["p7_scatter"]:
                        e.indirect_dma_start(
                            acc_d[0:r1 - r0, :],
                            IndirectOffsetOnAxis(idx_s[0:r1 - r0, CFG['p7_E'] - 1, ti:ti + 1], 0),
                            oute[0:r1 - r0, ti, :], None,
                            compute_op=ALU.add,
                        ).then_inc(sSC, 16)
                    else:
                        e.dma_start(out=acc_d[T:T + r1 - r0, :],
                                    in_=oute[0:r1 - r0, ti, :]).then_inc(sSC, 16)
                e.wait_ge(sSC, 48 * CFG['p7_E'])

            @blk.sync
            def _(e):
                for ee in range(CFG['p7_E']):
                    if ee >= 2:
                        e.wait_ge(sPh, 32 * (ee - 1))
                    for ti, (r0, r1) in enumerate(TT3):
                        e.wait_ge(sX, 48 * ee + 16 * (ti + 1))
                        for dc in range(8):
                            e.dma_start(out=xeT[ee % 2][:, dc, r0:r1],
                                        in_=xeb[ee % 2][0:r1 - r0, ti,
                                                        128 * dc:128 * (dc + 1)],
                                        transpose=True).then_inc(sXT, 16)

            @blk.scalar
            def _(e):
                for ee in range(CFG['p7_E']):
                    for ft in range(32):
                        FT = 32 * ee + ft
                        e.wait_ge(sPh, FT + 1)
                        if FT >= 2:
                            e.wait_ge(sPw, FT - 1)
                        e.activation(h1[ft % 2][:, :], pb[6 + (ft % 2)][:, 0:CAP],
                                     AF.Gelu_apprx_tanh,
                                     bias=b1all[:, 32 * ee + ft:32 * ee + ft + 1]
                                     ).then_inc(sH, 1)

            def self_w2(e, ee, ft):
                FT = 32 * ee + ft
                g = ft // 2
                G = 16 * ee + g
                e.wait_ge(sH, FT + 1)
                if ft == 0 and ee > 0:
                    e.wait_ge(sO, 6 * ee)
                for ti, (r0, r1) in enumerate(TT3):
                    for dh in range(2):
                        inst = e.matmul(
                            pb[2 * ti + dh][0:r1 - r0, 0:512],
                            h1[ft % 2][:, r0:r1],
                            w2e[G % 4][:, ft % 2, 512 * dh:512 * (dh + 1)],
                            start=(ft == 0), stop=False)
                inst.then_inc(sPw, 1)

            @blk.tensor
            def _(e):
                for ee in range(CFG['p7_E']):
                    e.wait_ge(sXT, 16 * 24 * (ee + 1))
                    e.wait_ge(sWb[ee % 2], 16 * (ee // 2 + 1))
                    for ft in range(32):
                        FT = 32 * ee + ft
                        g = ft // 2
                        G = 16 * ee + g
                        if ft % 2 == 0:
                            e.wait_ge(sWe[G % 4], 32 * (G // 4 + 1))
                        if FT >= 2:
                            e.wait_ge(sH, FT - 1)
                        hb = pb[6 + (ft % 2)]
                        for dc in range(8):
                            inst = e.matmul(
                                hb[:, 0:CAP],
                                w1e[G % 4][:, dc, 128 * (ft % 2):128 * (ft % 2 + 1)],
                                xeT[ee % 2][:, dc, :],
                                start=(dc == 0), stop=(dc == 7))
                        inst.then_inc(sPh, 1)
                        if ft >= 1:
                            self_w2(e, ee, ft - 1)
                    self_w2(e, ee, 31)
                    for ti, (r0, r1) in enumerate(TT3):
                        for dh in range(2):
                            inst = e.matmul(
                                pb[2 * ti + dh][0:r1 - r0, 0:512],
                                onesc[0:1, 0:r1 - r0],
                                b2er[ee % 2][0:1, 512 * dh:512 * (dh + 1)],
                                start=False, stop=True)
                    inst.then_inc(sPo, 6)

            @blk.vector
            def _(e):
                for ee in range(CFG['p7_E']):
                    dp = ee
                    e.wait_ge(sA5, 5 * (dp + 1))
                    e.tensor_scalar_mul(p_sb[:, dp, 0:128], p_sb[:, dp, 0:128],
                                        mask_halo[:, 0:1])
                    for q in range(4):
                        base = 128 + 512 * q
                        for c in range(5):
                            k0 = 8 * c
                            e.tensor_scalar_mul(chain[c][:, :],
                                                p_sb[:, dp, base - k0:base - k0 + 512],
                                                taps[:, dp, k0:k0 + 1])
                            for k in range(k0 + 1, k0 + 8):
                                e.scalar_tensor_tensor(
                                    chain[c][:, :],
                                    p_sb[:, dp, base - k:base - k + 512],
                                    taps[:, dp, k:k + 1], chain[c][:, :],
                                    ALU.mult, ALU.add)
                        e.tensor_tensor(tc0[:, :], chain[0][:, :], chain[1][:, :],
                                        ALU.add)
                        e.tensor_tensor(tc1[:, :], chain[2][:, :], chain[3][:, :],
                                        ALU.add)
                        e.tensor_tensor(tc0[:, :], tc0[:, :], tc1[:, :], ALU.add)
                        e.scalar_tensor_tensor(band[:, dp, 512 * q:512 * (q + 1)],
                                               chain[4][:, :], mw0t[:, dp:dp + 1],
                                               tc0[:, :], ALU.add, ALU.add)
                    if ee > 0 and CFG["p7_scblock"]:
                        e.wait_ge(sSC, 48 * ee)
                    for ti, (r0, r1) in enumerate(TT3):
                        for dh in range(2):
                            e.wait_ge(sPo, 6 * (ee + 1))
                            e.tensor_scalar_mul(
                                oute[0:r1 - r0, ti, 512 * dh:512 * (dh + 1)],
                                pb[2 * ti + dh][0:r1 - r0, 0:512],
                                gval[0:r1 - r0, ee, ti:ti + 1]).then_inc(sO, 1)

    _phases.append(_ph9)

    def _ph10():
        with nc.Block() as blk:
            @blk.gpsimd
            def _(e):
                e.dma_start(
                    out=w1sb[:, :, :],
                    in_=w2_d[:, :].rearrange("(dp p) dd -> p dp dd", p=128),
                ).then_inc(s8w, 16)
                for tt in range(16):
                    if tt >= 2:
                        e.wait_ge(s8a, tt - 1)
                    e.indirect_dma_start(
                        accsb[tt % 2][:, :], None,
                        acc_d[0:128, :],
                        IndirectOffsetOnAxis(own_ids[:, tt:tt + 1], 0),
                    ).then_inc(s8g[tt % 2], 16)

            @blk.sync
            def _(e):
                for tt in range(16):
                    if tt >= 2:
                        e.wait_ge(s8v, 2 * (tt - 1))
                    e.dma_start(out=xch[tt % 2][:, :],
                                in_=x_ext_d[128 * (tt + 1):128 * (tt + 2), :]
                                ).then_inc(s8x[tt % 2], 16)
                    if tt >= 2:
                        j = tt - 2
                        e.wait_ge(s8v, 2 * j + 2)
                        e.dma_start(out=out_d[128 * j:128 * (j + 1), :],
                                    in_=outch[j % 2][:, :]).then_inc(s8o, 16)
                for j in (14, 15):
                    e.wait_ge(s8v, 2 * j + 2)
                    e.dma_start(out=out_d[128 * j:128 * (j + 1), :],
                                in_=outch[j % 2][:, :]).then_inc(s8o, 16)
                e.wait_ge(s8o, 16 * 16)

            @blk.tensor
            def _(e):
                e.wait_ge(s8w, 16)
                for tt in range(16):
                    if tt >= 2:
                        e.wait_ge(s8v, 2 * (tt - 1))
                    for dp in range(8):
                        for dh in range(2):
                            e.matmul(pb[2 * (tt % 2) + dh][:, 0:512],
                                     p_sb[:, dp, 128 * tt:128 * (tt + 1)],
                                     w1sb[:, dp, 512 * dh:512 * (dh + 1)],
                                     start=(dp == 0), stop=False)
                    for dh in range(2):
                        e.matmul(pb[2 * (tt % 2) + dh][:, 0:512],
                                 onesc[0:1, 0:128],
                                 b2rb[0:1, 512 * dh:512 * (dh + 1)],
                                 start=False, stop=True).then_inc(s8p, 1)

            @blk.scalar
            def _(e):
                for tt in range(16):
                    e.wait_ge(s8g[tt % 2], 16 * (tt // 2 + 1))
                    if tt > 0:
                        e.wait_ge(s8v, 2 * tt)
                    e.activation(gsb[:, :], accsb[tt % 2][:, :],
                                 AF.Silu).then_inc(s8a, 1)

            @blk.vector
            def _(e):
                for tt in range(16):
                    e.wait_ge(s8a, tt + 1)
                    e.wait_ge(s8p, 2 * tt + 2)
                    e.wait_ge(s8x[tt % 2], 16 * (tt // 2 + 1))
                    if tt >= 2:
                        e.wait_ge(s8o, 16 * (tt - 1))
                    for dh in range(2):
                        e.tensor_tensor(outch[tt % 2][:, 512 * dh:512 * (dh + 1)],
                                        gsb[:, 512 * dh:512 * (dh + 1)],
                                        pb[2 * (tt % 2) + dh][:, 0:512], ALU.mult)
                    e.tensor_tensor(outch[tt % 2][:, :], outch[tt % 2][:, :],
                                    xch[tt % 2][:, :], ALU.add).then_inc(s8v, 2)

    _phases.append(_ph10)

    for _k in range(min(upto + 1, len(_phases))):
        _phases[_k]()

    if dbg:
        dbg_items = [
            ("d_probs", probs), ("d_pse", pse), ("d_lo2", lo2),
            ("d_cnts", cnts), ("d_starts", starts), ("d_csum", csum),
            ("d_totp", totp), ("d_mw0t", mw0t), ("d_b1T", b1T),
            ("d_b2T", b2T), ("d_lgt", lg_t), ("d_lrank", lrank),
            ("d_idsf", idsf), ("d_gval", gval), ("d_taps", taps),
            ("d_idxg", idx_g), ("d_idxs", idx_s), ("d_xnE", xnE),
            ("d_p", p_sb),
        ]
        sD = sem("sD")
        with nc.Block() as blk:
            @blk.sync
            def _(e):
                n = 0
                for name, t in dbg_items:
                    sh = list(t.shape)
                    fl = int(np.prod(sh[1:]))
                    od = nc.dram_tensor(name, [sh[0], fl], t.dtype,
                                        kind="ExternalOutput")
                    src = t[tuple(slice(None) for _ in sh)]
                    if len(sh) == 3:
                        src = src.rearrange("p a b -> p (a b)")
                    e.dma_start(out=od[:, :], in_=src).then_inc(sD, 16)
                    n += 16
                e.wait_ge(sD, n)

    ctx.close()
    return nc


def kernel(L0_attn_input_data, L0_attn_W1, L0_attn_bias1, L0_attn_W2,
           L0_attn_bias2, L0_attn_memory_weights, L0_ffn_Wg, L0_ffn_W1,
           L0_ffn_b1, L0_ffn_W2, L0_ffn_b2):
    def host_f32(a):
        a = np.asarray(a)
        return np.ascontiguousarray(a.astype(np.float32, copy=False))

    x = host_f32(L0_attn_input_data)
    w1 = host_f32(L0_attn_W1)
    b1 = host_f32(L0_attn_bias1).reshape(1, D)
    b2 = host_f32(L0_attn_bias2).reshape(1, D)
    w2 = host_f32(L0_attn_W2)
    mw = host_f32(L0_attn_memory_weights)
    wg = host_f32(L0_ffn_Wg)
    we1 = host_f32(L0_ffn_W1)
    wb1 = host_f32(L0_ffn_b1)
    we2 = host_f32(L0_ffn_W2)
    wb2 = host_f32(L0_ffn_b2)

    nc = build_nc()
    in_maps = make_in_maps(x, w1, b1, w2, b2, mw, wg, we1, wb1, we2, wb2)

    res = run_bass_kernel_spmd(nc, in_maps, core_ids=list(range(8)))
    out = np.empty((B, T, D), np.float32)
    for c in range(8):
        b, h = c // 2, c % 2
        out[b, HT * h:HT * (h + 1), :] = res.results[c]["out"]
    return out


def make_in_maps(x, w1, b1, w2, b2, mw, wg, we1, wb1, we2, wb2):
    pcol = np.arange(128)
    in_maps = []
    for c in range(8):
        b, h = c // 2, c % 2
        x_row = x[b]
        if h == 0:
            halo = np.zeros((128, D), np.float32)
        else:
            halo = x_row[HT - 128:HT]
        x_ext = np.concatenate([halo, x_row[HT * h:HT * (h + 1)]], axis=0)
        mask_halo = np.full((128, 1), float(h), np.float32)
        ks = np.arange(32)
        half_sel = np.broadcast_to(((ks // 16) == h).astype(np.float32),
                                   (128, 32)).copy()
        own_ids = (HT * h + 128 * np.arange(16)[None, :] +
                   pcol[:, None]).astype(np.int32)
        tok = 128 * np.arange(32)[None, :] + pcol[:, None]
        own = (tok // HT) == h
        ext_ids = np.where(own, 128 + tok - HT * h, EXT).astype(np.int32)
        in_maps.append({
            "x_row": np.ascontiguousarray(x_row),
            "x_ext": np.ascontiguousarray(x_ext),
            "mask_halo": mask_halo,
            "half_sel": half_sel,
            "own_ids": own_ids,
            "ext_ids": ext_ids,
            "w1": w1, "b1": b1, "w2": w2, "b2": b2, "mw": mw, "wg": wg,
            "we1": we1, "wb1": wb1, "we2": we2, "wb2": wb2,
        })
    return in_maps


# version 24
# speedup vs baseline: 1.1170x; 1.1170x over previous
import numpy as np
from contextlib import ExitStack

import concourse.bass as bass
import concourse.mybir as mybir
from concourse.bass import IndirectOffsetOnAxis
from concourse.bass_utils import run_bass_kernel_spmd
import concourse.bass_utils as _bass_utils

if not getattr(_bass_utils, "_ldw_patched", False):
    _orig_run_command = _bass_utils.run_command

    def _run_command_ldw(argv, **kwargs):
        argv = ["--enable-ldw-opt=true" if a == "--enable-ldw-opt=false" else a
                for a in argv]
        return _orig_run_command(argv, **kwargs)

    _bass_utils.run_command = _run_command_ldw
    _bass_utils._ldw_patched = True

F32 = mybir.dt.float32
BF16 = mybir.dt.bfloat16
I32 = mybir.dt.int32
AF = mybir.ActivationFunctionType
ALU = mybir.AluOpType
AX = mybir.AxisListType

B, T, D = 4, 4096, 1024
HT = T // 2
EXT = HT + 128
E, FF = 8, 4096
CAP = 320
JSEG = 12
EPS = 1e-5
NBIS = 36
DUMP = T


CFG = {"p7_gather": True, "p7_transpose": True, "p7_scatter": True,
       "p7_w2": True, "p7_E": 8, "p7_scblock": True}


def build_nc(upto=99, dbg=False):
    nc = bass.Bass("TRN2", target_bir_lowering=False,
                   dynamic_dma_scratch_size=16384)

    x_row_d = nc.dram_tensor("x_row", [T, D], F32, kind="ExternalInput")
    x_ext_d = nc.dram_tensor("x_ext", [EXT, D], F32, kind="ExternalInput")
    mask_halo_d = nc.dram_tensor("mask_halo", [128, 1], F32, kind="ExternalInput")
    half_sel_d = nc.dram_tensor("half_sel", [128, 32], F32, kind="ExternalInput")
    own_ids_d = nc.dram_tensor("own_ids", [128, 16], I32, kind="ExternalInput")
    ext_ids_d = nc.dram_tensor("ext_ids", [128, 32], I32, kind="ExternalInput")
    w1_d = nc.dram_tensor("w1", [D, D], F32, kind="ExternalInput")
    b1_d = nc.dram_tensor("b1", [1, D], F32, kind="ExternalInput")
    w2_d = nc.dram_tensor("w2", [D, D], F32, kind="ExternalInput")
    b2_d = nc.dram_tensor("b2", [1, D], F32, kind="ExternalInput")
    mw_d = nc.dram_tensor("mw", [41, D], F32, kind="ExternalInput")
    wg_d = nc.dram_tensor("wg", [E, D], F32, kind="ExternalInput")
    we1_d = nc.dram_tensor("we1", [E, D, FF], F32, kind="ExternalInput")
    wb1_d = nc.dram_tensor("wb1", [E, FF], F32, kind="ExternalInput")
    we2_d = nc.dram_tensor("we2", [E, FF, D], F32, kind="ExternalInput")
    wb2_d = nc.dram_tensor("wb2", [E, D], F32, kind="ExternalInput")

    out_d = nc.dram_tensor("out", [HT, D], F32, kind="ExternalOutput")

    xn_tok_d = nc.dram_tensor("xn_tok", [T, D], BF16, kind="Internal")
    xn_ext_d = nc.dram_tensor("xn_ext", [EXT + 1, D], BF16, kind="Internal")
    acc_d = nc.dram_tensor("acc", [T + 128, D], F32, kind="Internal")

    A = nc.alloc_sbuf_tensor
    xnE = A("xnE", [128, 8, EXT], BF16)
    lgch = A("lgch", [128, 8, 256], BF16)
    p_sb = A("p_sb", [128, 8, EXT], BF16)
    chain = [A(f"chain{i}", [128, 512], BF16) for i in range(5)]
    tc0 = A("tc0", [128, 512], F32)
    tc1 = A("tc1", [128, 512], F32)
    w1sb = A("w1sb", [128, 8, D], BF16)
    xch = [A(f"xch{i}", [128, D], F32) for i in range(2)]
    xnst = [A(f"xnst{i}", [128, D], BF16) for i in range(2)]
    stat = A("stat", [128, 16], F32)
    sqscr = A("sqscr", [128, D], F32)
    lgc = A("lgc", [8, 256], F32)
    lg_t = A("lg_t", [128, 256], F32)
    smx = A("smx", [128, 32], F32)
    probs = A("probs", [128, 256], F32)
    pse = A("pse", [128, 256], F32)
    csum = A("csum", [128, 8], F32)
    cscr = A("cscr", [128, 8], F32)
    totb = A("totb", [128, 8], BF16)
    totp = A("totp", [128, 8], F32)
    mwT = A("mwT", [128, 8, 41], F32)
    taps = A("taps", [128, 8, 40], F32)
    mw0t = A("mw0t", [128, 8], F32)
    b1T = A("b1T", [128, 8], F32)
    b2T = A("b2T", [128, 8], F32)
    wgT = A("wgT", [128, 8, 8], BF16)
    mwsb = A("mwsb", [41, 128], F32)
    wgsb = A("wgsb", [8, D], F32)
    bsb = A("bsb", [1, D], F32)
    b1rb = A("b1rb", [1, D], BF16)
    b2rb = A("b2rb", [1, D], BF16)
    identf = A("identf", [128, 128], F32)
    tri = A("tri", [128, 128], BF16)
    onesc = A("onesc", [128, 512], BF16)
    onesf = A("onesf", [128, 1], F32)
    onesr = A("onesr", [1, 128], F32)
    zeros = A("zeros", [128, D], F32)
    c4096 = A("c4096", [1, 1], BF16)
    mask_halo = A("mask_halo_sb", [128, 1], F32)
    half_sel = A("half_sel_sb", [128, 32], F32)
    own_ids = A("own_ids_sb", [128, 16], I32)
    ext_ids = A("ext_ids_sb", [128, 32], I32)
    lo2 = A("lo2", [128, 8], F32)
    hi2 = A("hi2", [128, 8], F32)
    mid2 = A("mid2", [128, 8], F32)
    cnt8 = A("cnt8", [1, 8], F32)
    cmp_sc = A("cmp_sc", [128, 256], F32)
    part8 = A("part8", [128, 8], F32)
    cge = A("cge", [128, 8], F32)
    cgei_f = A("cgei_f", [128, 8], F32)
    bscr = A("bscr", [128, 8], F32)
    maskf = A("maskf", [128, 256], F32)
    maske = A("maske", [128, 8, 32], F32)
    incl = A("incl", [128, 8, 32], F32)
    lrank = A("lrank", [128, 8, 32], F32)
    cnts = A("cnts", [128, 8], F32)
    cnts_b = A("cnts_b", [128, 8], BF16)
    starts = A("starts", [128, 8], F32)
    tokid1 = A("tokid1", [128, 32], F32)
    iotaJ = A("iotaJ", [128, JSEG], F32)
    iotaS = A("iotaS", [128, 128], F32)
    indsc = A("indsc", [128, JSEG * 32], F32)
    ind2 = A("ind2", [128, JSEG * 32], F32)
    ej = A("ej", [128, 128], F32)
    sdiff = A("sdiff", [128, 128], F32)
    ej2 = A("ej2", [128, 128], F32)
    scr2x = A("scr2x", [2, 128], F32)
    lst2 = A("lst2", [128, 8, 2 * JSEG], F32)
    idsf = A("idsf", [128, 8, 3], F32)
    gval = A("gval", [128, 8, 3], F32)
    iscr = A("iscr", [128, 8, 3], F32)
    idx_g = A("idx_g", [128, 8, 3], I32)
    idx_s = A("idx_s", [128, 8, 3], I32)
    xe = A("xe", [128, 3, D], BF16)
    xeT = [A(f"xeT{i}", [128, 8, CAP], BF16) for i in range(2)]
    h1 = [A(f"h1{i}", [128, CAP], BF16) for i in range(2)]
    w1e = [A(f"w1e{i}", [128, 8, 256], BF16) for i in range(2)]
    w2e = [A(f"w2e{i}", [128, 2, D], BF16) for i in range(2)]
    b1all = A("b1all", [128, E * FF // 128], F32)
    b2er = [A(f"b2er{i}", [1, D], BF16) for i in range(2)]
    def overlay(name, shape, dtype, host, byte_off=0):
        return nc.alloc_sbuf_tensor_at(
            name, shape, dtype, offset=nc.lookup_mloc(host).addr + byte_off)

    oute = overlay("oute", [128, 3, D], F32, xnE)
    w1e.append(overlay("w1e2", [128, 8, 256], BF16, wgsb))
    w2e.append(overlay("w2e2", [128, 2, D], BF16, bsb))
    w1e.append(overlay("w1e3", [128, 8, 256], BF16, zeros))
    w2e.append(overlay("w2e3", [128, 2, D], BF16, maskf))
    accsb = [overlay("accsb0", [128, D], F32, xnE, 12288),
             overlay("accsb1", [128, D], F32, xnE, 16384)]
    gsb = overlay("gsb", [128, D], F32, xnE, 20480)
    outch = [overlay("outch0", [128, D], F32, xnE, 24576),
             overlay("outch1", [128, D], F32, xnE, 28672)]
    xe2 = overlay("xe2", [128, 3, D], BF16, sqscr)
    assert nc.lookup_mloc(smx).addr >= nc.lookup_mloc(sqscr).addr + 6144
    xeb = [xe, xe2]
    lgchB = [lgch, overlay("lgch2", [128, 8, 256], BF16, xeT[0])]

    pb = [nc.alloc_psum_tensor(f"pb{i}", [128, 512], F32) for i in range(8)]

    ctx = ExitStack()

    def sem(name):
        return ctx.enter_context(nc.semaphore(name))

    s0 = sem("s0")
    s0g = sem("s0g")
    s0f = sem("s0f")
    sB1 = sem("sB1")
    sZ = sem("sZ")
    s1 = sem("s1")
    s1t = sem("s1t")
    sL = [sem("sL0"), sem("sL1")]
    sLa = sem("sLa")
    sLv = sem("sLv")
    sLs = [sem("sLs0"), sem("sLs1")]
    sES = sem("sES")
    sT2b = sem("sT2b")
    sTE = sem("sTE")
    sT2 = sem("sT2")
    sP2 = sem("sP2")
    sV2 = sem("sV2")
    sE2 = sem("sE2")
    sP2b = sem("sP2b")
    sBv = sem("sBv")
    sBp = sem("sBp")
    sRv = sem("sRv")
    sRp = sem("sRp")
    sP5 = sem("sP5")
    sA5 = sem("sA5")
    sWe = [sem("sWe0"), sem("sWe1"), sem("sWe2"), sem("sWe3")]
    sWb = [sem("sWb0"), sem("sWb1")]
    sX = sem("sX")
    sXT = sem("sXT")
    sH = sem("sH")
    sPh = sem("sPh")
    sPw = sem("sPw")
    sPo = sem("sPo")
    sO = sem("sO")
    sSC = sem("sSC")
    s8w = sem("s8w")
    s8p = sem("s8p")
    s8g = [sem("s8g0"), sem("s8g1")]
    s8x = [sem("s8x0"), sem("s8x1")]
    s8a = sem("s8a")
    s8v = sem("s8v")
    s8o = sem("s8o")

    sVsn = sem("sVsn")
    _VS = {"n": 0}

    def vs(e, inst):
        _VS["n"] += 1
        inst.then_inc(sVsn, 1)
        e.wait_ge(sVsn, _VS["n"])

    band = p_sb

    _phases = []

    def _ph0():
        with nc.Block() as blk:
            @blk.sync
            def _(e):
                for (dst, src) in [
                    (mask_halo[:, :], mask_halo_d[:, :]),
                    (half_sel[:, :], half_sel_d[:, :]),
                    (own_ids[:, :], own_ids_d[:, :]),
                    (ext_ids[:, :], ext_ids_d[:, :]),
                    (wgsb[:, :], wg_d[:, :]),
                    (bsb[:, :], b1_d[:, :]),
                ]:
                    e.dma_start(out=dst, in_=src).then_inc(s0, 16)

            @blk.gpsimd
            def _(e):
                e.dma_start(
                    out=w1sb[:, :, :],
                    in_=w1_d[:, :].rearrange("(dc p) dp -> p dc dp", p=128),
                ).then_inc(s0g, 16)
                e.dma_start(out=b1rb[:, :], in_=b1_d[:, :]).then_inc(s0g, 16)
                e.dma_start(out=b2rb[:, :], in_=b2_d[:, :]).then_inc(s0g, 16)
                e.memset(zeros[:, :], 0.0).then_inc(s0f, 1)
                e.memset(onesc[:, :], 1.0).then_inc(s0f, 1)
                e.memset(onesf[:, :], 1.0).then_inc(s0f, 1)
                e.memset(onesr[:, :], 1.0).then_inc(s0f, 1)
                e.memset(c4096[:, :], 4096.0).then_inc(s0f, 1)
                e.iota(tokid1[:, :], [[128, 32]], base=1, channel_multiplier=1,
                       allow_small_or_imprecise_dtypes=True).then_inc(s0f, 1)
                e.iota(iotaJ[:, :], [[1, JSEG]], base=1, channel_multiplier=0,
                       allow_small_or_imprecise_dtypes=True).then_inc(s0f, 1)
                e.iota(iotaS[:, :], [[1, 128]], base=0, channel_multiplier=0,
                       allow_small_or_imprecise_dtypes=True).then_inc(s0f, 1)
                e.memset(identf[:, :], 1.0).then_inc(s0f, 1)
                e.memset(tri[:, :], 1.0).then_inc(s0f, 1)
                e.wait_ge(s0f, 10)
                e.affine_select(identf[:, :], identf[:, :], [[1, 128]],
                                ALU.is_equal, 0.0, base=0,
                                channel_multiplier=-1).then_inc(s0f, 1)
                e.affine_select(tri[:, :], tri[:, :], [[1, 128]],
                                ALU.is_ge, 0.0, base=-1,
                                channel_multiplier=-1).then_inc(s0f, 1)
                e.wait_ge(s0f, 12)
                for i in range(33):
                    e.dma_start(out=acc_d[128 * i:128 * (i + 1), :],
                                in_=zeros[:, :]).then_inc(sZ, 16)

    _phases.append(_ph0)

    def _ph1():
        with nc.Block() as blk:
            @blk.sync
            def _(e):
                e.wait_ge(s0, 96)
                for dc in range(8):
                    if dc > 0:
                        e.wait_ge(s1t, dc)
                    e.dma_start(out=mwsb[:, :],
                                in_=mw_d[:, 128 * dc:128 * (dc + 1)]).then_inc(s1, 16)
                e.wait_ge(s1t, 9)
                e.dma_start(out=bsb[:, :], in_=b2_d[:, :]).then_inc(s1, 16)
                for half in range(2):
                    e.dma_start(
                        out=sqscr[:, 128 * half:128 * (half + 1)],
                        in_=wb1_d.rearrange("e (ft p) -> (e ft) p", p=128)
                        [128 * half:128 * (half + 1), :]).then_inc(sB1, 16)

            @blk.tensor
            def _(e):
                for dc in range(8):
                    e.wait_ge(s1, 16 * (dc + 1))
                    e.transpose(pb[0][:, 41 * dc:41 * (dc + 1)], mwsb[:, :],
                                identf[0:41, 0:41]).then_inc(s1t, 1)
                for dc in range(8):
                    e.transpose(pb[1][:, 8 * dc:8 * (dc + 1)],
                                wgsb[:, 128 * dc:128 * (dc + 1)], identf[0:8, 0:8])
                for dp in range(8):
                    inst = e.transpose(pb[2][:, dp:dp + 1],
                                       bsb[0:1, 128 * dp:128 * (dp + 1)],
                                       identf[0:1, 0:1])
                inst.then_inc(s1t, 1)
                e.wait_ge(s1, 16 * 9)
                for dp in range(8):
                    inst = e.transpose(pb[2][:, 8 + dp:9 + dp],
                                       bsb[0:1, 128 * dp:128 * (dp + 1)],
                                       identf[0:1, 0:1])
                inst.then_inc(s1t, 1)
                e.wait_ge(sB1, 32)
                for half in range(2):
                    e.transpose(pb[3][:, 128 * half:128 * (half + 1)],
                                sqscr[:, 128 * half:128 * (half + 1)],
                                identf[:, :]).then_inc(s1t, 1)

            @blk.vector
            def _(e):
                e.wait_ge(s1t, 12)
                e.tensor_copy(b1all[:, :], pb[3][:, 0:256])
                e.tensor_copy(mwT[:, :, :],
                              pb[0][:, 0:328].rearrange("p (dc k) -> p dc k", dc=8))
                e.tensor_copy(wgT[:, :, :],
                              pb[1][:, 0:64].rearrange("p (dc g) -> p dc g", dc=8))
                e.tensor_copy(b1T[:, :], pb[2][:, 0:8])
                e.tensor_copy(b2T[:, :], pb[2][:, 8:16])
                e.tensor_tensor(taps[:, :, :], mwT[:, :, 1:41],
                                mwT[:, :, 0:1].broadcast_to([128, 8, 40]),
                                ALU.subtract)

    _phases.append(_ph1)

    def _ph2b(blk):
        chunks = [(x_row_d, i, xn_tok_d) for i in range(32)] + \
                 [(x_ext_d, 0, xn_ext_d)]
        NCH = len(chunks)

        @blk.gpsimd
        def _(e):
            e.wait_ge(s0, 96)
            for j in range(NCH):
                e.wait_ge(sLv, 2 * j + 2)
                src2, cj, dj = chunks[j]
                e.dma_start(out=dj[128 * cj:128 * (cj + 1), :],
                            in_=xnst[j % 2][:, :]).then_inc(sLs[j % 2], 16)
                if j < 32:
                    e.indirect_dma_start(
                        xn_ext_d[0:128, :],
                        IndirectOffsetOnAxis(ext_ids[:, j:j + 1], 0),
                        xnst[j % 2][0:128, :], None,
                    ).then_inc(sES, 16)

        @blk.sync
        def _(e):
            for tc in range(16):
                e.wait_ge(sLs[0], 16 * (tc + 1))
                e.wait_ge(sLs[1], 16 * (tc + 1))
                if tc >= 2:
                    e.wait_ge(sV2, 2 * tc - 2)
                for dc in range(8):
                    e.dma_start(
                        out=lgchB[tc % 2][:, dc, :],
                        in_=xn_tok_d[256 * tc:256 * (tc + 1),
                                     128 * dc:128 * (dc + 1)],
                        transpose=True).then_inc(sT2, 16)

        @blk.scalar
        def _(e):
            for i in range(min(2, NCH)):
                e.dma_start(out=xch[i % 2][:, :],
                            in_=chunks[i][0][128 * chunks[i][1]:
                                             128 * chunks[i][1] + 128, :]
                            ).then_inc(sL[i % 2], 16)
            for i in range(NCH):
                e.wait_ge(sL[i % 2], 16 * (i // 2 + 1))
                if i >= 1:
                    e.wait_ge(sLv, 2 * i - 1)
                e.activation(sqscr[:, :], xch[i % 2][:, :],
                             AF.Square).then_inc(sLa, 1)
                e.wait_ge(sLv, 2 * i + 1)
                e.activation(stat[:, 5:6], stat[:, 4:5],
                             AF.Sqrt).then_inc(sLa, 1)
                if i + 2 < NCH:
                    e.wait_ge(sLv, 2 * i + 2)
                    src, ci, dst = chunks[i + 2]
                    e.dma_start(out=xch[i % 2][:, :],
                                in_=src[128 * ci:128 * (ci + 1), :]
                                ).then_inc(sL[i % 2], 16)

        def _evac_tc(e, tc):
            e.wait_ge(sP2, 2 * tc + 1)
            e.tensor_copy(lgc[:, :], pb[4][0:8, 0:256]).then_inc(sV2, 1)
            vs(e, e.tensor_reduce(cscr[:, :], lgchB[tc % 2][:, :, :], AX.X,
                                  ALU.add))
            e.tensor_tensor(csum[:, :],
                            csum[:, :] if tc > 0 else zeros[:, 0:8],
                            cscr[:, :], ALU.add).then_inc(sV2, 1)

        @blk.vector
        def _(e):
            for i in range(NCH):
                e.wait_ge(sL[i % 2], 16 * (i // 2 + 1))
                e.tensor_reduce(stat[:, 0:1], xch[i % 2][:, :], AX.X, ALU.add)
                e.wait_ge(sLa, 2 * i + 1)
                vs(e, e.tensor_reduce(stat[:, 1:2], sqscr[:, :], AX.X, ALU.add))
                vs(e, e.tensor_scalar_mul(stat[:, 2:3], stat[:, 0:1], 1.0 / D))
                vs(e, e.tensor_tensor(stat[:, 3:4], stat[:, 2:3], stat[:, 2:3],
                                      ALU.mult))
                vs(e, e.scalar_tensor_tensor(stat[:, 4:5], stat[:, 1:2], 1.0 / D,
                                             stat[:, 3:4], ALU.mult,
                                             ALU.subtract))
                vs(e, e.tensor_scalar_add(stat[:, 4:5], stat[:, 4:5], EPS))
                e.reciprocal(stat[:, 4:5], stat[:, 4:5]).then_inc(sLv, 1)
                e.wait_ge(sLa, 2 * i + 2)
                if i >= 2:
                    e.wait_ge(sLs[i % 2], 16 * ((i - 2) // 2 + 1))
                    e.wait_ge(sES, 16 * (i - 1))
                e.tensor_scalar(xnst[i % 2][:, :], xch[i % 2][:, :],
                                stat[:, 2:3], stat[:, 5:6],
                                ALU.subtract, ALU.mult).then_inc(sLv, 1)
                if i >= 3 and i % 2 == 1:
                    _evac_tc(e, (i - 3) // 2)
            _evac_tc(e, 15)
            e.wait_ge(sP2, 32)
            e.tensor_copy(lg_t[:, 0:128], pb[5][:, 0:128])
            vs(e, e.tensor_copy(lg_t[:, 128:256], pb[6][:, 0:128]))
            lg3 = lg_t[:, :].rearrange("p (k e) -> p k e", e=8)
            vs(e, e.reduce_max(smx[:, :], lg3, AX.X))
            e.tensor_tensor(probs[:, :].rearrange("p (k e) -> p k e", e=8), lg3,
                            smx[:, :].unsqueeze(-1).broadcast_to([128, 32, 8]),
                            ALU.subtract).then_inc(sV2, 2)
            e.wait_ge(sE2, 1)
            pr3 = probs[:, :].rearrange("p (k e) -> p k e", e=8)
            vs(e, e.tensor_reduce(smx[:, :], pr3, AX.X, ALU.add))
            vs(e, e.reciprocal(smx[:, :], smx[:, :]))
            e.tensor_tensor(pr3, pr3,
                            smx[:, :].unsqueeze(-1).broadcast_to([128, 32, 8]),
                            ALU.mult)
            e.tensor_scalar_mul(pse[:, :], probs[:, :], 2.0)
            e.tensor_copy(totb[:, :], csum[:, :]).then_inc(sV2, 1)

    _phases.append(lambda: None)

    def _ph3(blk):
        if True:
            @blk.scalar
            def _(e):
                e.wait_ge(sV2, 34)
                e.activation(probs[:, :], probs[:, :], AF.Exp).then_inc(sE2, 1)
                e.wait_ge(sLs[0], 16 * 17)
                e.wait_ge(sES, 16 * 32)
                for dc in range(8):
                    e.dma_start(out=xnE[:, dc, :],
                                in_=xn_ext_d[0:EXT, 128 * dc:128 * (dc + 1)],
                                transpose=True).then_inc(sTE, 16)

            @blk.tensor
            def _(e):
                for tc in range(16):
                    e.wait_ge(sT2, 128 * (tc + 1))
                    if tc > 0:
                        e.wait_ge(sV2, 2 * tc)
                    for dc in range(8):
                        inst = e.matmul(pb[4][0:8, 0:256], wgT[:, dc, :],
                                        lgchB[tc % 2][:, dc, :],
                                        start=(dc == 0), stop=(dc == 7))
                    inst.then_inc(sP2, 1)
                    e.wait_ge(sV2, 2 * tc + 1)
                    for half in range(2):
                        k = 2 * tc + half
                        bank = 5 if k < 16 else 6
                        inst = e.transpose(
                            pb[bank][:, 8 * (k % 16):8 * (k % 16) + 8],
                            lgc[0:8, 128 * half:128 * (half + 1)],
                            identf[0:8, 0:8])
                    inst.then_inc(sP2, 1)

    def _phM3():
        with nc.Block() as blk:
            _ph2b(blk)
            _ph3(blk)
            _ph4(blk)
            _ph5(blk)
            _ph6(blk)

    _phases.append(_phM3)

    def _ph4(blk):
        if True:
            @blk.tensor
            def _(e):
                e.wait_ge(sV2, 35)
                e.wait_ge(s0g, 48)
                for dp in range(8):
                    for dc in range(8):
                        e.matmul(pb[7][:, 2 * dp:2 * dp + 1],
                                 w1sb[:, dc, 128 * dp:128 * (dp + 1)],
                                 totb[:, dc:dc + 1], start=(dc == 0), stop=False)
                    inst = e.matmul(pb[7][:, 2 * dp:2 * dp + 1],
                                    b1rb[0:1, 128 * dp:128 * (dp + 1)],
                                    c4096[:, :], start=False, stop=True)
                inst.then_inc(sP2b, 1)

            @blk.vector
            def _(e):
                e.wait_ge(sP2b, 1)
                vs(e, e.tensor_copy(totp[:, :],
                                    pb[7][:, 0:16].rearrange("p (a b) -> p a b", b=2)[:, :, 0]))
                e.tensor_tensor(mw0t[:, :], mwT[:, :, 0], totp[:, :], ALU.mult)

    _phases.append(lambda: None)

    def _ph5(blk):
        if True:
            @blk.vector
            def _(e):
                vs(e, e.tensor_scalar(lo2[:, :], pse[:, 0:8], 0.0, None, ALU.mult))
                vs(e, e.tensor_scalar(hi2[:, :], pse[:, 0:8], 0.0, None, ALU.mult))
                vs(e, e.tensor_scalar_add(hi2[:, :], hi2[:, :], 2.0))
                for it in range(NBIS):
                    vs(e, e.tensor_tensor(mid2[:, :], lo2[:, :], hi2[:, :], ALU.add))
                    vs(e, e.tensor_scalar_mul(mid2[:, :], mid2[:, :], 0.5))
                    e.tensor_tensor(cmp_sc[:, :].rearrange("p (k e) -> p k e", e=8),
                                    pse[:, :].rearrange("p (k e) -> p k e", e=8),
                                    mid2[:, :].unsqueeze(1).broadcast_to([128, 32, 8]),
                                    ALU.is_ge)
                    e.tensor_reduce(part8[:, :],
                                    cmp_sc[:, :].rearrange("p (k e) -> p e k", e=8),
                                    AX.X, ALU.add).then_inc(sBv, 1)
                    e.wait_ge(sBp, 2 * it + 1)
                    e.tensor_copy(cnt8[:, :], pb[4][0:1, 0:8]).then_inc(sBv, 1)
                    e.wait_ge(sBp, 2 * it + 2)
                    vs(e, e.tensor_copy(cge[:, :], pb[4][:, 8:16]))
                    vs(e, e.tensor_scalar(cgei_f[:, :], cge[:, :], 512.0, None,
                                          ALU.is_ge))
                    vs(e, e.tensor_tensor(bscr[:, :], mid2[:, :], lo2[:, :],
                                          ALU.subtract))
                    vs(e, e.tensor_tensor(bscr[:, :], bscr[:, :], cgei_f[:, :],
                                          ALU.mult))
                    vs(e, e.tensor_tensor(lo2[:, :], lo2[:, :], bscr[:, :], ALU.add))
                    vs(e, e.tensor_tensor(bscr[:, :], hi2[:, :], mid2[:, :],
                                          ALU.subtract))
                    vs(e, e.tensor_tensor(bscr[:, :], bscr[:, :], cgei_f[:, :],
                                          ALU.mult))
                    e.tensor_tensor(hi2[:, :], mid2[:, :], bscr[:, :], ALU.add
                                    ).then_inc(sBv, 1)

            @blk.tensor
            def _(e):
                for it in range(NBIS):
                    e.wait_ge(sBv, 3 * it + 1)
                    e.matmul(pb[4][0:1, 0:8], onesf[:, :], part8[:, :],
                             start=True, stop=True).then_inc(sBp, 1)
                    e.wait_ge(sBv, 3 * it + 2)
                    e.matmul(pb[4][:, 8:16], onesr[:, :], cnt8[:, :],
                             start=True, stop=True).then_inc(sBp, 1)

    _phases.append(lambda: None)

    def _ph6(blk):
        if True:
            @blk.vector
            def _(e):
                pr3 = pse[:, :].rearrange("p (k e) -> p k e", e=8)
                mf3 = maskf[:, :].rearrange("p (k e) -> p k e", e=8)
                e.tensor_tensor(mf3, pr3,
                                lo2[:, :].unsqueeze(1).broadcast_to([128, 32, 8]),
                                ALU.is_ge)
                e.tensor_tensor(mf3, mf3,
                                half_sel[:, :].unsqueeze(-1).broadcast_to([128, 32, 8]),
                                ALU.mult)
                il = None
                for ee in range(E):
                    il = e.tensor_copy(maske[:, ee, :],
                                       maskf[:, :].rearrange("p (k e) -> p e k", e=8)[:, ee, :])
                vs(e, il)
                for ee in range(E):
                    il = e.tensor_tensor_scan(incl[:, ee, :], maske[:, ee, :],
                                              zeros[:, 0:32], 0.0, ALU.add, ALU.add)
                vs(e, il)
                vs(e, e.tensor_copy(cnts[:, :], incl[:, :, 31]))
                vs(e, e.tensor_copy(cnts_b[:, :], cnts[:, :]))
                e.tensor_tensor(lrank[:, :, :], incl[:, :, :], maske[:, :, :],
                                ALU.subtract)
                e.scalar_tensor_tensor(lrank[:, :, :], lrank[:, :, :], 1.0,
                                       maske[:, :, :], ALU.add, ALU.mult
                                       ).then_inc(sRv, 1)
                e.wait_ge(sRp, 1)
                vs(e, e.tensor_copy(starts[:, :], pb[4][:, 16:24]))
                e.tensor_copy(cscr[:, 0:8], starts[:, :]).then_inc(sRv, 1)
                for ee in range(E):
                    i3 = indsc[:, :].rearrange("p (j k) -> p j k", j=JSEG)
                    o3 = ind2[:, :].rearrange("p (j k) -> p j k", j=JSEG)
                    e.tensor_tensor(
                        i3,
                        lrank[:, ee, :].unsqueeze(1).broadcast_to([128, JSEG, 32]),
                        iotaJ[:, :].unsqueeze(-1).broadcast_to([128, JSEG, 32]),
                        ALU.is_equal)
                    e.tensor_tensor(
                        o3, i3,
                        tokid1[:, :].unsqueeze(1).broadcast_to([128, JSEG, 32]),
                        ALU.mult)
                    e.tensor_reduce(lst2[:, ee, 0:JSEG], o3, AX.X, ALU.add)
                    e.tensor_tensor(
                        o3, i3,
                        probs[:, :].rearrange("p (k e) -> p e k", e=8)[:, ee, :]
                            .unsqueeze(1).broadcast_to([128, JSEG, 32]),
                        ALU.mult)
                    e.tensor_reduce(lst2[:, ee, JSEG:2 * JSEG], o3, AX.X,
                                    ALU.add).then_inc(sRv, 1)
                ejb = [ej, ej2]
                rv, rp = 2 + E, 1
                for ee in range(E):
                    vs(e, e.tensor_single_scalar(sdiff[:, :], iotaS[:, :],
                                                 starts[:, ee:ee + 1],
                                                 ALU.subtract))
                    for cchunk in range(3):
                        for j in range(JSEG):
                            if j >= 2:
                                e.wait_ge(sRp, rp - 1)
                            elif 12 * cchunk + j >= 2 or ee > 0:
                                e.wait_ge(sRp, rp - 2)
                            e.tensor_scalar(ejb[j % 2][:, :], sdiff[:, :],
                                            float(j - 128 * cchunk), None,
                                            ALU.is_equal).then_inc(sRv, 1)
                            rv += 1
                            rp += 1
                        e.wait_ge(sRp, rp)
                        rp += 1
                        e.tensor_copy(scr2x[:, :], pb[5][0:2, 0:128]
                                      ).then_inc(sRv, 1)
                        rv += 1
                        e.wait_ge(sRp, rp)
                        vs(e, e.tensor_scalar(idsf[:, ee, cchunk:cchunk + 1],
                                              pb[5][:, 128:129],
                                              1.0, None, ALU.subtract))
                        inst = e.tensor_copy(gval[:, ee, cchunk:cchunk + 1],
                                             pb[5][:, 129:130])
                        inst.then_inc(sRv, 1)
                        rv += 1
                vs(e, e.tensor_copy(cscr[:, 0:8], starts[:, :]))
                vs(e, e.tensor_scalar_max(iscr[:, :, :], idsf[:, :, :], 0.0))
                vs(e, e.tensor_copy(idx_g[:, :, :], iscr[:, :, :]))
                vs(e, e.tensor_scalar(iscr[:, :, :], idsf[:, :, :], 0.0, None,
                                      ALU.is_lt))
                vs(e, e.scalar_tensor_tensor(iscr[:, :, :], iscr[:, :, :],
                                             float(DUMP + 1), idsf[:, :, :],
                                             ALU.mult, ALU.add))
                e.tensor_copy(idx_s[:, :, :], iscr[:, :, :])

            @blk.tensor
            def _(e):
                e.wait_ge(sRv, 1)
                e.matmul(pb[4][:, 16:24], tri[:, :], cnts_b[:, :],
                         start=True, stop=True).then_inc(sRp, 1)
                e.wait_ge(sRv, 2 + E)
                ejb = [ej, ej2]
                rv, rp = 2 + E, 1
                for ee in range(E):
                    for cchunk in range(3):
                        for j in range(JSEG):
                            rv += 1
                            e.wait_ge(sRv, rv)
                            e.matmul(pb[5][0:2, 0:128],
                                     lst2[:, ee, :].rearrange("p (a j) -> p j a", a=2)[:, j, :],
                                     ejb[j % 2][:, :],
                                     start=(j == 0), stop=(j == JSEG - 1)
                                     ).then_inc(sRp, 1)
                            rp += 1
                        rv += 1
                        e.wait_ge(sRv, rv)
                        e.transpose(pb[5][:, 128:130], scr2x[:, :],
                                    identf[0:2, 0:2]).then_inc(sRp, 1)
                        rp += 1
                        rv += 1

    _phases.append(lambda: None)

    def _ph7(blk):
        cols5 = [(0, 512), (512, 1024), (1024, 1536), (1536, 2048), (2048, EXT)]
        if True:
            @blk.tensor
            def _(e):
                e.wait_ge(sTE, 128)
                e.wait_ge(sV2, 34)
                idx = 0
                for dp in range(8):
                    for (c0, c1) in cols5:
                        if idx >= 2:
                            e.wait_ge(sA5, idx - 1)
                        bank = pb[4 + (idx % 2)]
                        for dc in range(8):
                            inst = e.matmul(bank[:, 0:c1 - c0],
                                            w1sb[:, dc, 128 * dp:128 * (dp + 1)],
                                            xnE[:, dc, c0:c1], start=(dc == 0),
                                            stop=(dc == 7))
                        inst.then_inc(sP5, 1)
                        idx += 1

            @blk.scalar
            def _(e):
                idx = 0
                for dp in range(8):
                    for (c0, c1) in cols5:
                        e.wait_ge(sP5, idx + 1)
                        e.activation(p_sb[:, dp, c0:c1],
                                     pb[4 + (idx % 2)][:, 0:c1 - c0], AF.Identity,
                                     bias=b1T[:, dp:dp + 1]).then_inc(sA5, 1)
                        idx += 1

    _phases.append(lambda: None)

    def _ph8():
        pass

    _phases.append(_ph8)

    def _ph9():
        TT3 = [(0, 128), (128, 256), (256, 320)]
        with nc.Block() as blk:
            _ph7(blk)
            def _gather(e, ee, ti):
                if CFG["p7_gather"]:
                    e.indirect_dma_start(
                        xeb[ee % 2][:, ti, :], None,
                        xn_tok_d[0:128, :],
                        IndirectOffsetOnAxis(idx_g[:, ee, ti:ti + 1], 0),
                    ).then_inc(sX, 16)
                else:
                    e.dma_start(out=xeb[ee % 2][:, ti, :],
                                in_=xn_tok_d[0:128, :]).then_inc(sX, 16)

            def _scatter(e, ee, ti, r0, r1):
                if CFG["p7_scatter"]:
                    e.indirect_dma_start(
                        acc_d[0:r1 - r0, :],
                        IndirectOffsetOnAxis(idx_s[0:r1 - r0, ee, ti:ti + 1], 0),
                        oute[0:r1 - r0, ti, :], None,
                        compute_op=ALU.add,
                    ).then_inc(sSC, 16)
                else:
                    e.dma_start(out=acc_d[T:T + r1 - r0, :],
                                in_=oute[0:r1 - r0, ti, :]).then_inc(sSC, 16)

            @blk.gpsimd
            def _(e):
                for ti in range(3):
                    _gather(e, 0, ti)
                for ee in range(CFG['p7_E']):
                    if ee + 1 < CFG['p7_E']:
                        if ee >= 1:
                            e.wait_ge(sXT, 16 * 24 * ee)
                        if CFG["p7_scblock"] and ee >= 2:
                            e.wait_ge(sSC, 48 * (ee - 1))
                        for ti in range(3):
                            _gather(e, ee + 1, ti)
                    if ee >= 2:
                        e.wait_ge(sPo, 6 * (ee - 1))
                    e.dma_start(out=b2er[ee % 2][:, :], in_=wb2_d[ee:ee + 1, :]
                                ).then_inc(sWb[ee % 2], 16)
                    for g in range(16):
                        G = 16 * ee + g
                        if G >= 4:
                            e.wait_ge(sPw, 2 * (G - 3))
                        e.dma_start(
                            out=w1e[G % 4][:, :, :],
                            in_=we1_d[ee, :, 256 * g:256 * (g + 1)]
                                .rearrange("(dc p) f -> p dc f", p=128),
                        ).then_inc(sWe[G % 4], 16)
                        e.dma_start(
                            out=w2e[G % 4][:, :, :],
                            in_=we2_d[ee, 256 * g:256 * (g + 1), :]
                                .rearrange("(ft p) dd -> p ft dd", p=128),
                        ).then_inc(sWe[G % 4], 16)
                    if ee > 0 and CFG["p7_scblock"]:
                        e.wait_ge(sO, 6 * ee)
                        e.wait_ge(sSC, 48 * (ee - 1))
                        e.wait_ge(sXT, 16 * 24 * min(ee + 2, CFG['p7_E']))
                        if ee == 1:
                            e.wait_ge(sZ, 528)
                        for ti, (r0, r1) in enumerate(TT3):
                            _scatter(e, ee - 1, ti, r0, r1)
                e.wait_ge(sO, 6 * CFG['p7_E'])
                if not CFG["p7_scblock"]:
                    return
                e.wait_ge(sSC, 48 * (CFG['p7_E'] - 1))
                for ti, (r0, r1) in enumerate(TT3):
                    if CFG["p7_scatter"]:
                        e.indirect_dma_start(
                            acc_d[0:r1 - r0, :],
                            IndirectOffsetOnAxis(idx_s[0:r1 - r0, CFG['p7_E'] - 1, ti:ti + 1], 0),
                            oute[0:r1 - r0, ti, :], None,
                            compute_op=ALU.add,
                        ).then_inc(sSC, 16)
                    else:
                        e.dma_start(out=acc_d[T:T + r1 - r0, :],
                                    in_=oute[0:r1 - r0, ti, :]).then_inc(sSC, 16)
                e.wait_ge(sSC, 48 * CFG['p7_E'])

            @blk.sync
            def _(e):
                for ee in range(CFG['p7_E']):
                    if ee >= 2:
                        e.wait_ge(sPh, 32 * (ee - 1))
                    for ti, (r0, r1) in enumerate(TT3):
                        e.wait_ge(sX, 48 * ee + 16 * (ti + 1))
                        for dc in range(8):
                            e.dma_start(out=xeT[ee % 2][:, dc, r0:r1],
                                        in_=xeb[ee % 2][0:r1 - r0, ti,
                                                        128 * dc:128 * (dc + 1)],
                                        transpose=True).then_inc(sXT, 16)

            @blk.scalar
            def _(e):
                for ee in range(CFG['p7_E']):
                    for ft in range(32):
                        FT = 32 * ee + ft
                        e.wait_ge(sPh, FT + 1)
                        if FT >= 2:
                            e.wait_ge(sPw, FT - 1)
                        e.activation(h1[ft % 2][:, :], pb[6 + (ft % 2)][:, 0:CAP],
                                     AF.Gelu_apprx_tanh,
                                     bias=b1all[:, 32 * ee + ft:32 * ee + ft + 1]
                                     ).then_inc(sH, 1)

            def self_w2(e, ee, ft):
                FT = 32 * ee + ft
                g = ft // 2
                G = 16 * ee + g
                e.wait_ge(sH, FT + 1)
                if ft == 0 and ee > 0:
                    e.wait_ge(sO, 6 * ee)
                for ti, (r0, r1) in enumerate(TT3):
                    for dh in range(2):
                        inst = e.matmul(
                            pb[2 * ti + dh][0:r1 - r0, 0:512],
                            h1[ft % 2][:, r0:r1],
                            w2e[G % 4][:, ft % 2, 512 * dh:512 * (dh + 1)],
                            start=(ft == 0), stop=False)
                inst.then_inc(sPw, 1)

            @blk.tensor
            def _(e):
                for ee in range(CFG['p7_E']):
                    e.wait_ge(sXT, 16 * 24 * (ee + 1))
                    e.wait_ge(sWb[ee % 2], 16 * (ee // 2 + 1))
                    for ft in range(32):
                        FT = 32 * ee + ft
                        g = ft // 2
                        G = 16 * ee + g
                        if ft % 2 == 0:
                            e.wait_ge(sWe[G % 4], 32 * (G // 4 + 1))
                        if FT >= 2:
                            e.wait_ge(sH, FT - 1)
                        hb = pb[6 + (ft % 2)]
                        for dc in range(8):
                            inst = e.matmul(
                                hb[:, 0:CAP],
                                w1e[G % 4][:, dc, 128 * (ft % 2):128 * (ft % 2 + 1)],
                                xeT[ee % 2][:, dc, :],
                                start=(dc == 0), stop=(dc == 7))
                        inst.then_inc(sPh, 1)
                        if ft >= 1:
                            self_w2(e, ee, ft - 1)
                    self_w2(e, ee, 31)
                    for ti, (r0, r1) in enumerate(TT3):
                        for dh in range(2):
                            inst = e.matmul(
                                pb[2 * ti + dh][0:r1 - r0, 0:512],
                                onesc[0:1, 0:r1 - r0],
                                b2er[ee % 2][0:1, 512 * dh:512 * (dh + 1)],
                                start=False, stop=True)
                    inst.then_inc(sPo, 6)

            @blk.vector
            def _(e):
                for ee in range(CFG['p7_E']):
                    dp = ee
                    e.wait_ge(sA5, 5 * (dp + 1))
                    e.tensor_scalar_mul(p_sb[:, dp, 0:128], p_sb[:, dp, 0:128],
                                        mask_halo[:, 0:1])
                    for q in range(4):
                        base = 128 + 512 * q
                        for c in range(5):
                            k0 = 8 * c
                            e.tensor_scalar_mul(chain[c][:, :],
                                                p_sb[:, dp, base - k0:base - k0 + 512],
                                                taps[:, dp, k0:k0 + 1])
                            for k in range(k0 + 1, k0 + 8):
                                e.scalar_tensor_tensor(
                                    chain[c][:, :],
                                    p_sb[:, dp, base - k:base - k + 512],
                                    taps[:, dp, k:k + 1], chain[c][:, :],
                                    ALU.mult, ALU.add)
                        e.tensor_tensor(tc0[:, :], chain[0][:, :], chain[1][:, :],
                                        ALU.add)
                        e.tensor_tensor(tc1[:, :], chain[2][:, :], chain[3][:, :],
                                        ALU.add)
                        e.tensor_tensor(tc0[:, :], tc0[:, :], tc1[:, :], ALU.add)
                        e.scalar_tensor_tensor(band[:, dp, 512 * q:512 * (q + 1)],
                                               chain[4][:, :], mw0t[:, dp:dp + 1],
                                               tc0[:, :], ALU.add, ALU.add)
                    if ee > 0 and CFG["p7_scblock"]:
                        e.wait_ge(sSC, 48 * ee)
                    for ti, (r0, r1) in enumerate(TT3):
                        for dh in range(2):
                            e.wait_ge(sPo, 6 * (ee + 1))
                            e.tensor_scalar_mul(
                                oute[0:r1 - r0, ti, 512 * dh:512 * (dh + 1)],
                                pb[2 * ti + dh][0:r1 - r0, 0:512],
                                gval[0:r1 - r0, ee, ti:ti + 1]).then_inc(sO, 1)

    _phases.append(_ph9)

    def _ph10():
        with nc.Block() as blk:
            @blk.gpsimd
            def _(e):
                e.dma_start(
                    out=w1sb[:, :, :],
                    in_=w2_d[:, :].rearrange("(dp p) dd -> p dp dd", p=128),
                ).then_inc(s8w, 16)
                for tt in range(16):
                    if tt >= 2:
                        e.wait_ge(s8a, tt - 1)
                    e.indirect_dma_start(
                        accsb[tt % 2][:, :], None,
                        acc_d[0:128, :],
                        IndirectOffsetOnAxis(own_ids[:, tt:tt + 1], 0),
                    ).then_inc(s8g[tt % 2], 16)

            @blk.sync
            def _(e):
                for tt in range(16):
                    if tt >= 2:
                        e.wait_ge(s8v, 2 * (tt - 1))
                    e.dma_start(out=xch[tt % 2][:, :],
                                in_=x_ext_d[128 * (tt + 1):128 * (tt + 2), :]
                                ).then_inc(s8x[tt % 2], 16)
                    if tt >= 2:
                        j = tt - 2
                        e.wait_ge(s8v, 2 * j + 2)
                        e.dma_start(out=out_d[128 * j:128 * (j + 1), :],
                                    in_=outch[j % 2][:, :]).then_inc(s8o, 16)
                for j in (14, 15):
                    e.wait_ge(s8v, 2 * j + 2)
                    e.dma_start(out=out_d[128 * j:128 * (j + 1), :],
                                in_=outch[j % 2][:, :]).then_inc(s8o, 16)
                e.wait_ge(s8o, 16 * 16)

            @blk.tensor
            def _(e):
                e.wait_ge(s8w, 16)
                for tt in range(16):
                    if tt >= 2:
                        e.wait_ge(s8v, 2 * (tt - 1))
                    for dp in range(8):
                        for dh in range(2):
                            e.matmul(pb[2 * (tt % 2) + dh][:, 0:512],
                                     p_sb[:, dp, 128 * tt:128 * (tt + 1)],
                                     w1sb[:, dp, 512 * dh:512 * (dh + 1)],
                                     start=(dp == 0), stop=False)
                    for dh in range(2):
                        e.matmul(pb[2 * (tt % 2) + dh][:, 0:512],
                                 onesc[0:1, 0:128],
                                 b2rb[0:1, 512 * dh:512 * (dh + 1)],
                                 start=False, stop=True).then_inc(s8p, 1)

            @blk.scalar
            def _(e):
                for tt in range(16):
                    e.wait_ge(s8g[tt % 2], 16 * (tt // 2 + 1))
                    if tt > 0:
                        e.wait_ge(s8v, 2 * tt)
                    e.activation(gsb[:, :], accsb[tt % 2][:, :],
                                 AF.Silu).then_inc(s8a, 1)

            @blk.vector
            def _(e):
                for tt in range(16):
                    e.wait_ge(s8a, tt + 1)
                    e.wait_ge(s8p, 2 * tt + 2)
                    e.wait_ge(s8x[tt % 2], 16 * (tt // 2 + 1))
                    if tt >= 2:
                        e.wait_ge(s8o, 16 * (tt - 1))
                    for dh in range(2):
                        e.tensor_tensor(outch[tt % 2][:, 512 * dh:512 * (dh + 1)],
                                        gsb[:, 512 * dh:512 * (dh + 1)],
                                        pb[2 * (tt % 2) + dh][:, 0:512], ALU.mult)
                    e.tensor_tensor(outch[tt % 2][:, :], outch[tt % 2][:, :],
                                    xch[tt % 2][:, :], ALU.add).then_inc(s8v, 2)

    _phases.append(_ph10)

    for _k in range(min(upto + 1, len(_phases))):
        _phases[_k]()

    if dbg:
        dbg_items = [
            ("d_probs", probs), ("d_pse", pse), ("d_lo2", lo2),
            ("d_cnts", cnts), ("d_starts", starts), ("d_csum", csum),
            ("d_totp", totp), ("d_mw0t", mw0t), ("d_b1T", b1T),
            ("d_b2T", b2T), ("d_lgt", lg_t), ("d_lrank", lrank),
            ("d_idsf", idsf), ("d_gval", gval), ("d_taps", taps),
            ("d_idxg", idx_g), ("d_idxs", idx_s), ("d_xnE", xnE),
            ("d_p", p_sb),
        ]
        sD = sem("sD")
        with nc.Block() as blk:
            @blk.sync
            def _(e):
                n = 0
                for name, t in dbg_items:
                    sh = list(t.shape)
                    fl = int(np.prod(sh[1:]))
                    od = nc.dram_tensor(name, [sh[0], fl], t.dtype,
                                        kind="ExternalOutput")
                    src = t[tuple(slice(None) for _ in sh)]
                    if len(sh) == 3:
                        src = src.rearrange("p a b -> p (a b)")
                    e.dma_start(out=od[:, :], in_=src).then_inc(sD, 16)
                    n += 16
                e.wait_ge(sD, n)

    ctx.close()
    return nc


def kernel(L0_attn_input_data, L0_attn_W1, L0_attn_bias1, L0_attn_W2,
           L0_attn_bias2, L0_attn_memory_weights, L0_ffn_Wg, L0_ffn_W1,
           L0_ffn_b1, L0_ffn_W2, L0_ffn_b2):
    def host_f32(a):
        a = np.asarray(a)
        return np.ascontiguousarray(a.astype(np.float32, copy=False))

    x = host_f32(L0_attn_input_data)
    w1 = host_f32(L0_attn_W1)
    b1 = host_f32(L0_attn_bias1).reshape(1, D)
    b2 = host_f32(L0_attn_bias2).reshape(1, D)
    w2 = host_f32(L0_attn_W2)
    mw = host_f32(L0_attn_memory_weights)
    wg = host_f32(L0_ffn_Wg)
    we1 = host_f32(L0_ffn_W1)
    wb1 = host_f32(L0_ffn_b1)
    we2 = host_f32(L0_ffn_W2)
    wb2 = host_f32(L0_ffn_b2)

    nc = build_nc()
    in_maps = make_in_maps(x, w1, b1, w2, b2, mw, wg, we1, wb1, we2, wb2)

    res = run_bass_kernel_spmd(nc, in_maps, core_ids=list(range(8)))
    out = np.empty((B, T, D), np.float32)
    for c in range(8):
        b, h = c // 2, c % 2
        out[b, HT * h:HT * (h + 1), :] = res.results[c]["out"]
    return out


def make_in_maps(x, w1, b1, w2, b2, mw, wg, we1, wb1, we2, wb2):
    pcol = np.arange(128)
    in_maps = []
    for c in range(8):
        b, h = c // 2, c % 2
        x_row = x[b]
        if h == 0:
            halo = np.zeros((128, D), np.float32)
        else:
            halo = x_row[HT - 128:HT]
        x_ext = np.concatenate([halo, x_row[HT * h:HT * (h + 1)]], axis=0)
        mask_halo = np.full((128, 1), float(h), np.float32)
        ks = np.arange(32)
        half_sel = np.broadcast_to(((ks // 16) == h).astype(np.float32),
                                   (128, 32)).copy()
        own_ids = (HT * h + 128 * np.arange(16)[None, :] +
                   pcol[:, None]).astype(np.int32)
        tok = 128 * np.arange(32)[None, :] + pcol[:, None]
        own = (tok // HT) == h
        ext_ids = np.where(own, 128 + tok - HT * h, EXT).astype(np.int32)
        in_maps.append({
            "x_row": np.ascontiguousarray(x_row),
            "x_ext": np.ascontiguousarray(x_ext),
            "mask_halo": mask_halo,
            "half_sel": half_sel,
            "own_ids": own_ids,
            "ext_ids": ext_ids,
            "w1": w1, "b1": b1, "w2": w2, "b2": b2, "mw": mw, "wg": wg,
            "we1": we1, "wb1": wb1, "we2": we2, "wb2": wb2,
        })
    return in_maps


# version 25
# speedup vs baseline: 1.1304x; 1.0120x over previous
import numpy as np
from contextlib import ExitStack

import concourse.bass as bass
import concourse.mybir as mybir
from concourse.bass import IndirectOffsetOnAxis
from concourse.bass_utils import run_bass_kernel_spmd
import concourse.bass_utils as _bass_utils

if not getattr(_bass_utils, "_ldw_patched", False):
    _orig_run_command = _bass_utils.run_command

    def _run_command_ldw(argv, **kwargs):
        argv = ["--enable-ldw-opt=true" if a == "--enable-ldw-opt=false" else a
                for a in argv]
        return _orig_run_command(argv, **kwargs)

    _bass_utils.run_command = _run_command_ldw
    _bass_utils._ldw_patched = True

F32 = mybir.dt.float32
BF16 = mybir.dt.bfloat16
I32 = mybir.dt.int32
AF = mybir.ActivationFunctionType
ALU = mybir.AluOpType
AX = mybir.AxisListType

B, T, D = 4, 4096, 1024
HT = T // 2
EXT = HT + 128
E, FF = 8, 4096
CAP = 320
JSEG = 12
EPS = 1e-5
NBIS = 36
DUMP = T


CFG = {"p7_gather": True, "p7_transpose": True, "p7_scatter": True,
       "p7_w2": True, "p7_E": 8, "p7_scblock": True}


def build_nc(upto=99, dbg=False):
    nc = bass.Bass("TRN2", target_bir_lowering=False,
                   dynamic_dma_scratch_size=16384)

    x_row_d = nc.dram_tensor("x_row", [T, D], F32, kind="ExternalInput")
    x_ext_d = nc.dram_tensor("x_ext", [EXT, D], F32, kind="ExternalInput")
    mask_halo_d = nc.dram_tensor("mask_halo", [128, 1], F32, kind="ExternalInput")
    half_sel_d = nc.dram_tensor("half_sel", [128, 32], F32, kind="ExternalInput")
    own_ids_d = nc.dram_tensor("own_ids", [128, 16], I32, kind="ExternalInput")
    ext_ids_d = nc.dram_tensor("ext_ids", [128, 32], I32, kind="ExternalInput")
    w1_d = nc.dram_tensor("w1", [D, D], F32, kind="ExternalInput")
    b1_d = nc.dram_tensor("b1", [1, D], F32, kind="ExternalInput")
    w2_d = nc.dram_tensor("w2", [D, D], F32, kind="ExternalInput")
    b2_d = nc.dram_tensor("b2", [1, D], F32, kind="ExternalInput")
    mw_d = nc.dram_tensor("mw", [41, D], F32, kind="ExternalInput")
    wg_d = nc.dram_tensor("wg", [E, D], F32, kind="ExternalInput")
    we1_d = nc.dram_tensor("we1", [E, D, FF], F32, kind="ExternalInput")
    wb1_d = nc.dram_tensor("wb1", [E, FF], F32, kind="ExternalInput")
    we2_d = nc.dram_tensor("we2", [E, FF, D], F32, kind="ExternalInput")
    wb2_d = nc.dram_tensor("wb2", [E, D], F32, kind="ExternalInput")

    out_d = nc.dram_tensor("out", [HT, D], F32, kind="ExternalOutput")

    xn_tok_d = nc.dram_tensor("xn_tok", [T, D], BF16, kind="Internal")
    xn_ext_d = nc.dram_tensor("xn_ext", [EXT + 1, D], BF16, kind="Internal")
    acc_d = nc.dram_tensor("acc", [T + 128, D], F32, kind="Internal")

    A = nc.alloc_sbuf_tensor
    xnE = A("xnE", [128, 8, EXT], BF16)
    lgch = A("lgch", [128, 8, 256], BF16)
    p_sb = A("p_sb", [128, 8, EXT], BF16)
    chain = [A(f"chain{i}", [128, 512], BF16) for i in range(5)]
    tc0 = A("tc0", [128, 512], F32)
    tc1 = A("tc1", [128, 512], F32)
    w1sb = A("w1sb", [128, 8, D], BF16)
    xch = [A(f"xch{i}", [128, D], F32) for i in range(2)]
    xnst = [A(f"xnst{i}", [128, D], BF16) for i in range(2)]
    stat = A("stat", [128, 16], F32)
    sqscr = A("sqscr", [128, D], F32)
    lgc = A("lgc", [8, 256], F32)
    lg_t = A("lg_t", [128, 256], F32)
    smx = A("smx", [128, 32], F32)
    probs = A("probs", [128, 256], F32)
    pse = A("pse", [128, 256], F32)
    csum = A("csum", [128, 8], F32)
    cscr = A("cscr", [128, 8], F32)
    totb = A("totb", [128, 8], BF16)
    totp = A("totp", [128, 8], F32)
    mwT = A("mwT", [128, 8, 41], F32)
    taps = A("taps", [128, 8, 40], F32)
    mw0t = A("mw0t", [128, 8], F32)
    b1T = A("b1T", [128, 8], F32)
    b2T = A("b2T", [128, 8], F32)
    wgT = A("wgT", [128, 8, 8], BF16)
    mwsb = A("mwsb", [41, 128], F32)
    wgsb = A("wgsb", [8, D], F32)
    bsb = A("bsb", [1, D], F32)
    b1rb = A("b1rb", [1, D], BF16)
    b2rb = A("b2rb", [1, D], BF16)
    identf = A("identf", [128, 128], F32)
    tri = A("tri", [128, 128], BF16)
    onesc = A("onesc", [128, 512], BF16)
    onesf = A("onesf", [128, 1], F32)
    onesr = A("onesr", [1, 128], F32)
    zeros = A("zeros", [128, D], F32)
    c4096 = A("c4096", [1, 1], BF16)
    mask_halo = A("mask_halo_sb", [128, 1], F32)
    half_sel = A("half_sel_sb", [128, 32], F32)
    own_ids = A("own_ids_sb", [128, 16], I32)
    ext_ids = A("ext_ids_sb", [128, 32], I32)
    lo2 = A("lo2", [128, 8], F32)
    hi2 = A("hi2", [128, 8], F32)
    mid2 = A("mid2", [128, 8], F32)
    cnt8 = A("cnt8", [1, 8], F32)
    cmp_sc = A("cmp_sc", [128, 256], F32)
    part8 = A("part8", [128, 8], F32)
    cge = A("cge", [128, 8], F32)
    cgei_f = A("cgei_f", [128, 8], F32)
    bscr = A("bscr", [128, 8], F32)
    maskf = A("maskf", [128, 256], F32)
    maske = A("maske", [128, 8, 32], F32)
    incl = A("incl", [128, 8, 32], F32)
    lrank = A("lrank", [128, 8, 32], F32)
    cnts = A("cnts", [128, 8], F32)
    cnts_b = A("cnts_b", [128, 8], BF16)
    starts = A("starts", [128, 8], F32)
    tokid1 = A("tokid1", [128, 32], F32)
    iotaJ = A("iotaJ", [128, JSEG], F32)
    iotaS = A("iotaS", [128, 128], F32)
    indsc = A("indsc", [128, JSEG * 32], F32)
    ind2 = A("ind2", [128, JSEG * 32], F32)
    ej = A("ej", [128, 128], F32)
    sdiff = A("sdiff", [128, 128], F32)
    ej2 = A("ej2", [128, 128], F32)
    scr2x = A("scr2x", [2, 128], F32)
    lst2 = A("lst2", [128, 8, 2 * JSEG], F32)
    idsf = A("idsf", [128, 8, 3], F32)
    gval = A("gval", [128, 8, 3], F32)
    iscr = A("iscr", [128, 8, 3], F32)
    idx_g = A("idx_g", [128, 8, 3], I32)
    idx_s = A("idx_s", [128, 8, 3], I32)
    xe = A("xe", [128, 3, D], BF16)
    xeT = [A(f"xeT{i}", [128, 8, CAP], BF16) for i in range(2)]
    h1 = [A(f"h1{i}", [128, CAP], BF16) for i in range(2)]
    w1e = [A(f"w1e{i}", [128, 8, 256], BF16) for i in range(2)]
    w2e = [A(f"w2e{i}", [128, 2, D], BF16) for i in range(2)]
    b1all = A("b1all", [128, E * FF // 128], F32)
    b2er = [A(f"b2er{i}", [1, D], BF16) for i in range(2)]
    def overlay(name, shape, dtype, host, byte_off=0):
        return nc.alloc_sbuf_tensor_at(
            name, shape, dtype, offset=nc.lookup_mloc(host).addr + byte_off)

    oute = overlay("oute", [128, 3, D], F32, xnE)
    w1e.append(overlay("w1e2", [128, 8, 256], BF16, wgsb))
    w2e.append(overlay("w2e2", [128, 2, D], BF16, bsb))
    w1e.append(overlay("w1e3", [128, 8, 256], BF16, zeros))
    w2e.append(overlay("w2e3", [128, 2, D], BF16, maskf))
    accsb = [overlay("accsb0", [128, D], F32, xnE, 12288),
             overlay("accsb1", [128, D], F32, xnE, 16384)]
    gsb = overlay("gsb", [128, D], F32, xnE, 20480)
    outch = [overlay("outch0", [128, D], F32, xnE, 24576),
             overlay("outch1", [128, D], F32, xnE, 28672)]
    xe2 = overlay("xe2", [128, 3, D], BF16, sqscr)
    assert nc.lookup_mloc(smx).addr >= nc.lookup_mloc(sqscr).addr + 6144
    xeb = [xe, xe2]
    lgchB = [lgch, overlay("lgch2", [128, 8, 256], BF16, xeT[0])]

    pb = [nc.alloc_psum_tensor(f"pb{i}", [128, 512], F32) for i in range(8)]

    ctx = ExitStack()

    def sem(name):
        return ctx.enter_context(nc.semaphore(name))

    s0 = sem("s0")
    s0g = sem("s0g")
    s0f = sem("s0f")
    sB1 = sem("sB1")
    sZ = sem("sZ")
    s1 = sem("s1")
    s1t = sem("s1t")
    sL = [sem("sL0"), sem("sL1")]
    sLa = sem("sLa")
    sLv = sem("sLv")
    sLs = [sem("sLs0"), sem("sLs1")]
    sES = sem("sES")
    sT2b = sem("sT2b")
    sTE = sem("sTE")
    sT2 = sem("sT2")
    sP2 = sem("sP2")
    sV2 = sem("sV2")
    sE2 = sem("sE2")
    sP2b = sem("sP2b")
    sBv = sem("sBv")
    sBp = sem("sBp")
    sRv = sem("sRv")
    sRp = sem("sRp")
    sP5 = sem("sP5")
    sA5 = sem("sA5")
    sWe = [sem("sWe0"), sem("sWe1"), sem("sWe2"), sem("sWe3")]
    sWb = [sem("sWb0"), sem("sWb1")]
    sX = sem("sX")
    sXT = sem("sXT")
    sH = sem("sH")
    sPh = sem("sPh")
    sPw = sem("sPw")
    sPo = sem("sPo")
    sO = sem("sO")
    sSC = sem("sSC")
    s8w = sem("s8w")
    s8p = sem("s8p")
    s8g = [sem("s8g0"), sem("s8g1")]
    s8x = [sem("s8x0"), sem("s8x1")]
    s8a = sem("s8a")
    s8v = sem("s8v")
    s8o = sem("s8o")

    sVsn = sem("sVsn")
    _VS = {"n": 0}

    def vs(e, inst):
        _VS["n"] += 1
        inst.then_inc(sVsn, 1)
        e.wait_ge(sVsn, _VS["n"])

    band = p_sb

    _phases = []

    def _ph0():
        with nc.Block() as blk:
            @blk.sync
            def _(e):
                for (dst, src) in [
                    (mask_halo[:, :], mask_halo_d[:, :]),
                    (half_sel[:, :], half_sel_d[:, :]),
                    (own_ids[:, :], own_ids_d[:, :]),
                    (ext_ids[:, :], ext_ids_d[:, :]),
                    (wgsb[:, :], wg_d[:, :]),
                    (bsb[:, :], b1_d[:, :]),
                ]:
                    e.dma_start(out=dst, in_=src).then_inc(s0, 16)

            @blk.gpsimd
            def _(e):
                e.dma_start(
                    out=w1sb[:, :, :],
                    in_=w1_d[:, :].rearrange("(dc p) dp -> p dc dp", p=128),
                ).then_inc(s0g, 16)
                e.dma_start(out=b1rb[:, :], in_=b1_d[:, :]).then_inc(s0g, 16)
                e.dma_start(out=b2rb[:, :], in_=b2_d[:, :]).then_inc(s0g, 16)
                e.memset(zeros[:, :], 0.0).then_inc(s0f, 1)
                e.memset(onesc[:, :], 1.0).then_inc(s0f, 1)
                e.memset(onesf[:, :], 1.0).then_inc(s0f, 1)
                e.memset(onesr[:, :], 1.0).then_inc(s0f, 1)
                e.memset(c4096[:, :], 4096.0).then_inc(s0f, 1)
                e.iota(tokid1[:, :], [[128, 32]], base=1, channel_multiplier=1,
                       allow_small_or_imprecise_dtypes=True).then_inc(s0f, 1)
                e.iota(iotaJ[:, :], [[1, JSEG]], base=1, channel_multiplier=0,
                       allow_small_or_imprecise_dtypes=True).then_inc(s0f, 1)
                e.iota(iotaS[:, :], [[1, 128]], base=0, channel_multiplier=0,
                       allow_small_or_imprecise_dtypes=True).then_inc(s0f, 1)
                e.memset(identf[:, :], 1.0).then_inc(s0f, 1)
                e.memset(tri[:, :], 1.0).then_inc(s0f, 1)
                e.wait_ge(s0f, 10)
                e.affine_select(identf[:, :], identf[:, :], [[1, 128]],
                                ALU.is_equal, 0.0, base=0,
                                channel_multiplier=-1).then_inc(s0f, 1)
                e.affine_select(tri[:, :], tri[:, :], [[1, 128]],
                                ALU.is_ge, 0.0, base=-1,
                                channel_multiplier=-1).then_inc(s0f, 1)
                e.wait_ge(s0f, 12)
                for i in range(33):
                    e.dma_start(out=acc_d[128 * i:128 * (i + 1), :],
                                in_=zeros[:, :]).then_inc(sZ, 16)

    _phases.append(_ph0)

    def _ph1():
        with nc.Block() as blk:
            @blk.sync
            def _(e):
                e.wait_ge(s0, 96)
                for dc in range(8):
                    if dc > 0:
                        e.wait_ge(s1t, dc)
                    e.dma_start(out=mwsb[:, :],
                                in_=mw_d[:, 128 * dc:128 * (dc + 1)]).then_inc(s1, 16)
                e.wait_ge(s1t, 9)
                e.dma_start(out=bsb[:, :], in_=b2_d[:, :]).then_inc(s1, 16)
                for half in range(2):
                    e.dma_start(
                        out=sqscr[:, 128 * half:128 * (half + 1)],
                        in_=wb1_d.rearrange("e (ft p) -> (e ft) p", p=128)
                        [128 * half:128 * (half + 1), :]).then_inc(sB1, 16)

            @blk.tensor
            def _(e):
                for dc in range(8):
                    e.wait_ge(s1, 16 * (dc + 1))
                    e.transpose(pb[0][:, 41 * dc:41 * (dc + 1)], mwsb[:, :],
                                identf[0:41, 0:41]).then_inc(s1t, 1)
                for dc in range(8):
                    e.transpose(pb[1][:, 8 * dc:8 * (dc + 1)],
                                wgsb[:, 128 * dc:128 * (dc + 1)], identf[0:8, 0:8])
                for dp in range(8):
                    inst = e.transpose(pb[2][:, dp:dp + 1],
                                       bsb[0:1, 128 * dp:128 * (dp + 1)],
                                       identf[0:1, 0:1])
                inst.then_inc(s1t, 1)
                e.wait_ge(s1, 16 * 9)
                for dp in range(8):
                    inst = e.transpose(pb[2][:, 8 + dp:9 + dp],
                                       bsb[0:1, 128 * dp:128 * (dp + 1)],
                                       identf[0:1, 0:1])
                inst.then_inc(s1t, 1)
                e.wait_ge(sB1, 32)
                for half in range(2):
                    e.transpose(pb[3][:, 128 * half:128 * (half + 1)],
                                sqscr[:, 128 * half:128 * (half + 1)],
                                identf[:, :]).then_inc(s1t, 1)

            @blk.vector
            def _(e):
                e.wait_ge(s1t, 12)
                e.tensor_copy(b1all[:, :], pb[3][:, 0:256])
                vs(e, e.tensor_copy(
                    mwT[:, :, :],
                    pb[0][:, 0:328].rearrange("p (dc k) -> p dc k", dc=8)))
                e.tensor_copy(wgT[:, :, :],
                              pb[1][:, 0:64].rearrange("p (dc g) -> p dc g", dc=8))
                e.tensor_copy(b1T[:, :], pb[2][:, 0:8])
                e.tensor_copy(b2T[:, :], pb[2][:, 8:16])
                e.tensor_tensor(taps[:, :, :], mwT[:, :, 1:41],
                                mwT[:, :, 0:1].broadcast_to([128, 8, 40]),
                                ALU.subtract)

    _phases.append(_ph1)

    def _ph2b(blk):
        chunks = [(x_row_d, i, xn_tok_d) for i in range(32)] + \
                 [(x_ext_d, 0, xn_ext_d)]
        NCH = len(chunks)

        @blk.gpsimd
        def _(e):
            e.wait_ge(s0, 96)
            for j in range(NCH):
                e.wait_ge(sLv, 2 * j + 2)
                src2, cj, dj = chunks[j]
                e.dma_start(out=dj[128 * cj:128 * (cj + 1), :],
                            in_=xnst[j % 2][:, :]).then_inc(sLs[j % 2], 16)
                if j < 32:
                    e.indirect_dma_start(
                        xn_ext_d[0:128, :],
                        IndirectOffsetOnAxis(ext_ids[:, j:j + 1], 0),
                        xnst[j % 2][0:128, :], None,
                    ).then_inc(sES, 16)

        @blk.sync
        def _(e):
            for tc in range(16):
                e.wait_ge(sLs[0], 16 * (tc + 1))
                e.wait_ge(sLs[1], 16 * (tc + 1))
                if tc >= 2:
                    e.wait_ge(sV2, 2 * tc - 2)
                for dc in range(8):
                    e.dma_start(
                        out=lgchB[tc % 2][:, dc, :],
                        in_=xn_tok_d[256 * tc:256 * (tc + 1),
                                     128 * dc:128 * (dc + 1)],
                        transpose=True).then_inc(sT2, 16)

        @blk.scalar
        def _(e):
            for i in range(min(2, NCH)):
                e.dma_start(out=xch[i % 2][:, :],
                            in_=chunks[i][0][128 * chunks[i][1]:
                                             128 * chunks[i][1] + 128, :]
                            ).then_inc(sL[i % 2], 16)
            for i in range(NCH):
                e.wait_ge(sL[i % 2], 16 * (i // 2 + 1))
                if i >= 1:
                    e.wait_ge(sLv, 2 * i - 1)
                e.activation(sqscr[:, :], xch[i % 2][:, :],
                             AF.Square).then_inc(sLa, 1)
                e.wait_ge(sLv, 2 * i + 1)
                e.activation(stat[:, 5:6], stat[:, 4:5],
                             AF.Sqrt).then_inc(sLa, 1)
                if i + 2 < NCH:
                    e.wait_ge(sLv, 2 * i + 2)
                    src, ci, dst = chunks[i + 2]
                    e.dma_start(out=xch[i % 2][:, :],
                                in_=src[128 * ci:128 * (ci + 1), :]
                                ).then_inc(sL[i % 2], 16)

        def _evac_tc(e, tc):
            e.wait_ge(sP2, 2 * tc + 1)
            e.tensor_copy(lgc[:, :], pb[4][0:8, 0:256]).then_inc(sV2, 1)
            vs(e, e.tensor_reduce(cscr[:, :], lgchB[tc % 2][:, :, :], AX.X,
                                  ALU.add))
            e.tensor_tensor(csum[:, :],
                            csum[:, :] if tc > 0 else zeros[:, 0:8],
                            cscr[:, :], ALU.add).then_inc(sV2, 1)

        @blk.vector
        def _(e):
            for i in range(NCH):
                e.wait_ge(sL[i % 2], 16 * (i // 2 + 1))
                e.tensor_reduce(stat[:, 0:1], xch[i % 2][:, :], AX.X, ALU.add)
                e.wait_ge(sLa, 2 * i + 1)
                vs(e, e.tensor_reduce(stat[:, 1:2], sqscr[:, :], AX.X, ALU.add))
                vs(e, e.tensor_scalar_mul(stat[:, 2:3], stat[:, 0:1], 1.0 / D))
                vs(e, e.tensor_tensor(stat[:, 3:4], stat[:, 2:3], stat[:, 2:3],
                                      ALU.mult))
                vs(e, e.scalar_tensor_tensor(stat[:, 4:5], stat[:, 1:2], 1.0 / D,
                                             stat[:, 3:4], ALU.mult,
                                             ALU.subtract))
                vs(e, e.tensor_scalar_add(stat[:, 4:5], stat[:, 4:5], EPS))
                e.reciprocal(stat[:, 4:5], stat[:, 4:5]).then_inc(sLv, 1)
                e.wait_ge(sLa, 2 * i + 2)
                if i >= 2:
                    e.wait_ge(sLs[i % 2], 16 * ((i - 2) // 2 + 1))
                    e.wait_ge(sES, 16 * (i - 1))
                e.tensor_scalar(xnst[i % 2][:, :], xch[i % 2][:, :],
                                stat[:, 2:3], stat[:, 5:6],
                                ALU.subtract, ALU.mult).then_inc(sLv, 1)
                if i >= 3 and i % 2 == 1:
                    _evac_tc(e, (i - 3) // 2)
            _evac_tc(e, 15)
            e.wait_ge(sP2, 32)
            e.tensor_copy(lg_t[:, 0:128], pb[5][:, 0:128])
            vs(e, e.tensor_copy(lg_t[:, 128:256], pb[6][:, 0:128]))
            lg3 = lg_t[:, :].rearrange("p (k e) -> p k e", e=8)
            vs(e, e.reduce_max(smx[:, :], lg3, AX.X))
            e.tensor_tensor(probs[:, :].rearrange("p (k e) -> p k e", e=8), lg3,
                            smx[:, :].unsqueeze(-1).broadcast_to([128, 32, 8]),
                            ALU.subtract).then_inc(sV2, 2)
            e.wait_ge(sE2, 1)
            pr3 = probs[:, :].rearrange("p (k e) -> p k e", e=8)
            vs(e, e.tensor_reduce(smx[:, :], pr3, AX.X, ALU.add))
            vs(e, e.reciprocal(smx[:, :], smx[:, :]))
            e.tensor_tensor(pr3, pr3,
                            smx[:, :].unsqueeze(-1).broadcast_to([128, 32, 8]),
                            ALU.mult)
            e.tensor_scalar_mul(pse[:, :], probs[:, :], 2.0)
            e.tensor_copy(totb[:, :], csum[:, :]).then_inc(sV2, 1)

    _phases.append(lambda: None)

    def _ph3(blk):
        if True:
            @blk.scalar
            def _(e):
                e.wait_ge(sV2, 34)
                e.activation(probs[:, :], probs[:, :], AF.Exp).then_inc(sE2, 1)
                e.wait_ge(sLs[0], 16 * 17)
                e.wait_ge(sES, 16 * 32)
                for dc in range(8):
                    e.dma_start(out=xnE[:, dc, :],
                                in_=xn_ext_d[0:EXT, 128 * dc:128 * (dc + 1)],
                                transpose=True).then_inc(sTE, 16)

            @blk.tensor
            def _(e):
                for tc in range(16):
                    e.wait_ge(sT2, 128 * (tc + 1))
                    if tc > 0:
                        e.wait_ge(sV2, 2 * tc)
                    for dc in range(8):
                        inst = e.matmul(pb[4][0:8, 0:256], wgT[:, dc, :],
                                        lgchB[tc % 2][:, dc, :],
                                        start=(dc == 0), stop=(dc == 7))
                    inst.then_inc(sP2, 1)
                    e.wait_ge(sV2, 2 * tc + 1)
                    for half in range(2):
                        k = 2 * tc + half
                        bank = 5 if k < 16 else 6
                        inst = e.transpose(
                            pb[bank][:, 8 * (k % 16):8 * (k % 16) + 8],
                            lgc[0:8, 128 * half:128 * (half + 1)],
                            identf[0:8, 0:8])
                    inst.then_inc(sP2, 1)

    def _phM3():
        with nc.Block() as blk:
            _ph2b(blk)
            _ph3(blk)
            _ph4(blk)
            _ph5(blk)
            _ph6(blk)

    _phases.append(_phM3)

    def _ph4(blk):
        if True:
            @blk.tensor
            def _(e):
                e.wait_ge(sV2, 35)
                e.wait_ge(s0g, 48)
                for dp in range(8):
                    for dc in range(8):
                        e.matmul(pb[7][:, 2 * dp:2 * dp + 1],
                                 w1sb[:, dc, 128 * dp:128 * (dp + 1)],
                                 totb[:, dc:dc + 1], start=(dc == 0), stop=False)
                    inst = e.matmul(pb[7][:, 2 * dp:2 * dp + 1],
                                    b1rb[0:1, 128 * dp:128 * (dp + 1)],
                                    c4096[:, :], start=False, stop=True)
                inst.then_inc(sP2b, 1)

            @blk.vector
            def _(e):
                e.wait_ge(sP2b, 1)
                vs(e, e.tensor_copy(totp[:, :],
                                    pb[7][:, 0:16].rearrange("p (a b) -> p a b", b=2)[:, :, 0]))
                e.tensor_tensor(mw0t[:, :], mwT[:, :, 0], totp[:, :], ALU.mult)

    _phases.append(lambda: None)

    def _ph5(blk):
        if True:
            @blk.vector
            def _(e):
                vs(e, e.tensor_scalar(lo2[:, :], pse[:, 0:8], 0.0, None, ALU.mult))
                vs(e, e.tensor_scalar(hi2[:, :], pse[:, 0:8], 0.0, None, ALU.mult))
                vs(e, e.tensor_scalar_add(hi2[:, :], hi2[:, :], 2.0))
                for it in range(NBIS):
                    vs(e, e.tensor_tensor(mid2[:, :], lo2[:, :], hi2[:, :], ALU.add))
                    vs(e, e.tensor_scalar_mul(mid2[:, :], mid2[:, :], 0.5))
                    e.tensor_tensor(cmp_sc[:, :].rearrange("p (k e) -> p k e", e=8),
                                    pse[:, :].rearrange("p (k e) -> p k e", e=8),
                                    mid2[:, :].unsqueeze(1).broadcast_to([128, 32, 8]),
                                    ALU.is_ge)
                    e.tensor_reduce(part8[:, :],
                                    cmp_sc[:, :].rearrange("p (k e) -> p e k", e=8),
                                    AX.X, ALU.add).then_inc(sBv, 1)
                    e.wait_ge(sBp, 2 * it + 1)
                    e.tensor_copy(cnt8[:, :], pb[4][0:1, 0:8]).then_inc(sBv, 1)
                    e.wait_ge(sBp, 2 * it + 2)
                    vs(e, e.tensor_copy(cge[:, :], pb[4][:, 8:16]))
                    vs(e, e.tensor_scalar(cgei_f[:, :], cge[:, :], 512.0, None,
                                          ALU.is_ge))
                    vs(e, e.tensor_tensor(bscr[:, :], mid2[:, :], lo2[:, :],
                                          ALU.subtract))
                    vs(e, e.tensor_tensor(bscr[:, :], bscr[:, :], cgei_f[:, :],
                                          ALU.mult))
                    vs(e, e.tensor_tensor(lo2[:, :], lo2[:, :], bscr[:, :], ALU.add))
                    vs(e, e.tensor_tensor(bscr[:, :], hi2[:, :], mid2[:, :],
                                          ALU.subtract))
                    vs(e, e.tensor_tensor(bscr[:, :], bscr[:, :], cgei_f[:, :],
                                          ALU.mult))
                    e.tensor_tensor(hi2[:, :], mid2[:, :], bscr[:, :], ALU.add
                                    ).then_inc(sBv, 1)

            @blk.tensor
            def _(e):
                for it in range(NBIS):
                    e.wait_ge(sBv, 3 * it + 1)
                    e.matmul(pb[4][0:1, 0:8], onesf[:, :], part8[:, :],
                             start=True, stop=True).then_inc(sBp, 1)
                    e.wait_ge(sBv, 3 * it + 2)
                    e.matmul(pb[4][:, 8:16], onesr[:, :], cnt8[:, :],
                             start=True, stop=True).then_inc(sBp, 1)

    _phases.append(lambda: None)

    def _ph6(blk):
        if True:
            @blk.vector
            def _(e):
                pr3 = pse[:, :].rearrange("p (k e) -> p k e", e=8)
                mf3 = maskf[:, :].rearrange("p (k e) -> p k e", e=8)
                e.tensor_tensor(mf3, pr3,
                                lo2[:, :].unsqueeze(1).broadcast_to([128, 32, 8]),
                                ALU.is_ge)
                e.tensor_tensor(mf3, mf3,
                                half_sel[:, :].unsqueeze(-1).broadcast_to([128, 32, 8]),
                                ALU.mult)
                il = None
                for ee in range(E):
                    il = e.tensor_copy(maske[:, ee, :],
                                       maskf[:, :].rearrange("p (k e) -> p e k", e=8)[:, ee, :])
                vs(e, il)
                for ee in range(E):
                    il = e.tensor_tensor_scan(incl[:, ee, :], maske[:, ee, :],
                                              zeros[:, 0:32], 0.0, ALU.add, ALU.add)
                vs(e, il)
                vs(e, e.tensor_copy(cnts[:, :], incl[:, :, 31]))
                vs(e, e.tensor_copy(cnts_b[:, :], cnts[:, :]))
                e.tensor_tensor(lrank[:, :, :], incl[:, :, :], maske[:, :, :],
                                ALU.subtract)
                e.scalar_tensor_tensor(lrank[:, :, :], lrank[:, :, :], 1.0,
                                       maske[:, :, :], ALU.add, ALU.mult
                                       ).then_inc(sRv, 1)
                e.wait_ge(sRp, 1)
                vs(e, e.tensor_copy(starts[:, :], pb[4][:, 16:24]))
                e.tensor_copy(cscr[:, 0:8], starts[:, :]).then_inc(sRv, 1)
                for ee in range(E):
                    i3 = indsc[:, :].rearrange("p (j k) -> p j k", j=JSEG)
                    o3 = ind2[:, :].rearrange("p (j k) -> p j k", j=JSEG)
                    e.tensor_tensor(
                        i3,
                        lrank[:, ee, :].unsqueeze(1).broadcast_to([128, JSEG, 32]),
                        iotaJ[:, :].unsqueeze(-1).broadcast_to([128, JSEG, 32]),
                        ALU.is_equal)
                    e.tensor_tensor(
                        o3, i3,
                        tokid1[:, :].unsqueeze(1).broadcast_to([128, JSEG, 32]),
                        ALU.mult)
                    e.tensor_reduce(lst2[:, ee, 0:JSEG], o3, AX.X, ALU.add)
                    e.tensor_tensor(
                        o3, i3,
                        probs[:, :].rearrange("p (k e) -> p e k", e=8)[:, ee, :]
                            .unsqueeze(1).broadcast_to([128, JSEG, 32]),
                        ALU.mult)
                    e.tensor_reduce(lst2[:, ee, JSEG:2 * JSEG], o3, AX.X,
                                    ALU.add).then_inc(sRv, 1)
                ejb = [ej, ej2]
                rv, rp = 2 + E, 1
                for ee in range(E):
                    vs(e, e.tensor_single_scalar(sdiff[:, :], iotaS[:, :],
                                                 starts[:, ee:ee + 1],
                                                 ALU.subtract))
                    for cchunk in range(3):
                        for j in range(JSEG):
                            if j >= 2:
                                e.wait_ge(sRp, rp - 1)
                            elif 12 * cchunk + j >= 2 or ee > 0:
                                e.wait_ge(sRp, rp - 2)
                            e.tensor_scalar(ejb[j % 2][:, :], sdiff[:, :],
                                            float(j - 128 * cchunk), None,
                                            ALU.is_equal).then_inc(sRv, 1)
                            rv += 1
                            rp += 1
                        e.wait_ge(sRp, rp)
                        rp += 1
                        e.tensor_copy(scr2x[:, :], pb[5][0:2, 0:128]
                                      ).then_inc(sRv, 1)
                        rv += 1
                        e.wait_ge(sRp, rp)
                        vs(e, e.tensor_scalar(idsf[:, ee, cchunk:cchunk + 1],
                                              pb[5][:, 128:129],
                                              1.0, None, ALU.subtract))
                        inst = e.tensor_copy(gval[:, ee, cchunk:cchunk + 1],
                                             pb[5][:, 129:130])
                        inst.then_inc(sRv, 1)
                        rv += 1
                vs(e, e.tensor_copy(cscr[:, 0:8], starts[:, :]))
                vs(e, e.tensor_scalar_max(iscr[:, :, :], idsf[:, :, :], 0.0))
                vs(e, e.tensor_copy(idx_g[:, :, :], iscr[:, :, :]))
                vs(e, e.tensor_scalar(iscr[:, :, :], idsf[:, :, :], 0.0, None,
                                      ALU.is_lt))
                vs(e, e.scalar_tensor_tensor(iscr[:, :, :], iscr[:, :, :],
                                             float(DUMP + 1), idsf[:, :, :],
                                             ALU.mult, ALU.add))
                e.tensor_copy(idx_s[:, :, :], iscr[:, :, :])

            @blk.tensor
            def _(e):
                e.wait_ge(sRv, 1)
                e.matmul(pb[4][:, 16:24], tri[:, :], cnts_b[:, :],
                         start=True, stop=True).then_inc(sRp, 1)
                e.wait_ge(sRv, 2 + E)
                ejb = [ej, ej2]
                rv, rp = 2 + E, 1
                for ee in range(E):
                    for cchunk in range(3):
                        for j in range(JSEG):
                            rv += 1
                            e.wait_ge(sRv, rv)
                            e.matmul(pb[5][0:2, 0:128],
                                     lst2[:, ee, :].rearrange("p (a j) -> p j a", a=2)[:, j, :],
                                     ejb[j % 2][:, :],
                                     start=(j == 0), stop=(j == JSEG - 1)
                                     ).then_inc(sRp, 1)
                            rp += 1
                        rv += 1
                        e.wait_ge(sRv, rv)
                        e.transpose(pb[5][:, 128:130], scr2x[:, :],
                                    identf[0:2, 0:2]).then_inc(sRp, 1)
                        rp += 1
                        rv += 1

    _phases.append(lambda: None)

    def _ph7(blk):
        cols5 = [(0, 512), (512, 1024), (1024, 1536), (1536, 2048), (2048, EXT)]
        if True:
            @blk.tensor
            def _(e):
                e.wait_ge(sTE, 128)
                e.wait_ge(sV2, 34)
                idx = 0
                for dp in range(8):
                    for (c0, c1) in cols5:
                        if idx >= 2:
                            e.wait_ge(sA5, idx - 1)
                        bank = pb[4 + (idx % 2)]
                        for dc in range(8):
                            inst = e.matmul(bank[:, 0:c1 - c0],
                                            w1sb[:, dc, 128 * dp:128 * (dp + 1)],
                                            xnE[:, dc, c0:c1], start=(dc == 0),
                                            stop=(dc == 7))
                        inst.then_inc(sP5, 1)
                        idx += 1

            @blk.scalar
            def _(e):
                idx = 0
                for dp in range(8):
                    for (c0, c1) in cols5:
                        e.wait_ge(sP5, idx + 1)
                        e.activation(p_sb[:, dp, c0:c1],
                                     pb[4 + (idx % 2)][:, 0:c1 - c0], AF.Identity,
                                     bias=b1T[:, dp:dp + 1]).then_inc(sA5, 1)
                        idx += 1

    _phases.append(lambda: None)

    def _ph8():
        pass

    _phases.append(_ph8)

    def _ph9():
        TT3 = [(0, 128), (128, 256), (256, 320)]
        with nc.Block() as blk:
            _ph7(blk)
            def _gather(e, ee, ti):
                if CFG["p7_gather"]:
                    e.indirect_dma_start(
                        xeb[ee % 2][:, ti, :], None,
                        xn_tok_d[0:128, :],
                        IndirectOffsetOnAxis(idx_g[:, ee, ti:ti + 1], 0),
                    ).then_inc(sX, 16)
                else:
                    e.dma_start(out=xeb[ee % 2][:, ti, :],
                                in_=xn_tok_d[0:128, :]).then_inc(sX, 16)

            def _scatter(e, ee, ti, r0, r1):
                if CFG["p7_scatter"]:
                    e.indirect_dma_start(
                        acc_d[0:r1 - r0, :],
                        IndirectOffsetOnAxis(idx_s[0:r1 - r0, ee, ti:ti + 1], 0),
                        oute[0:r1 - r0, ti, :], None,
                        compute_op=ALU.add,
                    ).then_inc(sSC, 16)
                else:
                    e.dma_start(out=acc_d[T:T + r1 - r0, :],
                                in_=oute[0:r1 - r0, ti, :]).then_inc(sSC, 16)

            @blk.gpsimd
            def _(e):
                for ti in range(3):
                    _gather(e, 0, ti)
                for ee in range(CFG['p7_E']):
                    if ee + 1 < CFG['p7_E']:
                        if ee >= 1:
                            e.wait_ge(sXT, 16 * 24 * ee)
                        if CFG["p7_scblock"] and ee >= 2:
                            e.wait_ge(sSC, 48 * (ee - 1))
                        for ti in range(3):
                            _gather(e, ee + 1, ti)
                    if ee >= 2:
                        e.wait_ge(sPo, 6 * (ee - 1))
                    e.dma_start(out=b2er[ee % 2][:, :], in_=wb2_d[ee:ee + 1, :]
                                ).then_inc(sWb[ee % 2], 16)
                    for g in range(16):
                        G = 16 * ee + g
                        if G >= 4:
                            e.wait_ge(sPw, 2 * (G - 3))
                        e.dma_start(
                            out=w1e[G % 4][:, :, :],
                            in_=we1_d[ee, :, 256 * g:256 * (g + 1)]
                                .rearrange("(dc p) f -> p dc f", p=128),
                        ).then_inc(sWe[G % 4], 16)
                        e.dma_start(
                            out=w2e[G % 4][:, :, :],
                            in_=we2_d[ee, 256 * g:256 * (g + 1), :]
                                .rearrange("(ft p) dd -> p ft dd", p=128),
                        ).then_inc(sWe[G % 4], 16)
                    if ee > 0 and CFG["p7_scblock"]:
                        e.wait_ge(sO, 6 * ee)
                        e.wait_ge(sSC, 48 * (ee - 1))
                        e.wait_ge(sXT, 16 * 24 * min(ee + 2, CFG['p7_E']))
                        if ee == 1:
                            e.wait_ge(sZ, 528)
                        for ti, (r0, r1) in enumerate(TT3):
                            _scatter(e, ee - 1, ti, r0, r1)
                e.wait_ge(sO, 6 * CFG['p7_E'])
                if not CFG["p7_scblock"]:
                    return
                e.wait_ge(sSC, 48 * (CFG['p7_E'] - 1))
                for ti, (r0, r1) in enumerate(TT3):
                    if CFG["p7_scatter"]:
                        e.indirect_dma_start(
                            acc_d[0:r1 - r0, :],
                            IndirectOffsetOnAxis(idx_s[0:r1 - r0, CFG['p7_E'] - 1, ti:ti + 1], 0),
                            oute[0:r1 - r0, ti, :], None,
                            compute_op=ALU.add,
                        ).then_inc(sSC, 16)
                    else:
                        e.dma_start(out=acc_d[T:T + r1 - r0, :],
                                    in_=oute[0:r1 - r0, ti, :]).then_inc(sSC, 16)
                e.wait_ge(sSC, 48 * CFG['p7_E'])

            @blk.sync
            def _(e):
                for ee in range(CFG['p7_E']):
                    if ee >= 2:
                        e.wait_ge(sPh, 32 * (ee - 1))
                    for ti, (r0, r1) in enumerate(TT3):
                        e.wait_ge(sX, 48 * ee + 16 * (ti + 1))
                        for dc in range(8):
                            e.dma_start(out=xeT[ee % 2][:, dc, r0:r1],
                                        in_=xeb[ee % 2][0:r1 - r0, ti,
                                                        128 * dc:128 * (dc + 1)],
                                        transpose=True).then_inc(sXT, 16)

            @blk.scalar
            def _(e):
                for ee in range(CFG['p7_E']):
                    for ft in range(32):
                        FT = 32 * ee + ft
                        e.wait_ge(sPh, FT + 1)
                        if FT >= 2:
                            e.wait_ge(sPw, FT - 1)
                        e.activation(h1[ft % 2][:, :], pb[6 + (ft % 2)][:, 0:CAP],
                                     AF.Gelu_apprx_tanh,
                                     bias=b1all[:, 32 * ee + ft:32 * ee + ft + 1]
                                     ).then_inc(sH, 1)

            def self_w2(e, ee, ft):
                FT = 32 * ee + ft
                g = ft // 2
                G = 16 * ee + g
                e.wait_ge(sH, FT + 1)
                if ft == 0 and ee > 0:
                    e.wait_ge(sO, 6 * ee)
                for ti, (r0, r1) in enumerate(TT3):
                    for dh in range(2):
                        inst = e.matmul(
                            pb[2 * ti + dh][0:r1 - r0, 0:512],
                            h1[ft % 2][:, r0:r1],
                            w2e[G % 4][:, ft % 2, 512 * dh:512 * (dh + 1)],
                            start=(ft == 0), stop=False)
                inst.then_inc(sPw, 1)

            @blk.tensor
            def _(e):
                for ee in range(CFG['p7_E']):
                    e.wait_ge(sXT, 16 * 24 * (ee + 1))
                    e.wait_ge(sWb[ee % 2], 16 * (ee // 2 + 1))
                    for ft in range(32):
                        FT = 32 * ee + ft
                        g = ft // 2
                        G = 16 * ee + g
                        if ft % 2 == 0:
                            e.wait_ge(sWe[G % 4], 32 * (G // 4 + 1))
                        if FT >= 2:
                            e.wait_ge(sH, FT - 1)
                        hb = pb[6 + (ft % 2)]
                        for dc in range(8):
                            inst = e.matmul(
                                hb[:, 0:CAP],
                                w1e[G % 4][:, dc, 128 * (ft % 2):128 * (ft % 2 + 1)],
                                xeT[ee % 2][:, dc, :],
                                start=(dc == 0), stop=(dc == 7))
                        inst.then_inc(sPh, 1)
                        if ft >= 1:
                            self_w2(e, ee, ft - 1)
                    self_w2(e, ee, 31)
                    for ti, (r0, r1) in enumerate(TT3):
                        for dh in range(2):
                            inst = e.matmul(
                                pb[2 * ti + dh][0:r1 - r0, 0:512],
                                onesc[0:1, 0:r1 - r0],
                                b2er[ee % 2][0:1, 512 * dh:512 * (dh + 1)],
                                start=False, stop=True)
                    inst.then_inc(sPo, 6)

            @blk.vector
            def _(e):
                for ee in range(CFG['p7_E']):
                    dp = ee
                    e.wait_ge(sA5, 5 * (dp + 1))
                    e.tensor_scalar_mul(p_sb[:, dp, 0:128], p_sb[:, dp, 0:128],
                                        mask_halo[:, 0:1])
                    for q in range(4):
                        base = 128 + 512 * q
                        for c in range(5):
                            k0 = 8 * c
                            e.tensor_scalar_mul(chain[c][:, :],
                                                p_sb[:, dp, base - k0:base - k0 + 512],
                                                taps[:, dp, k0:k0 + 1])
                            for k in range(k0 + 1, k0 + 8):
                                e.scalar_tensor_tensor(
                                    chain[c][:, :],
                                    p_sb[:, dp, base - k:base - k + 512],
                                    taps[:, dp, k:k + 1], chain[c][:, :],
                                    ALU.mult, ALU.add)
                        e.tensor_tensor(tc0[:, :], chain[0][:, :], chain[1][:, :],
                                        ALU.add)
                        e.tensor_tensor(tc1[:, :], chain[2][:, :], chain[3][:, :],
                                        ALU.add)
                        e.tensor_tensor(tc0[:, :], tc0[:, :], tc1[:, :], ALU.add)
                        e.scalar_tensor_tensor(band[:, dp, 512 * q:512 * (q + 1)],
                                               chain[4][:, :], mw0t[:, dp:dp + 1],
                                               tc0[:, :], ALU.add, ALU.add)
                    if ee > 0 and CFG["p7_scblock"]:
                        e.wait_ge(sSC, 48 * ee)
                    for ti, (r0, r1) in enumerate(TT3):
                        for dh in range(2):
                            e.wait_ge(sPo, 6 * (ee + 1))
                            e.tensor_scalar_mul(
                                oute[0:r1 - r0, ti, 512 * dh:512 * (dh + 1)],
                                pb[2 * ti + dh][0:r1 - r0, 0:512],
                                gval[0:r1 - r0, ee, ti:ti + 1]).then_inc(sO, 1)

    _phases.append(_ph9)

    def _ph10():
        with nc.Block() as blk:
            @blk.gpsimd
            def _(e):
                e.dma_start(
                    out=w1sb[:, :, :],
                    in_=w2_d[:, :].rearrange("(dp p) dd -> p dp dd", p=128),
                ).then_inc(s8w, 16)
                for tt in range(16):
                    if tt >= 2:
                        e.wait_ge(s8a, tt - 1)
                    e.indirect_dma_start(
                        accsb[tt % 2][:, :], None,
                        acc_d[0:128, :],
                        IndirectOffsetOnAxis(own_ids[:, tt:tt + 1], 0),
                    ).then_inc(s8g[tt % 2], 16)

            @blk.sync
            def _(e):
                for tt in range(16):
                    if tt >= 2:
                        e.wait_ge(s8v, 2 * (tt - 1))
                    e.dma_start(out=xch[tt % 2][:, :],
                                in_=x_ext_d[128 * (tt + 1):128 * (tt + 2), :]
                                ).then_inc(s8x[tt % 2], 16)
                    if tt >= 2:
                        j = tt - 2
                        e.wait_ge(s8v, 2 * j + 2)
                        e.dma_start(out=out_d[128 * j:128 * (j + 1), :],
                                    in_=outch[j % 2][:, :]).then_inc(s8o, 16)
                for j in (14, 15):
                    e.wait_ge(s8v, 2 * j + 2)
                    e.dma_start(out=out_d[128 * j:128 * (j + 1), :],
                                in_=outch[j % 2][:, :]).then_inc(s8o, 16)
                e.wait_ge(s8o, 16 * 16)

            @blk.tensor
            def _(e):
                e.wait_ge(s8w, 16)
                for tt in range(16):
                    if tt >= 2:
                        e.wait_ge(s8v, 2 * (tt - 1))
                    for dp in range(8):
                        for dh in range(2):
                            e.matmul(pb[2 * (tt % 2) + dh][:, 0:512],
                                     p_sb[:, dp, 128 * tt:128 * (tt + 1)],
                                     w1sb[:, dp, 512 * dh:512 * (dh + 1)],
                                     start=(dp == 0), stop=False)
                    for dh in range(2):
                        e.matmul(pb[2 * (tt % 2) + dh][:, 0:512],
                                 onesc[0:1, 0:128],
                                 b2rb[0:1, 512 * dh:512 * (dh + 1)],
                                 start=False, stop=True).then_inc(s8p, 1)

            @blk.scalar
            def _(e):
                for tt in range(16):
                    e.wait_ge(s8g[tt % 2], 16 * (tt // 2 + 1))
                    if tt > 0:
                        e.wait_ge(s8v, 2 * tt)
                    e.activation(gsb[:, :], accsb[tt % 2][:, :],
                                 AF.Silu).then_inc(s8a, 1)

            @blk.vector
            def _(e):
                for tt in range(16):
                    e.wait_ge(s8a, tt + 1)
                    e.wait_ge(s8p, 2 * tt + 2)
                    e.wait_ge(s8x[tt % 2], 16 * (tt // 2 + 1))
                    if tt >= 2:
                        e.wait_ge(s8o, 16 * (tt - 1))
                    for dh in range(2):
                        e.tensor_tensor(outch[tt % 2][:, 512 * dh:512 * (dh + 1)],
                                        gsb[:, 512 * dh:512 * (dh + 1)],
                                        pb[2 * (tt % 2) + dh][:, 0:512], ALU.mult)
                    e.tensor_tensor(outch[tt % 2][:, :], outch[tt % 2][:, :],
                                    xch[tt % 2][:, :], ALU.add).then_inc(s8v, 2)

    _phases.append(_ph10)

    for _k in range(min(upto + 1, len(_phases))):
        _phases[_k]()

    if dbg:
        dbg_items = [
            ("d_probs", probs), ("d_pse", pse), ("d_lo2", lo2),
            ("d_cnts", cnts), ("d_starts", starts), ("d_csum", csum),
            ("d_totp", totp), ("d_mw0t", mw0t), ("d_b1T", b1T),
            ("d_b2T", b2T), ("d_lgt", lg_t), ("d_lrank", lrank),
            ("d_idsf", idsf), ("d_gval", gval), ("d_taps", taps),
            ("d_idxg", idx_g), ("d_idxs", idx_s), ("d_xnE", xnE),
            ("d_p", p_sb),
        ]
        sD = sem("sD")
        with nc.Block() as blk:
            @blk.sync
            def _(e):
                n = 0
                for name, t in dbg_items:
                    sh = list(t.shape)
                    fl = int(np.prod(sh[1:]))
                    od = nc.dram_tensor(name, [sh[0], fl], t.dtype,
                                        kind="ExternalOutput")
                    src = t[tuple(slice(None) for _ in sh)]
                    if len(sh) == 3:
                        src = src.rearrange("p a b -> p (a b)")
                    e.dma_start(out=od[:, :], in_=src).then_inc(sD, 16)
                    n += 16
                e.wait_ge(sD, n)

    ctx.close()
    return nc


def kernel(L0_attn_input_data, L0_attn_W1, L0_attn_bias1, L0_attn_W2,
           L0_attn_bias2, L0_attn_memory_weights, L0_ffn_Wg, L0_ffn_W1,
           L0_ffn_b1, L0_ffn_W2, L0_ffn_b2):
    def host_f32(a):
        a = np.asarray(a)
        return np.ascontiguousarray(a.astype(np.float32, copy=False))

    x = host_f32(L0_attn_input_data)
    w1 = host_f32(L0_attn_W1)
    b1 = host_f32(L0_attn_bias1).reshape(1, D)
    b2 = host_f32(L0_attn_bias2).reshape(1, D)
    w2 = host_f32(L0_attn_W2)
    mw = host_f32(L0_attn_memory_weights)
    wg = host_f32(L0_ffn_Wg)
    we1 = host_f32(L0_ffn_W1)
    wb1 = host_f32(L0_ffn_b1)
    we2 = host_f32(L0_ffn_W2)
    wb2 = host_f32(L0_ffn_b2)

    nc = build_nc()
    in_maps = make_in_maps(x, w1, b1, w2, b2, mw, wg, we1, wb1, we2, wb2)

    res = run_bass_kernel_spmd(nc, in_maps, core_ids=list(range(8)))
    out = np.empty((B, T, D), np.float32)
    for c in range(8):
        b, h = c // 2, c % 2
        out[b, HT * h:HT * (h + 1), :] = res.results[c]["out"]
    return out


def make_in_maps(x, w1, b1, w2, b2, mw, wg, we1, wb1, we2, wb2):
    pcol = np.arange(128)
    in_maps = []
    for c in range(8):
        b, h = c // 2, c % 2
        x_row = x[b]
        if h == 0:
            halo = np.zeros((128, D), np.float32)
        else:
            halo = x_row[HT - 128:HT]
        x_ext = np.concatenate([halo, x_row[HT * h:HT * (h + 1)]], axis=0)
        mask_halo = np.full((128, 1), float(h), np.float32)
        ks = np.arange(32)
        half_sel = np.broadcast_to(((ks // 16) == h).astype(np.float32),
                                   (128, 32)).copy()
        own_ids = (HT * h + 128 * np.arange(16)[None, :] +
                   pcol[:, None]).astype(np.int32)
        tok = 128 * np.arange(32)[None, :] + pcol[:, None]
        own = (tok // HT) == h
        ext_ids = np.where(own, 128 + tok - HT * h, EXT).astype(np.int32)
        in_maps.append({
            "x_row": np.ascontiguousarray(x_row),
            "x_ext": np.ascontiguousarray(x_ext),
            "mask_halo": mask_halo,
            "half_sel": half_sel,
            "own_ids": own_ids,
            "ext_ids": ext_ids,
            "w1": w1, "b1": b1, "w2": w2, "b2": b2, "mw": mw, "wg": wg,
            "we1": we1, "wb1": wb1, "we2": we2, "wb2": wb2,
        })
    return in_maps


# version 26
# speedup vs baseline: 1.1634x; 1.0292x over previous
import numpy as np
from contextlib import ExitStack

import concourse.bass as bass
import concourse.mybir as mybir
from concourse.bass import IndirectOffsetOnAxis
from concourse.bass_utils import run_bass_kernel_spmd
import concourse.bass_utils as _bass_utils

if not getattr(_bass_utils, "_ldw_patched", False):
    _orig_run_command = _bass_utils.run_command

    def _run_command_ldw(argv, **kwargs):
        argv = ["--enable-ldw-opt=true" if a == "--enable-ldw-opt=false" else a
                for a in argv]
        return _orig_run_command(argv, **kwargs)

    _bass_utils.run_command = _run_command_ldw
    _bass_utils._ldw_patched = True

F32 = mybir.dt.float32
BF16 = mybir.dt.bfloat16
I32 = mybir.dt.int32
AF = mybir.ActivationFunctionType
ALU = mybir.AluOpType
AX = mybir.AxisListType

B, T, D = 4, 4096, 1024
HT = T // 2
EXT = HT + 128
E, FF = 8, 4096
CAP = 320
JSEG = 12
EPS = 1e-5
NBIS = 36
DUMP = T


CFG = {"p7_gather": True, "p7_transpose": True, "p7_scatter": True,
       "p7_w2": True, "p7_E": 8, "p7_scblock": True}


def build_nc(upto=99, dbg=False):
    nc = bass.Bass("TRN2", target_bir_lowering=False,
                   dynamic_dma_scratch_size=16384)

    x_row_d = nc.dram_tensor("x_row", [T, D], F32, kind="ExternalInput")
    x_ext_d = nc.dram_tensor("x_ext", [EXT, D], F32, kind="ExternalInput")
    mask_halo_d = nc.dram_tensor("mask_halo", [128, 1], F32, kind="ExternalInput")
    half_sel_d = nc.dram_tensor("half_sel", [128, 32], F32, kind="ExternalInput")
    own_ids_d = nc.dram_tensor("own_ids", [128, 16], I32, kind="ExternalInput")
    ext_ids_d = nc.dram_tensor("ext_ids", [128, 32], I32, kind="ExternalInput")
    w1_d = nc.dram_tensor("w1", [D, D], F32, kind="ExternalInput")
    b1_d = nc.dram_tensor("b1", [1, D], F32, kind="ExternalInput")
    w2_d = nc.dram_tensor("w2", [D, D], F32, kind="ExternalInput")
    b2_d = nc.dram_tensor("b2", [1, D], F32, kind="ExternalInput")
    mw_d = nc.dram_tensor("mw", [41, D], F32, kind="ExternalInput")
    wg_d = nc.dram_tensor("wg", [E, D], F32, kind="ExternalInput")
    we1_d = nc.dram_tensor("we1", [E, D, FF], F32, kind="ExternalInput")
    wb1_d = nc.dram_tensor("wb1", [E, FF], F32, kind="ExternalInput")
    we2_d = nc.dram_tensor("we2", [E, FF, D], F32, kind="ExternalInput")
    wb2_d = nc.dram_tensor("wb2", [E, D], F32, kind="ExternalInput")

    out_d = nc.dram_tensor("out", [HT, D], F32, kind="ExternalOutput")

    xn_tok_d = nc.dram_tensor("xn_tok", [T, D], BF16, kind="Internal")
    xn_ext_d = nc.dram_tensor("xn_ext", [EXT + 1, D], BF16, kind="Internal")
    acc_d = nc.dram_tensor("acc", [T + 128, D], F32, kind="Internal")

    A = nc.alloc_sbuf_tensor
    xnE = A("xnE", [128, 8, EXT], BF16)
    lgch = A("lgch", [128, 8, 256], BF16)
    p_sb = A("p_sb", [128, 8, EXT], BF16)
    chain = [A(f"chain{i}", [128, 512], BF16) for i in range(5)]
    tc0 = A("tc0", [128, 512], F32)
    tc1 = A("tc1", [128, 512], F32)
    w1sb = A("w1sb", [128, 8, D], BF16)
    xch = [A(f"xch{i}", [128, D], F32) for i in range(2)]
    xnst = [A(f"xnst{i}", [128, D], BF16) for i in range(2)]
    stat = A("stat", [128, 16], F32)
    sqscr = A("sqscr", [128, D], F32)
    lgc = A("lgc", [8, 256], F32)
    lg_t = A("lg_t", [128, 256], F32)
    smx = A("smx", [128, 32], F32)
    probs = A("probs", [128, 256], F32)
    pse = A("pse", [128, 256], F32)
    csum = A("csum", [128, 8], F32)
    cscr = A("cscr", [128, 8], F32)
    totb = A("totb", [128, 8], BF16)
    totp = A("totp", [128, 8], F32)
    mwT = A("mwT", [128, 8, 41], F32)
    taps = A("taps", [128, 8, 40], F32)
    mw0t = A("mw0t", [128, 8], F32)
    b1T = A("b1T", [128, 8], F32)
    b2T = A("b2T", [128, 8], F32)
    wgT = A("wgT", [128, 8, 8], BF16)
    mwsb = A("mwsb", [41, 128], F32)
    wgsb = A("wgsb", [8, D], F32)
    bsb = A("bsb", [1, D], F32)
    b1rb = A("b1rb", [1, D], BF16)
    b2rb = A("b2rb", [1, D], BF16)
    identf = A("identf", [128, 128], F32)
    tri = A("tri", [128, 128], BF16)
    onesc = A("onesc", [128, 512], BF16)
    onesf = A("onesf", [128, 1], F32)
    onesr = A("onesr", [1, 128], F32)
    zeros = A("zeros", [128, D], F32)
    c4096 = A("c4096", [1, 1], BF16)
    mask_halo = A("mask_halo_sb", [128, 1], F32)
    half_sel = A("half_sel_sb", [128, 32], F32)
    own_ids = A("own_ids_sb", [128, 16], I32)
    ext_ids = A("ext_ids_sb", [128, 32], I32)
    lo2 = A("lo2", [128, 8], F32)
    hi2 = A("hi2", [128, 8], F32)
    mid2 = A("mid2", [128, 8], F32)
    cnt8 = A("cnt8", [1, 8], F32)
    cmp_sc = A("cmp_sc", [128, 256], F32)
    part8 = A("part8", [128, 8], F32)
    cge = A("cge", [128, 8], F32)
    cgei_f = A("cgei_f", [128, 8], F32)
    bscr = A("bscr", [128, 8], F32)
    maskf = A("maskf", [128, 256], F32)
    maske = A("maske", [128, 8, 32], F32)
    incl = A("incl", [128, 8, 32], F32)
    lrank = A("lrank", [128, 8, 32], F32)
    cnts = A("cnts", [128, 8], F32)
    cnts_b = A("cnts_b", [128, 8], BF16)
    starts = A("starts", [128, 8], F32)
    tokid1 = A("tokid1", [128, 32], F32)
    iotaJ = A("iotaJ", [128, JSEG], F32)
    iotaS = A("iotaS", [128, 128], F32)
    indsc = A("indsc", [128, JSEG * 32], F32)
    ind2 = A("ind2", [128, JSEG * 32], F32)
    ej = A("ej", [128, 128], F32)
    sdiff = A("sdiff", [128, 128], F32)
    ej2 = A("ej2", [128, 128], F32)
    scr2x = A("scr2x", [2, 128], F32)
    lst2 = A("lst2", [128, 8, 2 * JSEG], F32)
    idsf = A("idsf", [128, 8, 3], F32)
    gval = A("gval", [128, 8, 3], F32)
    iscr = A("iscr", [128, 8, 3], F32)
    idx_g = A("idx_g", [128, 8, 3], I32)
    idx_s = A("idx_s", [128, 8, 3], I32)
    xe = A("xe", [128, 3, D], BF16)
    xeT = [A(f"xeT{i}", [128, 8, CAP], BF16) for i in range(2)]
    h1 = [A(f"h1{i}", [128, CAP], BF16) for i in range(2)]
    w1e = [A(f"w1e{i}", [128, 8, 256], BF16) for i in range(2)]
    w2e = [A(f"w2e{i}", [128, 2, D], BF16) for i in range(2)]
    b1all = A("b1all", [128, E * FF // 128], F32)
    b2er = [A(f"b2er{i}", [1, D], BF16) for i in range(2)]
    def overlay(name, shape, dtype, host, byte_off=0):
        return nc.alloc_sbuf_tensor_at(
            name, shape, dtype, offset=nc.lookup_mloc(host).addr + byte_off)

    oute = overlay("oute", [128, 3, D], F32, xnE)
    w1e.append(overlay("w1e2", [128, 8, 256], BF16, wgsb))
    w2e.append(overlay("w2e2", [128, 2, D], BF16, bsb))
    w1e.append(overlay("w1e3", [128, 8, 256], BF16, zeros))
    w2e.append(overlay("w2e3", [128, 2, D], BF16, maskf))
    accsb = [overlay("accsb0", [128, D], F32, xnE, 12288),
             overlay("accsb1", [128, D], F32, xnE, 16384)]
    gsb = overlay("gsb", [128, D], F32, xnE, 20480)
    outch = [overlay("outch0", [128, D], F32, xnE, 24576),
             overlay("outch1", [128, D], F32, xnE, 28672)]
    xe2 = overlay("xe2", [128, 3, D], BF16, sqscr)
    assert nc.lookup_mloc(smx).addr >= nc.lookup_mloc(sqscr).addr + 6144
    xeb = [xe, xe2]
    lgchB = [lgch, overlay("lgch2", [128, 8, 256], BF16, xeT[0])]

    pb = [nc.alloc_psum_tensor(f"pb{i}", [128, 512], F32) for i in range(8)]

    ctx = ExitStack()

    def sem(name):
        return ctx.enter_context(nc.semaphore(name))

    s0 = sem("s0")
    s0g = sem("s0g")
    s0f = sem("s0f")
    sB1 = sem("sB1")
    sZ = sem("sZ")
    s1 = sem("s1")
    s1t = sem("s1t")
    sL = [sem("sL0"), sem("sL1")]
    sLa = sem("sLa")
    sLv = sem("sLv")
    sLs = [sem("sLs0"), sem("sLs1")]
    sES = sem("sES")
    sT2b = sem("sT2b")
    sTE = sem("sTE")
    sT2 = sem("sT2")
    sP2 = sem("sP2")
    sV2 = sem("sV2")
    sE2 = sem("sE2")
    sP2b = sem("sP2b")
    sBv = sem("sBv")
    sBp = sem("sBp")
    sRv = sem("sRv")
    sRp = sem("sRp")
    sP5 = sem("sP5")
    sA5 = sem("sA5")
    sWe = [sem("sWe0"), sem("sWe1"), sem("sWe2"), sem("sWe3")]
    sWb = [sem("sWb0"), sem("sWb1")]
    sX = sem("sX")
    sXT = sem("sXT")
    sH = sem("sH")
    sPh = sem("sPh")
    sPw = sem("sPw")
    sPo = sem("sPo")
    sO = sem("sO")
    sSC = sem("sSC")
    s8w = sem("s8w")
    s8p = sem("s8p")
    s8g = [sem("s8g0"), sem("s8g1")]
    s8x = [sem("s8x0"), sem("s8x1")]
    s8a = sem("s8a")
    s8v = sem("s8v")
    s8o = sem("s8o")

    sVsn = sem("sVsn")
    _VS = {"n": 0}

    def vs(e, inst):
        _VS["n"] += 1
        inst.then_inc(sVsn, 1)
        e.wait_ge(sVsn, _VS["n"])

    band = p_sb

    _phases = []

    def _ph0():
        with nc.Block() as blk:
            @blk.sync
            def _(e):
                for (dst, src) in [
                    (mask_halo[:, :], mask_halo_d[:, :]),
                    (half_sel[:, :], half_sel_d[:, :]),
                    (own_ids[:, :], own_ids_d[:, :]),
                    (ext_ids[:, :], ext_ids_d[:, :]),
                    (wgsb[:, :], wg_d[:, :]),
                    (bsb[:, :], b1_d[:, :]),
                ]:
                    e.dma_start(out=dst, in_=src).then_inc(s0, 16)

            @blk.gpsimd
            def _(e):
                e.dma_start(
                    out=w1sb[:, :, :],
                    in_=w1_d[:, :].rearrange("(dc p) dp -> p dc dp", p=128),
                ).then_inc(s0g, 16)
                e.dma_start(out=b1rb[:, :], in_=b1_d[:, :]).then_inc(s0g, 16)
                e.dma_start(out=b2rb[:, :], in_=b2_d[:, :]).then_inc(s0g, 16)
                e.memset(zeros[:, :], 0.0).then_inc(s0f, 1)
                e.memset(onesc[:, :], 1.0).then_inc(s0f, 1)
                e.memset(onesf[:, :], 1.0).then_inc(s0f, 1)
                e.memset(onesr[:, :], 1.0).then_inc(s0f, 1)
                e.memset(c4096[:, :], 4096.0).then_inc(s0f, 1)
                e.iota(tokid1[:, :], [[128, 32]], base=1, channel_multiplier=1,
                       allow_small_or_imprecise_dtypes=True).then_inc(s0f, 1)
                e.iota(iotaJ[:, :], [[1, JSEG]], base=1, channel_multiplier=0,
                       allow_small_or_imprecise_dtypes=True).then_inc(s0f, 1)
                e.iota(iotaS[:, :], [[1, 128]], base=0, channel_multiplier=0,
                       allow_small_or_imprecise_dtypes=True).then_inc(s0f, 1)
                e.memset(identf[:, :], 1.0).then_inc(s0f, 1)
                e.memset(tri[:, :], 1.0).then_inc(s0f, 1)
                e.wait_ge(s0f, 10)
                e.affine_select(identf[:, :], identf[:, :], [[1, 128]],
                                ALU.is_equal, 0.0, base=0,
                                channel_multiplier=-1).then_inc(s0f, 1)
                e.affine_select(tri[:, :], tri[:, :], [[1, 128]],
                                ALU.is_ge, 0.0, base=-1,
                                channel_multiplier=-1).then_inc(s0f, 1)
                e.wait_ge(s0f, 12)
                for i in range(33):
                    e.dma_start(out=acc_d[128 * i:128 * (i + 1), :],
                                in_=zeros[:, :]).then_inc(sZ, 16)

    _phases.append(_ph0)

    def _ph1():
        with nc.Block() as blk:
            @blk.sync
            def _(e):
                e.wait_ge(s0, 96)
                for dc in range(8):
                    if dc > 0:
                        e.wait_ge(s1t, dc)
                    e.dma_start(out=mwsb[:, :],
                                in_=mw_d[:, 128 * dc:128 * (dc + 1)]).then_inc(s1, 16)
                e.wait_ge(s1t, 9)
                e.dma_start(out=bsb[:, :], in_=b2_d[:, :]).then_inc(s1, 16)
                for half in range(2):
                    e.dma_start(
                        out=sqscr[:, 128 * half:128 * (half + 1)],
                        in_=wb1_d.rearrange("e (ft p) -> (e ft) p", p=128)
                        [128 * half:128 * (half + 1), :]).then_inc(sB1, 16)

            @blk.tensor
            def _(e):
                for dc in range(8):
                    e.wait_ge(s1, 16 * (dc + 1))
                    e.transpose(pb[0][:, 41 * dc:41 * (dc + 1)], mwsb[:, :],
                                identf[0:41, 0:41]).then_inc(s1t, 1)
                for dc in range(8):
                    e.transpose(pb[1][:, 8 * dc:8 * (dc + 1)],
                                wgsb[:, 128 * dc:128 * (dc + 1)], identf[0:8, 0:8])
                for dp in range(8):
                    inst = e.transpose(pb[2][:, dp:dp + 1],
                                       bsb[0:1, 128 * dp:128 * (dp + 1)],
                                       identf[0:1, 0:1])
                inst.then_inc(s1t, 1)
                e.wait_ge(s1, 16 * 9)
                for dp in range(8):
                    inst = e.transpose(pb[2][:, 8 + dp:9 + dp],
                                       bsb[0:1, 128 * dp:128 * (dp + 1)],
                                       identf[0:1, 0:1])
                inst.then_inc(s1t, 1)
                e.wait_ge(sB1, 32)
                for half in range(2):
                    e.transpose(pb[3][:, 128 * half:128 * (half + 1)],
                                sqscr[:, 128 * half:128 * (half + 1)],
                                identf[:, :]).then_inc(s1t, 1)

            @blk.vector
            def _(e):
                e.wait_ge(s1t, 12)
                e.tensor_copy(b1all[:, :], pb[3][:, 0:256])
                vs(e, e.tensor_copy(
                    mwT[:, :, :],
                    pb[0][:, 0:328].rearrange("p (dc k) -> p dc k", dc=8)))
                e.tensor_copy(wgT[:, :, :],
                              pb[1][:, 0:64].rearrange("p (dc g) -> p dc g", dc=8))
                e.tensor_copy(b1T[:, :], pb[2][:, 0:8])
                e.tensor_copy(b2T[:, :], pb[2][:, 8:16])
                e.tensor_tensor(taps[:, :, :], mwT[:, :, 1:41],
                                mwT[:, :, 0:1].broadcast_to([128, 8, 40]),
                                ALU.subtract)

    _phases.append(_ph1)

    def _ph2b(blk):
        chunks = [(x_row_d, i, xn_tok_d) for i in range(32)] + \
                 [(x_ext_d, 0, xn_ext_d)]
        NCH = len(chunks)

        @blk.gpsimd
        def _(e):
            e.wait_ge(s0, 96)
            for j in range(NCH):
                e.wait_ge(sLv, 2 * j + 2)
                src2, cj, dj = chunks[j]
                e.dma_start(out=dj[128 * cj:128 * (cj + 1), :],
                            in_=xnst[j % 2][:, :]).then_inc(sLs[j % 2], 16)
                if j < 32:
                    e.indirect_dma_start(
                        xn_ext_d[:, :],
                        IndirectOffsetOnAxis(ext_ids[:, j:j + 1], 0),
                        xnst[j % 2][0:128, :], None,
                    ).then_inc(sES, 16)

        @blk.sync
        def _(e):
            for tc in range(16):
                e.wait_ge(sLs[0], 16 * (tc + 1))
                e.wait_ge(sLs[1], 16 * (tc + 1))
                if tc >= 2:
                    e.wait_ge(sV2, 2 * tc - 2)
                for dc in range(8):
                    e.dma_start(
                        out=lgchB[tc % 2][:, dc, :],
                        in_=xn_tok_d[256 * tc:256 * (tc + 1),
                                     128 * dc:128 * (dc + 1)],
                        transpose=True).then_inc(sT2, 16)

        @blk.scalar
        def _(e):
            for i in range(min(2, NCH)):
                e.dma_start(out=xch[i % 2][:, :],
                            in_=chunks[i][0][128 * chunks[i][1]:
                                             128 * chunks[i][1] + 128, :]
                            ).then_inc(sL[i % 2], 16)
            for i in range(NCH):
                e.wait_ge(sL[i % 2], 16 * (i // 2 + 1))
                if i >= 1:
                    e.wait_ge(sLv, 2 * i - 1)
                e.activation(sqscr[:, :], xch[i % 2][:, :],
                             AF.Square).then_inc(sLa, 1)
                e.wait_ge(sLv, 2 * i + 1)
                e.activation(stat[:, 5:6], stat[:, 4:5],
                             AF.Sqrt).then_inc(sLa, 1)
                if i + 2 < NCH:
                    e.wait_ge(sLv, 2 * i + 2)
                    src, ci, dst = chunks[i + 2]
                    e.dma_start(out=xch[i % 2][:, :],
                                in_=src[128 * ci:128 * (ci + 1), :]
                                ).then_inc(sL[i % 2], 16)

        def _evac_tc(e, tc):
            e.wait_ge(sP2, 2 * tc + 1)
            e.tensor_copy(lgc[:, :], pb[4][0:8, 0:256]).then_inc(sV2, 1)
            vs(e, e.tensor_reduce(cscr[:, :], lgchB[tc % 2][:, :, :], AX.X,
                                  ALU.add))
            e.tensor_tensor(csum[:, :],
                            csum[:, :] if tc > 0 else zeros[:, 0:8],
                            cscr[:, :], ALU.add).then_inc(sV2, 1)

        @blk.vector
        def _(e):
            for i in range(NCH):
                e.wait_ge(sL[i % 2], 16 * (i // 2 + 1))
                e.tensor_reduce(stat[:, 0:1], xch[i % 2][:, :], AX.X, ALU.add)
                e.wait_ge(sLa, 2 * i + 1)
                vs(e, e.tensor_reduce(stat[:, 1:2], sqscr[:, :], AX.X, ALU.add))
                vs(e, e.tensor_scalar_mul(stat[:, 2:3], stat[:, 0:1], 1.0 / D))
                vs(e, e.tensor_tensor(stat[:, 3:4], stat[:, 2:3], stat[:, 2:3],
                                      ALU.mult))
                vs(e, e.scalar_tensor_tensor(stat[:, 4:5], stat[:, 1:2], 1.0 / D,
                                             stat[:, 3:4], ALU.mult,
                                             ALU.subtract))
                vs(e, e.tensor_scalar_add(stat[:, 4:5], stat[:, 4:5], EPS))
                e.reciprocal(stat[:, 4:5], stat[:, 4:5]).then_inc(sLv, 1)
                e.wait_ge(sLa, 2 * i + 2)
                if i >= 2:
                    e.wait_ge(sLs[i % 2], 16 * ((i - 2) // 2 + 1))
                    e.wait_ge(sES, 16 * (i - 1))
                e.tensor_scalar(xnst[i % 2][:, :], xch[i % 2][:, :],
                                stat[:, 2:3], stat[:, 5:6],
                                ALU.subtract, ALU.mult).then_inc(sLv, 1)
                if i >= 3 and i % 2 == 1:
                    _evac_tc(e, (i - 3) // 2)
            _evac_tc(e, 15)
            e.wait_ge(sP2, 32)
            e.tensor_copy(lg_t[:, 0:128], pb[5][:, 0:128])
            vs(e, e.tensor_copy(lg_t[:, 128:256], pb[6][:, 0:128]))
            lg3 = lg_t[:, :].rearrange("p (k e) -> p k e", e=8)
            vs(e, e.reduce_max(smx[:, :], lg3, AX.X))
            e.tensor_tensor(probs[:, :].rearrange("p (k e) -> p k e", e=8), lg3,
                            smx[:, :].unsqueeze(-1).broadcast_to([128, 32, 8]),
                            ALU.subtract).then_inc(sV2, 2)
            e.wait_ge(sE2, 1)
            pr3 = probs[:, :].rearrange("p (k e) -> p k e", e=8)
            vs(e, e.tensor_reduce(smx[:, :], pr3, AX.X, ALU.add))
            vs(e, e.reciprocal(smx[:, :], smx[:, :]))
            e.tensor_tensor(pr3, pr3,
                            smx[:, :].unsqueeze(-1).broadcast_to([128, 32, 8]),
                            ALU.mult)
            e.tensor_scalar_mul(pse[:, :], probs[:, :], 2.0)
            e.tensor_copy(totb[:, :], csum[:, :]).then_inc(sV2, 1)

    _phases.append(lambda: None)

    def _ph3(blk):
        if True:
            @blk.scalar
            def _(e):
                e.wait_ge(sV2, 34)
                e.activation(probs[:, :], probs[:, :], AF.Exp).then_inc(sE2, 1)
                e.wait_ge(sLs[0], 16 * 17)
                e.wait_ge(sES, 16 * 32)
                for dc in range(8):
                    e.dma_start(out=xnE[:, dc, :],
                                in_=xn_ext_d[0:EXT, 128 * dc:128 * (dc + 1)],
                                transpose=True).then_inc(sTE, 16)

            @blk.tensor
            def _(e):
                for tc in range(16):
                    e.wait_ge(sT2, 128 * (tc + 1))
                    if tc > 0:
                        e.wait_ge(sV2, 2 * tc)
                    for dc in range(8):
                        inst = e.matmul(pb[4][0:8, 0:256], wgT[:, dc, :],
                                        lgchB[tc % 2][:, dc, :],
                                        start=(dc == 0), stop=(dc == 7))
                    inst.then_inc(sP2, 1)
                    e.wait_ge(sV2, 2 * tc + 1)
                    for half in range(2):
                        k = 2 * tc + half
                        bank = 5 if k < 16 else 6
                        inst = e.transpose(
                            pb[bank][:, 8 * (k % 16):8 * (k % 16) + 8],
                            lgc[0:8, 128 * half:128 * (half + 1)],
                            identf[0:8, 0:8])
                    inst.then_inc(sP2, 1)

    def _phM3():
        with nc.Block() as blk:
            _ph2b(blk)
            _ph3(blk)
            _ph4(blk)
            _ph5(blk)
            _ph6(blk)

    _phases.append(_phM3)

    def _ph4(blk):
        if True:
            @blk.tensor
            def _(e):
                e.wait_ge(sV2, 35)
                e.wait_ge(s0g, 48)
                for dp in range(8):
                    for dc in range(8):
                        e.matmul(pb[7][:, 2 * dp:2 * dp + 1],
                                 w1sb[:, dc, 128 * dp:128 * (dp + 1)],
                                 totb[:, dc:dc + 1], start=(dc == 0), stop=False)
                    inst = e.matmul(pb[7][:, 2 * dp:2 * dp + 1],
                                    b1rb[0:1, 128 * dp:128 * (dp + 1)],
                                    c4096[:, :], start=False, stop=True)
                inst.then_inc(sP2b, 1)

            @blk.vector
            def _(e):
                e.wait_ge(sP2b, 1)
                vs(e, e.tensor_copy(totp[:, :],
                                    pb[7][:, 0:16].rearrange("p (a b) -> p a b", b=2)[:, :, 0]))
                e.tensor_tensor(mw0t[:, :], mwT[:, :, 0], totp[:, :], ALU.mult)

    _phases.append(lambda: None)

    def _ph5(blk):
        if True:
            @blk.vector
            def _(e):
                vs(e, e.tensor_scalar(lo2[:, :], pse[:, 0:8], 0.0, None, ALU.mult))
                vs(e, e.tensor_scalar(hi2[:, :], pse[:, 0:8], 0.0, None, ALU.mult))
                vs(e, e.tensor_scalar_add(hi2[:, :], hi2[:, :], 2.0))
                for it in range(NBIS):
                    vs(e, e.tensor_tensor(mid2[:, :], lo2[:, :], hi2[:, :], ALU.add))
                    vs(e, e.tensor_scalar_mul(mid2[:, :], mid2[:, :], 0.5))
                    e.tensor_tensor(cmp_sc[:, :].rearrange("p (k e) -> p k e", e=8),
                                    pse[:, :].rearrange("p (k e) -> p k e", e=8),
                                    mid2[:, :].unsqueeze(1).broadcast_to([128, 32, 8]),
                                    ALU.is_ge)
                    e.tensor_reduce(part8[:, :],
                                    cmp_sc[:, :].rearrange("p (k e) -> p e k", e=8),
                                    AX.X, ALU.add).then_inc(sBv, 1)
                    e.wait_ge(sBp, 2 * it + 1)
                    e.tensor_copy(cnt8[:, :], pb[4][0:1, 0:8]).then_inc(sBv, 1)
                    e.wait_ge(sBp, 2 * it + 2)
                    vs(e, e.tensor_copy(cge[:, :], pb[4][:, 8:16]))
                    vs(e, e.tensor_scalar(cgei_f[:, :], cge[:, :], 512.0, None,
                                          ALU.is_ge))
                    vs(e, e.tensor_tensor(bscr[:, :], mid2[:, :], lo2[:, :],
                                          ALU.subtract))
                    vs(e, e.tensor_tensor(bscr[:, :], bscr[:, :], cgei_f[:, :],
                                          ALU.mult))
                    vs(e, e.tensor_tensor(lo2[:, :], lo2[:, :], bscr[:, :], ALU.add))
                    vs(e, e.tensor_tensor(bscr[:, :], hi2[:, :], mid2[:, :],
                                          ALU.subtract))
                    vs(e, e.tensor_tensor(bscr[:, :], bscr[:, :], cgei_f[:, :],
                                          ALU.mult))
                    e.tensor_tensor(hi2[:, :], mid2[:, :], bscr[:, :], ALU.add
                                    ).then_inc(sBv, 1)

            @blk.tensor
            def _(e):
                for it in range(NBIS):
                    e.wait_ge(sBv, 3 * it + 1)
                    e.matmul(pb[4][0:1, 0:8], onesf[:, :], part8[:, :],
                             start=True, stop=True).then_inc(sBp, 1)
                    e.wait_ge(sBv, 3 * it + 2)
                    e.matmul(pb[4][:, 8:16], onesr[:, :], cnt8[:, :],
                             start=True, stop=True).then_inc(sBp, 1)

    _phases.append(lambda: None)

    def _ph6(blk):
        if True:
            @blk.vector
            def _(e):
                pr3 = pse[:, :].rearrange("p (k e) -> p k e", e=8)
                mf3 = maskf[:, :].rearrange("p (k e) -> p k e", e=8)
                e.tensor_tensor(mf3, pr3,
                                lo2[:, :].unsqueeze(1).broadcast_to([128, 32, 8]),
                                ALU.is_ge)
                e.tensor_tensor(mf3, mf3,
                                half_sel[:, :].unsqueeze(-1).broadcast_to([128, 32, 8]),
                                ALU.mult)
                il = None
                for ee in range(E):
                    il = e.tensor_copy(maske[:, ee, :],
                                       maskf[:, :].rearrange("p (k e) -> p e k", e=8)[:, ee, :])
                vs(e, il)
                for ee in range(E):
                    il = e.tensor_tensor_scan(incl[:, ee, :], maske[:, ee, :],
                                              zeros[:, 0:32], 0.0, ALU.add, ALU.add)
                vs(e, il)
                vs(e, e.tensor_copy(cnts[:, :], incl[:, :, 31]))
                vs(e, e.tensor_copy(cnts_b[:, :], cnts[:, :]))
                e.tensor_tensor(lrank[:, :, :], incl[:, :, :], maske[:, :, :],
                                ALU.subtract)
                e.scalar_tensor_tensor(lrank[:, :, :], lrank[:, :, :], 1.0,
                                       maske[:, :, :], ALU.add, ALU.mult
                                       ).then_inc(sRv, 1)
                e.wait_ge(sRp, 1)
                vs(e, e.tensor_copy(starts[:, :], pb[4][:, 16:24]))
                e.tensor_copy(cscr[:, 0:8], starts[:, :]).then_inc(sRv, 1)
                for ee in range(E):
                    i3 = indsc[:, :].rearrange("p (j k) -> p j k", j=JSEG)
                    o3 = ind2[:, :].rearrange("p (j k) -> p j k", j=JSEG)
                    e.tensor_tensor(
                        i3,
                        lrank[:, ee, :].unsqueeze(1).broadcast_to([128, JSEG, 32]),
                        iotaJ[:, :].unsqueeze(-1).broadcast_to([128, JSEG, 32]),
                        ALU.is_equal)
                    e.tensor_tensor(
                        o3, i3,
                        tokid1[:, :].unsqueeze(1).broadcast_to([128, JSEG, 32]),
                        ALU.mult)
                    e.tensor_reduce(lst2[:, ee, 0:JSEG], o3, AX.X, ALU.add)
                    e.tensor_tensor(
                        o3, i3,
                        probs[:, :].rearrange("p (k e) -> p e k", e=8)[:, ee, :]
                            .unsqueeze(1).broadcast_to([128, JSEG, 32]),
                        ALU.mult)
                    e.tensor_reduce(lst2[:, ee, JSEG:2 * JSEG], o3, AX.X,
                                    ALU.add).then_inc(sRv, 1)
                ejb = [ej, ej2]
                rv, rp = 2 + E, 1
                for ee in range(E):
                    vs(e, e.tensor_single_scalar(sdiff[:, :], iotaS[:, :],
                                                 starts[:, ee:ee + 1],
                                                 ALU.subtract))
                    for cchunk in range(3):
                        for j in range(JSEG):
                            if j >= 2:
                                e.wait_ge(sRp, rp - 1)
                            elif 12 * cchunk + j >= 2 or ee > 0:
                                e.wait_ge(sRp, rp - 2)
                            e.tensor_scalar(ejb[j % 2][:, :], sdiff[:, :],
                                            float(j - 128 * cchunk), None,
                                            ALU.is_equal).then_inc(sRv, 1)
                            rv += 1
                            rp += 1
                        e.wait_ge(sRp, rp)
                        rp += 1
                        e.tensor_copy(scr2x[:, :], pb[5][0:2, 0:128]
                                      ).then_inc(sRv, 1)
                        rv += 1
                        e.wait_ge(sRp, rp)
                        vs(e, e.tensor_scalar(idsf[:, ee, cchunk:cchunk + 1],
                                              pb[5][:, 128:129],
                                              1.0, None, ALU.subtract))
                        inst = e.tensor_copy(gval[:, ee, cchunk:cchunk + 1],
                                             pb[5][:, 129:130])
                        inst.then_inc(sRv, 1)
                        rv += 1
                vs(e, e.tensor_copy(cscr[:, 0:8], starts[:, :]))
                vs(e, e.tensor_scalar_max(iscr[:, :, :], idsf[:, :, :], 0.0))
                vs(e, e.tensor_copy(idx_g[:, :, :], iscr[:, :, :]))
                vs(e, e.tensor_scalar(iscr[:, :, :], idsf[:, :, :], 0.0, None,
                                      ALU.is_lt))
                vs(e, e.scalar_tensor_tensor(iscr[:, :, :], iscr[:, :, :],
                                             float(DUMP + 1), idsf[:, :, :],
                                             ALU.mult, ALU.add))
                e.tensor_copy(idx_s[:, :, :], iscr[:, :, :])

            @blk.tensor
            def _(e):
                e.wait_ge(sRv, 1)
                e.matmul(pb[4][:, 16:24], tri[:, :], cnts_b[:, :],
                         start=True, stop=True).then_inc(sRp, 1)
                e.wait_ge(sRv, 2 + E)
                ejb = [ej, ej2]
                rv, rp = 2 + E, 1
                for ee in range(E):
                    for cchunk in range(3):
                        for j in range(JSEG):
                            rv += 1
                            e.wait_ge(sRv, rv)
                            e.matmul(pb[5][0:2, 0:128],
                                     lst2[:, ee, :].rearrange("p (a j) -> p j a", a=2)[:, j, :],
                                     ejb[j % 2][:, :],
                                     start=(j == 0), stop=(j == JSEG - 1)
                                     ).then_inc(sRp, 1)
                            rp += 1
                        rv += 1
                        e.wait_ge(sRv, rv)
                        e.transpose(pb[5][:, 128:130], scr2x[:, :],
                                    identf[0:2, 0:2]).then_inc(sRp, 1)
                        rp += 1
                        rv += 1

    _phases.append(lambda: None)

    def _ph7(blk):
        cols5 = [(0, 512), (512, 1024), (1024, 1536), (1536, 2048), (2048, EXT)]
        if True:
            @blk.tensor
            def _(e):
                e.wait_ge(sTE, 128)
                e.wait_ge(sV2, 34)
                idx = 0
                for dp in range(8):
                    for (c0, c1) in cols5:
                        if idx >= 2:
                            e.wait_ge(sA5, idx - 1)
                        bank = pb[4 + (idx % 2)]
                        for dc in range(8):
                            inst = e.matmul(bank[:, 0:c1 - c0],
                                            w1sb[:, dc, 128 * dp:128 * (dp + 1)],
                                            xnE[:, dc, c0:c1], start=(dc == 0),
                                            stop=(dc == 7))
                        inst.then_inc(sP5, 1)
                        idx += 1

            @blk.scalar
            def _(e):
                idx = 0
                for dp in range(8):
                    for (c0, c1) in cols5:
                        e.wait_ge(sP5, idx + 1)
                        e.activation(p_sb[:, dp, c0:c1],
                                     pb[4 + (idx % 2)][:, 0:c1 - c0], AF.Identity,
                                     bias=b1T[:, dp:dp + 1]).then_inc(sA5, 1)
                        idx += 1

    _phases.append(lambda: None)

    def _ph8():
        pass

    _phases.append(_ph8)

    def _ph9():
        TT3 = [(0, 128), (128, 256), (256, 320)]
        with nc.Block() as blk:
            _ph7(blk)
            def _gather(e, ee, ti):
                if CFG["p7_gather"]:
                    e.indirect_dma_start(
                        xeb[ee % 2][:, ti, :], None,
                        xn_tok_d[0:128, :],
                        IndirectOffsetOnAxis(idx_g[:, ee, ti:ti + 1], 0),
                    ).then_inc(sX, 16)
                else:
                    e.dma_start(out=xeb[ee % 2][:, ti, :],
                                in_=xn_tok_d[0:128, :]).then_inc(sX, 16)

            def _scatter(e, ee, ti, r0, r1):
                if CFG["p7_scatter"]:
                    e.indirect_dma_start(
                        acc_d[0:r1 - r0, :],
                        IndirectOffsetOnAxis(idx_s[0:r1 - r0, ee, ti:ti + 1], 0),
                        oute[0:r1 - r0, ti, :], None,
                        compute_op=ALU.add,
                    ).then_inc(sSC, 16)
                else:
                    e.dma_start(out=acc_d[T:T + r1 - r0, :],
                                in_=oute[0:r1 - r0, ti, :]).then_inc(sSC, 16)

            @blk.gpsimd
            def _(e):
                for ti in range(3):
                    _gather(e, 0, ti)
                for ee in range(CFG['p7_E']):
                    if ee + 1 < CFG['p7_E']:
                        if ee >= 1:
                            e.wait_ge(sXT, 16 * 24 * ee)
                        if CFG["p7_scblock"] and ee >= 2:
                            e.wait_ge(sSC, 48 * (ee - 1))
                        for ti in range(3):
                            _gather(e, ee + 1, ti)
                    if ee >= 2:
                        e.wait_ge(sPo, 6 * (ee - 1))
                    e.dma_start(out=b2er[ee % 2][:, :], in_=wb2_d[ee:ee + 1, :]
                                ).then_inc(sWb[ee % 2], 16)
                    for g in range(16):
                        G = 16 * ee + g
                        if G >= 4:
                            e.wait_ge(sPw, 2 * (G - 3))
                        e.dma_start(
                            out=w1e[G % 4][:, :, :],
                            in_=we1_d[ee, :, 256 * g:256 * (g + 1)]
                                .rearrange("(dc p) f -> p dc f", p=128),
                        ).then_inc(sWe[G % 4], 16)
                        e.dma_start(
                            out=w2e[G % 4][:, :, :],
                            in_=we2_d[ee, 256 * g:256 * (g + 1), :]
                                .rearrange("(ft p) dd -> p ft dd", p=128),
                        ).then_inc(sWe[G % 4], 16)
                    if ee > 0 and CFG["p7_scblock"]:
                        e.wait_ge(sO, 6 * ee)
                        e.wait_ge(sSC, 48 * (ee - 1))
                        e.wait_ge(sXT, 16 * 24 * min(ee + 2, CFG['p7_E']))
                        if ee == 1:
                            e.wait_ge(sZ, 528)
                        for ti, (r0, r1) in enumerate(TT3):
                            _scatter(e, ee - 1, ti, r0, r1)
                e.wait_ge(sO, 6 * CFG['p7_E'])
                if not CFG["p7_scblock"]:
                    return
                e.wait_ge(sSC, 48 * (CFG['p7_E'] - 1))
                for ti, (r0, r1) in enumerate(TT3):
                    if CFG["p7_scatter"]:
                        e.indirect_dma_start(
                            acc_d[0:r1 - r0, :],
                            IndirectOffsetOnAxis(idx_s[0:r1 - r0, CFG['p7_E'] - 1, ti:ti + 1], 0),
                            oute[0:r1 - r0, ti, :], None,
                            compute_op=ALU.add,
                        ).then_inc(sSC, 16)
                    else:
                        e.dma_start(out=acc_d[T:T + r1 - r0, :],
                                    in_=oute[0:r1 - r0, ti, :]).then_inc(sSC, 16)
                e.wait_ge(sSC, 48 * CFG['p7_E'])

            @blk.sync
            def _(e):
                for ee in range(CFG['p7_E']):
                    if ee >= 2:
                        e.wait_ge(sPh, 32 * (ee - 1))
                    for ti, (r0, r1) in enumerate(TT3):
                        e.wait_ge(sX, 48 * ee + 16 * (ti + 1))
                        for dc in range(8):
                            e.dma_start(out=xeT[ee % 2][:, dc, r0:r1],
                                        in_=xeb[ee % 2][0:r1 - r0, ti,
                                                        128 * dc:128 * (dc + 1)],
                                        transpose=True).then_inc(sXT, 16)

            @blk.scalar
            def _(e):
                for ee in range(CFG['p7_E']):
                    for ft in range(32):
                        FT = 32 * ee + ft
                        e.wait_ge(sPh, FT + 1)
                        if FT >= 2:
                            e.wait_ge(sPw, FT - 1)
                        e.activation(h1[ft % 2][:, :], pb[6 + (ft % 2)][:, 0:CAP],
                                     AF.Gelu_apprx_tanh,
                                     bias=b1all[:, 32 * ee + ft:32 * ee + ft + 1]
                                     ).then_inc(sH, 1)

            def self_w2(e, ee, ft):
                FT = 32 * ee + ft
                g = ft // 2
                G = 16 * ee + g
                e.wait_ge(sH, FT + 1)
                if ft == 0 and ee > 0:
                    e.wait_ge(sO, 6 * ee)
                for ti, (r0, r1) in enumerate(TT3):
                    for dh in range(2):
                        inst = e.matmul(
                            pb[2 * ti + dh][0:r1 - r0, 0:512],
                            h1[ft % 2][:, r0:r1],
                            w2e[G % 4][:, ft % 2, 512 * dh:512 * (dh + 1)],
                            start=(ft == 0), stop=False)
                inst.then_inc(sPw, 1)

            @blk.tensor
            def _(e):
                for ee in range(CFG['p7_E']):
                    e.wait_ge(sXT, 16 * 24 * (ee + 1))
                    e.wait_ge(sWb[ee % 2], 16 * (ee // 2 + 1))
                    for ft in range(32):
                        FT = 32 * ee + ft
                        g = ft // 2
                        G = 16 * ee + g
                        if ft % 2 == 0:
                            e.wait_ge(sWe[G % 4], 32 * (G // 4 + 1))
                        if FT >= 2:
                            e.wait_ge(sH, FT - 1)
                        hb = pb[6 + (ft % 2)]
                        for dc in range(8):
                            inst = e.matmul(
                                hb[:, 0:CAP],
                                w1e[G % 4][:, dc, 128 * (ft % 2):128 * (ft % 2 + 1)],
                                xeT[ee % 2][:, dc, :],
                                start=(dc == 0), stop=(dc == 7))
                        inst.then_inc(sPh, 1)
                        if ft >= 1:
                            self_w2(e, ee, ft - 1)
                    self_w2(e, ee, 31)
                    for ti, (r0, r1) in enumerate(TT3):
                        for dh in range(2):
                            inst = e.matmul(
                                pb[2 * ti + dh][0:r1 - r0, 0:512],
                                onesc[0:1, 0:r1 - r0],
                                b2er[ee % 2][0:1, 512 * dh:512 * (dh + 1)],
                                start=False, stop=True)
                    inst.then_inc(sPo, 6)

            @blk.vector
            def _(e):
                for ee in range(CFG['p7_E']):
                    dp = ee
                    e.wait_ge(sA5, 5 * (dp + 1))
                    e.tensor_scalar_mul(p_sb[:, dp, 0:128], p_sb[:, dp, 0:128],
                                        mask_halo[:, 0:1])
                    for q in range(4):
                        base = 128 + 512 * q
                        for c in range(5):
                            k0 = 8 * c
                            e.tensor_scalar_mul(chain[c][:, :],
                                                p_sb[:, dp, base - k0:base - k0 + 512],
                                                taps[:, dp, k0:k0 + 1])
                            for k in range(k0 + 1, k0 + 8):
                                e.scalar_tensor_tensor(
                                    chain[c][:, :],
                                    p_sb[:, dp, base - k:base - k + 512],
                                    taps[:, dp, k:k + 1], chain[c][:, :],
                                    ALU.mult, ALU.add)
                        e.tensor_tensor(tc0[:, :], chain[0][:, :], chain[1][:, :],
                                        ALU.add)
                        e.tensor_tensor(tc1[:, :], chain[2][:, :], chain[3][:, :],
                                        ALU.add)
                        e.tensor_tensor(tc0[:, :], tc0[:, :], tc1[:, :], ALU.add)
                        e.scalar_tensor_tensor(band[:, dp, 512 * q:512 * (q + 1)],
                                               chain[4][:, :], mw0t[:, dp:dp + 1],
                                               tc0[:, :], ALU.add, ALU.add)
                    if ee > 0 and CFG["p7_scblock"]:
                        e.wait_ge(sSC, 48 * ee)
                    for ti, (r0, r1) in enumerate(TT3):
                        for dh in range(2):
                            e.wait_ge(sPo, 6 * (ee + 1))
                            e.tensor_scalar_mul(
                                oute[0:r1 - r0, ti, 512 * dh:512 * (dh + 1)],
                                pb[2 * ti + dh][0:r1 - r0, 0:512],
                                gval[0:r1 - r0, ee, ti:ti + 1]).then_inc(sO, 1)

    _phases.append(_ph9)

    def _ph10():
        with nc.Block() as blk:
            @blk.gpsimd
            def _(e):
                e.dma_start(
                    out=w1sb[:, :, :],
                    in_=w2_d[:, :].rearrange("(dp p) dd -> p dp dd", p=128),
                ).then_inc(s8w, 16)
                for tt in range(16):
                    if tt >= 2:
                        e.wait_ge(s8a, tt - 1)
                    e.indirect_dma_start(
                        accsb[tt % 2][:, :], None,
                        acc_d[0:128, :],
                        IndirectOffsetOnAxis(own_ids[:, tt:tt + 1], 0),
                    ).then_inc(s8g[tt % 2], 16)

            @blk.sync
            def _(e):
                for tt in range(16):
                    if tt >= 2:
                        e.wait_ge(s8v, 2 * (tt - 1))
                    e.dma_start(out=xch[tt % 2][:, :],
                                in_=x_ext_d[128 * (tt + 1):128 * (tt + 2), :]
                                ).then_inc(s8x[tt % 2], 16)
                    if tt >= 2:
                        j = tt - 2
                        e.wait_ge(s8v, 2 * j + 2)
                        e.dma_start(out=out_d[128 * j:128 * (j + 1), :],
                                    in_=outch[j % 2][:, :]).then_inc(s8o, 16)
                for j in (14, 15):
                    e.wait_ge(s8v, 2 * j + 2)
                    e.dma_start(out=out_d[128 * j:128 * (j + 1), :],
                                in_=outch[j % 2][:, :]).then_inc(s8o, 16)
                e.wait_ge(s8o, 16 * 16)

            @blk.tensor
            def _(e):
                e.wait_ge(s8w, 16)
                for tt in range(16):
                    if tt >= 2:
                        e.wait_ge(s8v, 2 * (tt - 1))
                    for dp in range(8):
                        for dh in range(2):
                            e.matmul(pb[2 * (tt % 2) + dh][:, 0:512],
                                     p_sb[:, dp, 128 * tt:128 * (tt + 1)],
                                     w1sb[:, dp, 512 * dh:512 * (dh + 1)],
                                     start=(dp == 0), stop=False)
                    for dh in range(2):
                        e.matmul(pb[2 * (tt % 2) + dh][:, 0:512],
                                 onesc[0:1, 0:128],
                                 b2rb[0:1, 512 * dh:512 * (dh + 1)],
                                 start=False, stop=True).then_inc(s8p, 1)

            @blk.scalar
            def _(e):
                for tt in range(16):
                    e.wait_ge(s8g[tt % 2], 16 * (tt // 2 + 1))
                    if tt > 0:
                        e.wait_ge(s8v, 2 * tt)
                    e.activation(gsb[:, :], accsb[tt % 2][:, :],
                                 AF.Silu).then_inc(s8a, 1)

            @blk.vector
            def _(e):
                for tt in range(16):
                    e.wait_ge(s8a, tt + 1)
                    e.wait_ge(s8p, 2 * tt + 2)
                    e.wait_ge(s8x[tt % 2], 16 * (tt // 2 + 1))
                    if tt >= 2:
                        e.wait_ge(s8o, 16 * (tt - 1))
                    for dh in range(2):
                        e.tensor_tensor(outch[tt % 2][:, 512 * dh:512 * (dh + 1)],
                                        gsb[:, 512 * dh:512 * (dh + 1)],
                                        pb[2 * (tt % 2) + dh][:, 0:512], ALU.mult)
                    e.tensor_tensor(outch[tt % 2][:, :], outch[tt % 2][:, :],
                                    xch[tt % 2][:, :], ALU.add).then_inc(s8v, 2)

    _phases.append(_ph10)

    for _k in range(min(upto + 1, len(_phases))):
        _phases[_k]()

    if dbg:
        dbg_items = [
            ("d_probs", probs), ("d_pse", pse), ("d_lo2", lo2),
            ("d_cnts", cnts), ("d_starts", starts), ("d_csum", csum),
            ("d_totp", totp), ("d_mw0t", mw0t), ("d_b1T", b1T),
            ("d_b2T", b2T), ("d_lgt", lg_t), ("d_lrank", lrank),
            ("d_idsf", idsf), ("d_gval", gval), ("d_taps", taps),
            ("d_idxg", idx_g), ("d_idxs", idx_s), ("d_xnE", xnE),
            ("d_p", p_sb),
        ]
        sD = sem("sD")
        with nc.Block() as blk:
            @blk.sync
            def _(e):
                n = 0
                for name, t in dbg_items:
                    sh = list(t.shape)
                    fl = int(np.prod(sh[1:]))
                    od = nc.dram_tensor(name, [sh[0], fl], t.dtype,
                                        kind="ExternalOutput")
                    src = t[tuple(slice(None) for _ in sh)]
                    if len(sh) == 3:
                        src = src.rearrange("p a b -> p (a b)")
                    e.dma_start(out=od[:, :], in_=src).then_inc(sD, 16)
                    n += 16
                e.wait_ge(sD, n)

    ctx.close()
    return nc


def kernel(L0_attn_input_data, L0_attn_W1, L0_attn_bias1, L0_attn_W2,
           L0_attn_bias2, L0_attn_memory_weights, L0_ffn_Wg, L0_ffn_W1,
           L0_ffn_b1, L0_ffn_W2, L0_ffn_b2):
    def host_f32(a):
        a = np.asarray(a)
        return np.ascontiguousarray(a.astype(np.float32, copy=False))

    x = host_f32(L0_attn_input_data)
    w1 = host_f32(L0_attn_W1)
    b1 = host_f32(L0_attn_bias1).reshape(1, D)
    b2 = host_f32(L0_attn_bias2).reshape(1, D)
    w2 = host_f32(L0_attn_W2)
    mw = host_f32(L0_attn_memory_weights)
    wg = host_f32(L0_ffn_Wg)
    we1 = host_f32(L0_ffn_W1)
    wb1 = host_f32(L0_ffn_b1)
    we2 = host_f32(L0_ffn_W2)
    wb2 = host_f32(L0_ffn_b2)

    nc = build_nc()
    in_maps = make_in_maps(x, w1, b1, w2, b2, mw, wg, we1, wb1, we2, wb2)

    res = run_bass_kernel_spmd(nc, in_maps, core_ids=list(range(8)))
    out = np.empty((B, T, D), np.float32)
    for c in range(8):
        b, h = c // 2, c % 2
        out[b, HT * h:HT * (h + 1), :] = res.results[c]["out"]
    return out


def make_in_maps(x, w1, b1, w2, b2, mw, wg, we1, wb1, we2, wb2):
    pcol = np.arange(128)
    in_maps = []
    for c in range(8):
        b, h = c // 2, c % 2
        x_row = x[b]
        if h == 0:
            halo = np.zeros((128, D), np.float32)
        else:
            halo = x_row[HT - 128:HT]
        x_ext = np.concatenate([halo, x_row[HT * h:HT * (h + 1)]], axis=0)
        mask_halo = np.full((128, 1), float(h), np.float32)
        ks = np.arange(32)
        half_sel = np.broadcast_to(((ks // 16) == h).astype(np.float32),
                                   (128, 32)).copy()
        own_ids = (HT * h + 128 * np.arange(16)[None, :] +
                   pcol[:, None]).astype(np.int32)
        tok = 128 * np.arange(32)[None, :] + pcol[:, None]
        own = (tok // HT) == h
        ext_ids = np.where(own, 128 + tok - HT * h, EXT).astype(np.int32)
        in_maps.append({
            "x_row": np.ascontiguousarray(x_row),
            "x_ext": np.ascontiguousarray(x_ext),
            "mask_halo": mask_halo,
            "half_sel": half_sel,
            "own_ids": own_ids,
            "ext_ids": ext_ids,
            "w1": w1, "b1": b1, "w2": w2, "b2": b2, "mw": mw, "wg": wg,
            "we1": we1, "wb1": wb1, "we2": we2, "wb2": wb2,
        })
    return in_maps
